# version 07
# speedup vs baseline: 1.1910x; 1.0531x over previous
import contextlib
import math

import numpy as np
import ml_dtypes

import concourse.bass as bass
import concourse.mybir as mybir
from concourse.bass_utils import run_bass_kernel_spmd
from concourse.bass_core_rust import IndirectOffsetOnAxis

F32 = mybir.dt.float32
BF16 = mybir.dt.bfloat16
I32 = mybir.dt.int32
AF = mybir.ActivationFunctionType
ALU = mybir.AluOpType
AX = mybir.AxisListType

NCORES = 8
B, S, D = 4, 4096, 2048
TLOC = 2048
NCH = 16
NDCH = 16
EPS0 = 1e-8
FDIM = 768
H0 = 16

ENGINES = ("sync", "tensor", "scalar", "vector", "gpsimd")



class Op:
    __slots__ = ("engine", "fn", "after", "amt", "need_inc", "tok", "idx", "is_async")

    def __init__(self, engine, fn, after, amt, idx, is_async):
        self.engine, self.fn, self.after, self.amt, self.idx = engine, fn, after, amt, idx
        self.need_inc = False
        self.tok = None
        self.is_async = is_async


class Buf:
    __slots__ = ("writer", "readers", "name")

    def __init__(self, name=""):
        self.writer, self.readers, self.name = None, [], name


class _Rec:
    def __init__(self, P, engine):
        self._P, self._e = P, engine

    def __getattr__(self, name):
        P, engine = self._P, self._e
        is_dma = name.startswith("dma") or name == "indirect_dma_start"
        is_coll = name == "collective_compute"

        def record(*args, reads=(), writes=(), after=(), **kw):
            def fn(eng):
                return getattr(eng, name)(*args, **kw)
            return P.op(engine, fn, reads=reads, writes=writes, after=after,
                        dma=is_dma, is_async=is_dma or is_coll)
        return record


class Prog:
    def __init__(self, nc):
        self.nc = nc
        self.ops = []
        self._es = contextlib.ExitStack()
        self.sems = {e: self._es.enter_context(nc.semaphore(f"s_{e}")) for e in ENGINES}
        self.sy = _Rec(self, "sync")
        self.t = _Rec(self, "tensor")
        self.a = _Rec(self, "scalar")
        self.v = _Rec(self, "vector")
        self.g = _Rec(self, "gpsimd")
        self._consts = {}
        self._const_sb = self._es.enter_context(nc.sbuf_tensor("constants", [128, 16], F32))
        self._const_buf = Buf("constants")
        self._const_n = 0
        self._barrier = []

    def barrier(self):
        last = {}
        for o in self.ops:
            last[o.engine] = o
        self._barrier = list(last.values())

    def const_tile(self, v):
        v = float(v)
        if v not in self._consts:
            i = self._const_n
            assert i < 16
            self._const_n += 1
            self.v.memset(self._const_sb[:, i:i + 1], v, writes=(self._const_buf,))
            self._consts[v] = self._const_sb[:, i:i + 1]
        return self._consts[v]

    def op(self, engine, fn, reads=(), writes=(), after=(), dma=False, is_async=False):
        deps = list(after) + self._barrier
        for b in reads:
            if b.writer is not None:
                deps.append(b.writer)
        for b in writes:
            if b.writer is not None:
                deps.append(b.writer)
            deps.extend(b.readers)
        o = Op(engine, fn, tuple(dict.fromkeys(d for d in deps if d is not None)),
               16 if dma else 1, len(self.ops), is_async)
        if dma:
            o.need_inc = True
        for a in o.after:
            a.need_inc = True
        for b in reads:
            b.readers.append(o)
        for b in writes:
            b.writer, b.readers = o, []
        self.ops.append(o)
        return o

    def raw(self, engine, fn, reads=(), writes=(), after=(), dma=False):
        return self.op(engine, fn, reads=reads, writes=writes, after=after, dma=dma,
                       is_async=dma)

    def fence(self, after):
        last_dma = {}
        for o in self.ops:
            if o.amt == 16:
                last_dma[o.engine] = o
        return self.op("sync", lambda e: e.nop(),
                       after=list(after) + list(last_dma.values()))

    def finalize_and_emit(self):
        counts = {e: 0 for e in ENGINES}
        for o in self.ops:
            if o.need_inc:
                counts[o.engine] += o.amt
                o.tok = counts[o.engine]
        per = {e: [o for o in self.ops if o.engine == e] for e in ENGINES}
        with self.nc.Block() as block:
            for ename in ENGINES:
                if not per[ename]:
                    continue

                def run(eng, ename=ename):
                    waited = {}
                    for o in per[ename]:
                        for a in o.after:
                            if a.tok is None:
                                continue
                            if waited.get(a.engine, -1) < a.tok:
                                eng.wait_ge(self.sems[a.engine], a.tok)
                                waited[a.engine] = a.tok
                        res = o.fn(eng)
                        if o.need_inc:
                            last = res[-1] if isinstance(res, (list, tuple)) else res
                            assert last is not None, f"inc-needed op emitted no instr ({ename})"
                            last.then_inc(self.sems[ename], o.amt)
                getattr(block, ename)(run)

    def close(self):
        self._es.close()


def mm_group(P, pairs, reads=(), writes=(), after=()):
    for (l, r, o_ap, st, sp) in pairs:
        assert l.shape[0] == r.shape[0], f"K mismatch {l.shape} vs {r.shape}"
        assert o_ap.shape[-1] == r.shape[-1], f"N mismatch {o_ap.shape} vs {r.shape}"
        assert o_ap.shape[0] == l.shape[-1], f"M mismatch {o_ap.shape} vs {l.shape}"
    def fn(eng):
        return [eng.matmul(o_ap, l, r, start=st, stop=sp) for (l, r, o_ap, st, sp) in pairs]
    return P.raw("tensor", fn, reads=reads, writes=writes, after=after)


def mm_acc(P, psum_ap, lhs_rhs, reads=(), writes=(), after=()):
    n = len(lhs_rhs)
    pairs = [(l, r, psum_ap, i == 0, i == n - 1) for i, (l, r) in enumerate(lhs_rhs)]
    return mm_group(P, pairs, reads=reads, writes=writes, after=after)


def tr_group(P, items, ident, reads=(), writes=(), after=()):
    def fn(eng):
        return [eng.transpose(o, i, ident) for o, i in items]
    return P.raw("tensor", fn, reads=reads, writes=writes, after=after)



def _chunk_map():
    cores = []
    for core in range(NCORES):
        b, p = core // 2, core % 2
        cores.append([(b, 2 * s + p) for s in range(NCH)])
    return cores


def _bf(x):
    return np.ascontiguousarray(np.asarray(x).astype(ml_dtypes.bfloat16))


def _f32(x):
    return np.ascontiguousarray(np.asarray(x), dtype=np.float32)


def _tile(v, n=128):
    v = np.asarray(v, np.float32)
    return np.ascontiguousarray(np.broadcast_to(v[None, :], (n, v.shape[0])))


def rowap(dram_ap, row0, nrows, ncols=128):
    n = nrows * ncols
    return bass.AP(dram_ap.tensor, row0 * ncols, [[n, 1], [1, n]])



class LNStats:
    def __init__(self, P, es, tag):
        nc = P.nc
        self.mean_neg = es.enter_context(nc.sbuf_tensor(f"{tag}_mn", [128, NCH], F32))
        self.rstd = es.enter_context(nc.sbuf_tensor(f"{tag}_rs", [128, NCH], F32))
        self.sq = es.enter_context(nc.sbuf_tensor(f"{tag}_sq", [128, NCH], F32))
        self.mean2 = es.enter_context(nc.sbuf_tensor(f"{tag}_m2", [128, NCH], F32))
        self.junk = es.enter_context(nc.sbuf_tensor(f"{tag}_jk", [128, 2048], F32))
        self.b = Buf(tag + "_stats")

    def stat_chunk(self, P, c, ap, reads=(), after=()):
        P.v.reduce_sum(self.mean_neg[:, c:c + 1], ap, axis=AX.X,
                       reads=reads, writes=(self.b,), after=after)
        P.a.activation(self.junk[:, 0:ap.shape[-1]], ap, AF.Square,
                       accum_out=self.sq[:, c:c + 1],
                       reads=reads, writes=(self.b,), after=after)

    def finish(self, P, n):
        P.v.tensor_scalar(self.mean_neg[:], self.mean_neg[:], -1.0 / n, None,
                          ALU.mult, writes=(self.b,))
        P.v.tensor_mul(self.mean2[:], self.mean_neg[:], self.mean_neg[:], writes=(self.b,))
        P.v.tensor_scalar(self.sq[:], self.sq[:], 1.0 / n, None, ALU.mult, writes=(self.b,))
        P.v.tensor_sub(self.sq[:], self.sq[:], self.mean2[:], writes=(self.b,))
        P.a.activation(self.rstd[:], self.sq[:], AF.Sqrt, bias=P.const_tile(1e-5),
                       reads=(P._const_buf,), writes=(self.b,))
        P.v.reciprocal(self.rstd[:], self.rstd[:], writes=(self.b,))

    def apply(self, P, c, out_ap, in_ap, reads=(), writes=(), after=()):
        return P.v.tensor_scalar(out_ap, in_ap, self.mean_neg[:, c:c + 1],
                                 self.rstd[:, c:c + 1], ALU.add, ALU.mult,
                                 reads=tuple(reads) + (self.b,), writes=writes, after=after)


def emit_preln_xinT(P, es, x_dram, xinT, b_xinT, ident, b_ident, tag, dbg=None):
    nc = P.nc
    ln = LNStats(P, es, tag)
    xc = [es.enter_context(nc.sbuf_tensor(f"{tag}_xc{i}", [128, 2048], F32)) for i in range(2)]
    b_xc = [Buf(f"{tag}_xc{i}") for i in range(2)]
    lnc = [es.enter_context(nc.sbuf_tensor(f"{tag}_ln{i}", [128, 2048], BF16)) for i in range(2)]
    b_lnc = [Buf(f"{tag}_ln{i}") for i in range(2)]
    pt = [es.enter_context(nc.psum_tensor(f"{tag}_pt{i}", [128, 512], BF16)) for i in range(2)]
    b_pt = [Buf(f"{tag}_pt{i}") for i in range(2)]
    for c in range(NCH):
        i = c % 2
        P.sy.dma_start(out=xc[i][:], in_=x_dram[c * 128:(c + 1) * 128, :], writes=(b_xc[i],))
        ln.stat_chunk(P, c, xc[i][:], reads=(b_xc[i],))
    ln.finish(P, D)
    for c in range(NCH):
        i = c % 2
        P.sy.dma_start(out=xc[i][:], in_=x_dram[c * 128:(c + 1) * 128, :], writes=(b_xc[i],))
        ln.apply(P, c, lnc[i][:], xc[i][:], reads=(b_xc[i],), writes=(b_lnc[i],))
        if c == 0 and dbg is not None:
            P.g.dma_start(out=dbg["dbg_var"], in_=ln.sq[:], reads=(ln.b,))
            P.g.dma_start(out=dbg["dbg_lnc"], in_=lnc[i][:], reads=(b_lnc[i],))
            P.g.dma_start(out=dbg["dbg_mn"], in_=ln.mean_neg[:], reads=(ln.b,))
            P.g.dma_start(out=dbg["dbg_rstd"], in_=ln.rstd[:], reads=(ln.b,))
            P.g.dma_start(out=dbg["dbg_xc"], in_=xc[i][:], reads=(b_xc[i],))
        for g in range(4):
            j = (c * 4 + g) % 2
            tr_group(P, [(pt[j][:, k * 128:(k + 1) * 128],
                          lnc[i][:, (g * 4 + k) * 128:(g * 4 + k + 1) * 128])
                         for k in range(4)], ident[:],
                     reads=(b_lnc[i], b_ident), writes=(b_pt[j],))
            P.a.copy(xinT[:, g * 4:(g + 1) * 4, c * 128:(c + 1) * 128],
                     pt[j][:].rearrange("p (k f) -> p k f", k=4),
                     reads=(b_pt[j],), writes=(b_xinT,))


def emit_postln_residual(P, es, src_ap_fn, b_src, x_dram, xout_dram, tag, extra_reads=()):
    nc = P.nc
    ln = LNStats(P, es, tag)
    xc = [es.enter_context(nc.sbuf_tensor(f"{tag}_xc{i}", [128, 2048], F32)) for i in range(2)]
    b_xc = [Buf(f"{tag}_xc{i}") for i in range(2)]
    lo = [es.enter_context(nc.sbuf_tensor(f"{tag}_lo{i}", [128, 2048], F32)) for i in range(2)]
    b_lo = [Buf(f"{tag}_lo{i}") for i in range(2)]
    for c in range(NCH):
        ln.stat_chunk(P, c, src_ap_fn(c), reads=(b_src,) + tuple(extra_reads))
    ln.finish(P, D)
    stores = []
    for c in range(NCH):
        i = c % 2
        P.sy.dma_start(out=xc[i][:], in_=x_dram[c * 128:(c + 1) * 128, :], writes=(b_xc[i],))
        ln.apply(P, c, lo[i][:], src_ap_fn(c), reads=(b_src,), writes=(b_lo[i],))
        P.v.tensor_add(lo[i][:], lo[i][:], xc[i][:], reads=(b_xc[i],), writes=(b_lo[i],))
        st = P.sy.dma_start(out=xout_dram[c * 128:(c + 1) * 128, :], in_=lo[i][:],
                            reads=(b_lo[i],))
        stores.append(st)
    return stores



def emit_l0a(P, es, io):
    nc = P.nc
    ident = es.enter_context(nc.sbuf_tensor("ident_sb", [128, 128], BF16))
    b_ident = Buf("ident")
    P.sy.dma_start(out=ident[:], in_=io["ident"][:, :], writes=(b_ident,))

    xinT = es.enter_context(nc.sbuf_tensor("xinT", [128, NDCH, 2048], BF16))
    b_xinT = Buf("xinT")
    u4sh = es.enter_context(nc.sbuf_tensor("dpfp_u_sh", [128, NCH, 256], BF16))
    b_u4sh = Buf("u4sh")

    with contextlib.ExitStack() as es1:
        emit_preln_xinT(P, es1, io["x0"], xinT, b_xinT, ident, b_ident, "l0a_pre",
                        dbg=io if (13 in DBG_SET and "dbg_lnc" in io) else None)
    P.barrier()

    arin, arout = io["arin"], io["arout"]
    b_arin = Buf("arin")

    with contextlib.ExitStack() as es1:
        k_bf = es1.enter_context(nc.sbuf_tensor("k_bf", [128, NCH, 512], BF16))
        v_bf = es1.enter_context(nc.sbuf_tensor("v_bf", [128, NCH, 512], BF16))
        b_k, b_v = Buf("k"), Buf("v")
        wsl = es1.enter_context(nc.sbuf_tensor("wsl", [128, NDCH, 512], BF16))
        b_wsl = Buf("wsl")
        bias = es1.enter_context(nc.sbuf_tensor("biasb", [128, 512], F32))
        b_bias = Buf("biasb")
        pk = [es1.enter_context(nc.psum_tensor(f"pk{i}", [128, 512], F32)) for i in range(2)]
        b_pk = [Buf(f"pk{i}") for i in range(2)]

        for (wname, bname, dst, b_dst) in [("l0a_wkT", "l0a_bkB", k_bf, b_k),
                                           ("l0a_wvT", "l0a_bvB", v_bf, b_v)]:
            P.g.dma_start(out=wsl[:], in_=io[wname].rearrange("(dd p) n -> p dd n", p=128),
                          writes=(b_wsl,))
            P.sy.dma_start(out=bias[:], in_=io[bname][:, :], writes=(b_bias,))
            for c in range(NCH):
                i = c % 2
                mm_acc(P, pk[i][:],
                       [(xinT[:, dd, c * 128:(c + 1) * 128], wsl[:, dd, :]) for dd in range(NDCH)],
                       reads=(b_xinT, b_wsl), writes=(b_pk[i],))
                P.v.tensor_add(dst[:, c, :], pk[i][:], bias[:],
                               reads=(b_pk[i], b_bias), writes=(b_dst,))

        mk_bf = es1.enter_context(nc.sbuf_tensor("mk_bf", [128, NCH, FDIM], BF16))
        mkT = es1.enter_context(nc.sbuf_tensor("mkT", [128, 6, 2048], BF16))
        b_mk, b_mkT = Buf("mk"), Buf("mkT")
        mb_g = es1.enter_context(nc.sbuf_tensor("mb_g", [128, NCH, 512], BF16))
        b_mb = Buf("mb")
        wmv = es1.enter_context(nc.sbuf_tensor("wmv", [128, NCH, 128], BF16))
        b_wmv = Buf("wmv")
        u4, b_u4 = u4sh, b_u4sh
        nsq = es1.enter_context(nc.sbuf_tensor("nsq", [128, NCH], F32))
        rno = es1.enter_context(nc.sbuf_tensor("rno", [128, NCH], F32))
        mksq = es1.enter_context(nc.sbuf_tensor("mksq", [128, NCH], F32))
        b_nsq, b_rno, b_mksq = Buf("nsq"), Buf("rno"), Buf("mksq")
        den = es1.enter_context(nc.sbuf_tensor("den", [128, NCH], F32))
        rden = es1.enter_context(nc.sbuf_tensor("rden", [128, NCH], F32))
        coef = es1.enter_context(nc.sbuf_tensor("coef", [128, NCH], BF16))
        scr = es1.enter_context(nc.sbuf_tensor("scr", [128, NCH], F32))
        b_den, b_rden, b_coef, b_scr = Buf("den"), Buf("rden"), Buf("coef"), Buf("scr")
        normb = es1.enter_context(nc.sbuf_tensor("normb", [128, FDIM], BF16))
        b_normb = Buf("normb")
        mem_h = es1.enter_context(nc.sbuf_tensor("mem_h", [128, 6, 128], BF16))
        b_memh = Buf("memh")
        dsc = es1.enter_context(nc.sbuf_tensor("delta_s", [128, 768], F32))
        b_dsc = Buf("delta_s")
        dnr = es1.enter_context(nc.sbuf_tensor("dn_s", [1, 768], F32))
        b_dnr = Buf("dn_s")
        junk = es1.enter_context(nc.sbuf_tensor("jk768", [128, FDIM], F32))
        pv_sb = es1.enter_context(nc.sbuf_tensor("pv_sb", [128, NCH, 128], BF16))
        b_pvsb = Buf("pv_sb")

        pt = [es1.enter_context(nc.psum_tensor(f"qt{i}", [128, 512], BF16)) for i in range(2)]
        b_pt = [Buf(f"qt{i}") for i in range(2)]
        p768 = es1.enter_context(nc.psum_tensor("p768", [128, 768], F32))
        b_p768 = Buf("p768")
        p1x = es1.enter_context(nc.psum_tensor("p1x", [1, 768], F32))
        b_p1x = Buf("p1x")

        def dpfp_batch(dst4, b_dst, src4_ap, src_bufs):
            P.a.activation(u4[:, :, 0:128], src4_ap, AF.Relu, reads=src_bufs, writes=(b_u4,))
            P.a.activation(u4[:, :, 128:256], src4_ap, AF.Relu, scale=-1.0,
                           reads=src_bufs, writes=(b_u4,))
            for j in (1, 2, 3):
                s = (j - 1) * 256
                P.v.tensor_copy(dst4[:, :, s:s + j], u4[:, :, 256 - j:256],
                                reads=(b_u4,), writes=(b_dst,))
                P.v.tensor_copy(dst4[:, :, s + j:s + 256], u4[:, :, 0:256 - j],
                                reads=(b_u4,), writes=(b_dst,))
            for j in (1, 2, 3):
                s = (j - 1) * 256
                P.v.tensor_mul(dst4[:, :, s:s + 256], dst4[:, :, s:s + 256], u4[:, :, 0:256],
                               reads=(b_u4,), writes=(b_dst,))
            for c in range(NCH):
                P.a.activation(junk[:], dst4[:, c, :], AF.Square, accum_out=nsq[:, c:c + 1],
                               reads=(b_dst,), writes=(b_nsq,))

        def norm_scale(dst4, b_dst, with_mksq):
            P.a.activation(rno[:], nsq[:], AF.Sqrt, reads=(b_nsq,), writes=(b_rno,))
            P.v.tensor_scalar(rno[:], rno[:], 1e-12, None, ALU.max, writes=(b_rno,))
            P.v.reciprocal(rno[:], rno[:], writes=(b_rno,))
            for c in range(NCH):
                P.v.tensor_scalar(dst4[:, c, :], dst4[:, c, :], rno[:, c:c + 1], None,
                                  ALU.mult, reads=(b_rno,), writes=(b_dst,))
                if with_mksq:
                    P.v.scalar_tensor_tensor(junk[:], dst4[:, c, :], 1.0, dst4[:, c, :],
                                             ALU.bypass, ALU.mult, accum_out=mksq[:, c:c + 1],
                                             reads=(b_dst,), writes=(b_mksq,))

        def transpose_to(dstT, b_dstT, src4, b_src):
            for jj in range(6):
                for cg in range(4):
                    j = (jj * 4 + cg) % 2
                    tr_group(P, [(pt[j][:, k * 128:(k + 1) * 128],
                                  src4[:, cg * 4 + k, jj * 128:(jj + 1) * 128])
                                 for k in range(4)], ident[:],
                             reads=(b_src, b_ident), writes=(b_pt[j],))
                    P.a.copy(dstT[:, jj, cg * 512:(cg + 1) * 512], pt[j][:],
                             reads=(b_pt[j],), writes=(b_dstT,))

        if "dbg_xinT" in io and 1 in DBG_SET:
            if 11 in DBG_SET:
                for dd in range(NDCH):
                    P.g.dma_start(out=io["dbg_xinT"][dd * 128:(dd + 1) * 128, :],
                                  in_=xinT[:, dd, :], reads=(b_xinT,))
            if 12 in DBG_SET:
                for c2 in range(NCH):
                    P.g.dma_start(out=io["dbg_k"][:, c2 * 512:(c2 + 1) * 512],
                                  in_=k_bf[:, c2, :], reads=(b_k,))
                    P.g.dma_start(out=io["dbg_v"][:, c2 * 512:(c2 + 1) * 512],
                                  in_=v_bf[:, c2, :], reads=(b_v,))
        for g in range(4):
            dpfp_batch(mk_bf, b_mk, k_bf[:, :, g * 128:(g + 1) * 128], (b_k,))
            norm_scale(mk_bf, b_mk, with_mksq=True)
            if g == 0 and "dbg_mk" in io and 2 in DBG_SET:
                for c2 in range(NCH):
                    P.g.dma_start(out=io["dbg_mk"][:, c2 * FDIM:(c2 + 1) * FDIM],
                                   in_=mk_bf[:, c2, :], reads=(b_mk,))
                P.g.dma_start(out=io["dbg_mksq"], in_=mksq[:], reads=(b_mksq,))
            transpose_to(mkT, b_mkT, mk_bf, b_mk)
            P.g.dma_start(out=wsl[:], in_=io["l0a_wmbT"][:, g * 512:(g + 1) * 512]
                          .rearrange("(dd p) n -> p dd n", p=128), writes=(b_wsl,))
            P.sy.dma_start(out=bias[:], in_=io["l0a_bmbB"][:, g * 512:(g + 1) * 512],
                           writes=(b_bias,))
            for c in range(NCH):
                i = c % 2
                mm_acc(P, pk[i][:],
                       [(xinT[:, dd, c * 128:(c + 1) * 128], wsl[:, dd, :]) for dd in range(NDCH)],
                       reads=(b_xinT, b_wsl), writes=(b_pk[i],))
                P.v.tensor_add(mb_g[:, c, :], pk[i][:], bias[:],
                               reads=(b_pk[i], b_bias), writes=(b_mb,))
                P.a.activation(mb_g[:, c, :], mb_g[:, c, :], AF.Sigmoid, writes=(b_mb,))
            for hh in range(4):
                h = 4 * g + hh
                P.g.dma_start(out=normb[:], in_=io["l0a_normb"][h * 128:(h + 1) * 128, :],
                              writes=(b_normb,))
                P.g.dma_start(out=mem_h[:], in_=io["l0a_mem"][h * 768:(h + 1) * 768, :]
                              .rearrange("(jj p) d -> p jj d", p=128), writes=(b_memh,))
                for c in range(NCH):
                    P.v.scalar_tensor_tensor(junk[:], mk_bf[:, c, :], 1.0, normb[:],
                                             ALU.bypass, ALU.mult, accum_out=den[:, c:c + 1],
                                             reads=(b_mk, b_normb), writes=(b_den,))
                P.v.tensor_scalar(den[:], den[:], EPS0, None, ALU.add, writes=(b_den,))
                P.v.reciprocal(rden[:], den[:], reads=(b_den,), writes=(b_rden,))
                P.v.reciprocal(scr[:], mksq[:], reads=(b_mksq,), writes=(b_scr,))
                P.v.tensor_mul(scr[:], den[:], scr[:], reads=(b_den,), writes=(b_scr,))
                P.v.tensor_scalar(scr[:], scr[:], -1.0, 1.0, ALU.mult, ALU.add, writes=(b_scr,))
                P.v.tensor_scalar(coef[:], scr[:], 0.0, 1.0, ALU.max, ALU.min,
                                  reads=(b_scr,), writes=(b_coef,))
                for c in range(NCH):
                    i = c % 2
                    mm_acc(P, pk[i][:, 0:128],
                           [(mkT[:, jj, c * 128:(c + 1) * 128], mem_h[:, jj, :])
                            for jj in range(6)],
                           reads=(b_mkT, b_memh), writes=(b_pk[i],))
                    P.v.tensor_scalar(pv_sb[:, c, :], pk[i][:, 0:128], rden[:, c:c + 1],
                                      None, ALU.mult, reads=(b_pk[i], b_rden),
                                      writes=(b_pvsb,))
                P.v.tensor_sub(pv_sb[:], v_bf[:, :, g * 128:(g + 1) * 128], pv_sb[:],
                               reads=(b_v,), writes=(b_pvsb,))
                P.v.tensor_mul(wmv[:], pv_sb[:], mb_g[:, :, hh * 128:(hh + 1) * 128],
                               reads=(b_pvsb, b_mb), writes=(b_wmv,))
                if h == 0 and "dbg_den" in io and 3 in DBG_SET:
                    P.g.dma_start(out=io["dbg_den"], in_=den[:], reads=(b_den,))
                    P.g.dma_start(out=io["dbg_coef"], in_=coef[:], reads=(b_coef,))
                    P.g.dma_start(out=io["dbg_wmv"],
                                   in_=wmv[:].rearrange("p c n -> p (c n)"), reads=(b_wmv,))
                    for c2 in range(NCH):
                        P.g.dma_start(out=io["dbg_mb"][:, c2 * 512:(c2 + 1) * 512],
                                       in_=mb_g[:, c2, :], reads=(b_mb,))
                for jj in range(6):
                    mm_group(P, [(mk_bf[:, c, jj * 128:(jj + 1) * 128], wmv[:, c, :],
                                  p768[:, jj * 128:(jj + 1) * 128], c == 0, c == NCH - 1)
                                 for c in range(NCH)],
                             reads=(b_mk, b_wmv), writes=(b_p768,))
                P.a.copy(dsc[:], p768[:], reads=(b_p768,), writes=(b_dsc,))
                P.sy.dma_start(out=arin[h * 768:(h + 1) * 768, :]
                               .rearrange("(jj p) d -> p jj d", p=128),
                               in_=dsc[:].rearrange("p (jj d) -> p jj d", jj=6),
                               reads=(b_dsc,), writes=(b_arin,))
                mm_group(P, [(coef[:, c:c + 1], mk_bf[:, c, 0:512],
                              p1x[:, 0:512], c == 0, c == NCH - 1) for c in range(NCH)] +
                            [(coef[:, c:c + 1], mk_bf[:, c, 512:768],
                              p1x[:, 512:768], c == 0, c == NCH - 1) for c in range(NCH)],
                         reads=(b_coef, b_mk), writes=(b_p1x,))
                P.a.copy(dnr[:], p1x[:], reads=(b_p1x,), writes=(b_dnr,))
                P.sy.dma_start(out=rowap(arin, 12288 + 6 * h, 6),
                               in_=dnr[:], reads=(b_dnr,), writes=(b_arin,))

    P.barrier()

    b_arout = Buf("arout")
    P.g.collective_compute(
        "AllReduce", ALU.add, replica_groups=[list(range(NCORES))],
        ins=[arin.tensor.ap().opt()], outs=[arout.tensor.ap().opt()],
        reads=(b_arin,), writes=(b_arout,))
    P.g.collective_compute(
        "AllReduce", ALU.add, replica_groups=[list(range(NCORES))],
        ins=[io["fence_in"].tensor.ap().opt()], outs=[io["fence_out"].tensor.ap().opt()],
        writes=(b_arout,))
    if "dbg_arout" in io and 6 in DBG_SET:
        P.g.dma_start(out=io["dbg_arout"], in_=arout[:, :], reads=(b_arout,))

    with contextlib.ExitStack() as es0:
      attn = es0.enter_context(nc.sbuf_tensor("attn", [128, NCH, 2048], BF16))
      b_attn = Buf("attn")
      with contextlib.ExitStack() as es1:
        mq_bf = es1.enter_context(nc.sbuf_tensor("mq_bf", [128, NCH, FDIM], BF16))
        mqT = es1.enter_context(nc.sbuf_tensor("mqT", [128, 6, 2048], BF16))
        b_mq, b_mqT = Buf("mq"), Buf("mqT")
        q_h = es1.enter_context(nc.sbuf_tensor("q_h", [128, NCH, 128], BF16))
        b_qh = Buf("qh")
        u4, b_u4 = u4sh, b_u4sh
        nsq = es1.enter_context(nc.sbuf_tensor("nsq2", [128, NCH], F32))
        rno = es1.enter_context(nc.sbuf_tensor("rno2", [128, NCH], F32))
        b_nsq, b_rno = Buf("nsq2"), Buf("rno2")
        dena = es1.enter_context(nc.sbuf_tensor("dena", [128, NCH], F32))
        rdena = es1.enter_context(nc.sbuf_tensor("rdena", [128, NCH], F32))
        b_dena, b_rdena = Buf("dena"), Buf("rdena")
        nm_h = es1.enter_context(nc.sbuf_tensor("nm_h", [128, 6, 128], BF16))
        nm_f = es1.enter_context(nc.sbuf_tensor("nm_f", [128, 6, 128], BF16))
        b_nmh, b_nmf = Buf("nmh"), Buf("nmf")
        nn_row = es1.enter_context(nc.sbuf_tensor("nn_row", [1, 768], F32))
        nn_bf = es1.enter_context(nc.sbuf_tensor("nn_bf", [1, 768], BF16))
        nnb = es1.enter_context(nc.sbuf_tensor("nnb", [128, FDIM], BF16))
        b_nnr, b_nnbf, b_nnb = Buf("nnr"), Buf("nnbf"), Buf("nnb")
        ones1 = es1.enter_context(nc.sbuf_tensor("ones1_sb", [1, 128], BF16))
        b_ones = Buf("ones1")
        P.sy.dma_start(out=ones1[:], in_=io["ones1"][:, :], writes=(b_ones,))
        wq_h = es1.enter_context(nc.sbuf_tensor("wq_h", [128, NDCH, 128], BF16))
        wg_h = es1.enter_context(nc.sbuf_tensor("wg_h", [128, NDCH, 128], BF16))
        b_wq, b_wg = Buf("wq"), Buf("wg")
        biasq = es1.enter_context(nc.sbuf_tensor("biasq", [128, 128], BF16))
        biasg = es1.enter_context(nc.sbuf_tensor("biasg", [128, 128], BF16))
        b_bq, b_bg = Buf("bq"), Buf("bg")
        t_mh = [es1.enter_context(nc.sbuf_tensor(f"t_mh{i}", [128, 128], BF16))
                for i in range(2)]
        t_g = [es1.enter_context(nc.sbuf_tensor(f"t_g{i}", [128, 128], BF16))
               for i in range(2)]
        b_tmh = [Buf(f"tmh{i}") for i in range(2)]
        b_tg = [Buf(f"tg{i}") for i in range(2)]

        pk = [es1.enter_context(nc.psum_tensor(f"qq{i}", [128, 512], F32)) for i in range(2)]
        b_pk = [Buf(f"qq{i}") for i in range(2)]
        pgg = [es1.enter_context(nc.psum_tensor(f"qg{i}", [128, 512], F32)) for i in range(2)]
        b_pgg = [Buf(f"qg{i}") for i in range(2)]
        pt = [es1.enter_context(nc.psum_tensor(f"qw{i}", [128, 512], BF16)) for i in range(2)]
        b_pt = [Buf(f"qw{i}") for i in range(2)]
        p768 = es1.enter_context(nc.psum_tensor("p768b", [128, 768], F32))
        b_p768 = Buf("p768b")

        def dpfp_batch2(dst4, b_dst, src4_ap, src_bufs, dump_ap):
            P.a.activation(u4[:, :, 0:128], src4_ap, AF.Relu, reads=src_bufs, writes=(b_u4,))
            P.a.activation(u4[:, :, 128:256], src4_ap, AF.Relu, scale=-1.0,
                           reads=src_bufs, writes=(b_u4,))
            for j in (1, 2, 3):
                s = (j - 1) * 256
                P.v.tensor_copy(dst4[:, :, s:s + j], u4[:, :, 256 - j:256],
                                reads=(b_u4,), writes=(b_dst,))
                P.v.tensor_copy(dst4[:, :, s + j:s + 256], u4[:, :, 0:256 - j],
                                reads=(b_u4,), writes=(b_dst,))
            for j in (1, 2, 3):
                s = (j - 1) * 256
                P.v.tensor_mul(dst4[:, :, s:s + 256], dst4[:, :, s:s + 256], u4[:, :, 0:256],
                               reads=(b_u4,), writes=(b_dst,))
            for c in range(NCH):
                P.a.activation(dump_ap, dst4[:, c, :], AF.Square,
                               accum_out=nsq[:, c:c + 1], reads=(b_dst,), writes=(b_nsq,))

        for h in range(H0):
            P.g.dma_start(out=wq_h[:], in_=io["l0a_wqT"][:, h * 128:(h + 1) * 128]
                          .rearrange("(dd p) n -> p dd n", p=128), writes=(b_wq,))
            P.g.dma_start(out=wg_h[:], in_=io["l0a_wgT"][:, h * 128:(h + 1) * 128]
                          .rearrange("(dd p) n -> p dd n", p=128), writes=(b_wg,))
            P.g.dma_start(out=biasq[:], in_=io["l0a_bqB"][:, h * 128:(h + 1) * 128],
                          writes=(b_bq,))
            P.g.dma_start(out=biasg[:], in_=io["l0a_bgB"][:, h * 128:(h + 1) * 128],
                          writes=(b_bg,))
            for c in range(NCH):
                i = c % 2
                mm_acc(P, pk[i][:, 0:128],
                       [(xinT[:, dd, c * 128:(c + 1) * 128], wq_h[:, dd, :])
                        for dd in range(NDCH)],
                       reads=(b_xinT, b_wq), writes=(b_pk[i],))
                P.v.tensor_add(q_h[:, c, :], pk[i][:, 0:128], biasq[:],
                               reads=(b_pk[i], b_bq), writes=(b_qh,))
            dpfp_batch2(mq_bf, b_mq, q_h[:], (b_qh,),
                        attn[:, 0:6, h * 128:(h + 1) * 128])
            P.a.activation(rno[:], nsq[:], AF.Sqrt, reads=(b_nsq,), writes=(b_rno,))
            P.v.tensor_scalar(rno[:], rno[:], 1e-12, None, ALU.max, writes=(b_rno,))
            P.v.reciprocal(rno[:], rno[:], writes=(b_rno,))
            for c in range(NCH):
                P.v.tensor_scalar(mq_bf[:, c, :], mq_bf[:, c, :], rno[:, c:c + 1], None,
                                  ALU.mult, reads=(b_rno,), writes=(b_mq,))
            for jj in range(6):
                for cg in range(4):
                    j = (jj * 4 + cg) % 2
                    tr_group(P, [(pt[j][:, k * 128:(k + 1) * 128],
                                  mq_bf[:, cg * 4 + k, jj * 128:(jj + 1) * 128])
                                 for k in range(4)], ident[:],
                             reads=(b_mq, b_ident), writes=(b_pt[j],))
                    P.a.copy(mqT[:, jj, cg * 512:(cg + 1) * 512], pt[j][:],
                             reads=(b_pt[j],), writes=(b_mqT,))
            P.g.dma_start(out=nm_f[:], in_=io["l0a_mem"][h * 768:(h + 1) * 768, :]
                          .rearrange("(jj p) d -> p jj d", p=128), writes=(b_nmf,))
            P.g.dma_start(out=nm_h[:], in_=arout[h * 768:(h + 1) * 768, :]
                          .rearrange("(jj p) d -> p jj d", p=128),
                          reads=(b_arout,), writes=(b_nmh,))
            P.v.tensor_add(nm_h[:], nm_f[:], nm_h[:], reads=(b_nmf,), writes=(b_nmh,))
            P.sy.dma_start(out=nn_row[:], in_=io["l0a_norm"][h:h + 1, :], writes=(b_nnr,))
            P.g.dma_start(out=nn_row[:], in_=rowap(arout, 12288 + 6 * h, 6),
                          accum_op=ALU.add, reads=(b_arout,), writes=(b_nnr,))
            P.v.tensor_copy(nn_bf[:], nn_row[:], reads=(b_nnr,), writes=(b_nnbf,))
            mm_group(P, [(ones1[:], nn_bf[:, 0:512], p768[:, 0:512], True, True),
                         (ones1[:], nn_bf[:, 512:768], p768[:, 512:768], True, True)],
                     reads=(b_ones, b_nnbf), writes=(b_p768,))
            P.a.copy(nnb[:], p768[:], reads=(b_p768,), writes=(b_nnb,))
            for c in range(NCH):
                P.v.scalar_tensor_tensor(q_h[:, 0:6, :], mq_bf[:, c, :], 1.0, nnb[:],
                                         ALU.bypass, ALU.mult, accum_out=dena[:, c:c + 1],
                                         reads=(b_mq, b_nnb), writes=(b_dena,))
            P.v.tensor_scalar(dena[:], dena[:], EPS0, None, ALU.add, writes=(b_dena,))
            P.v.reciprocal(rdena[:], dena[:], reads=(b_dena,), writes=(b_rdena,))
            if h == 0 and "dbg_mq" in io and 4 in DBG_SET:
                for c2 in range(NCH):
                    P.g.dma_start(out=io["dbg_mq"][:, c2 * FDIM:(c2 + 1) * FDIM],
                                   in_=mq_bf[:, c2, :], reads=(b_mq,))
                P.g.dma_start(out=io["dbg_dena"], in_=dena[:], reads=(b_dena,))
                P.g.dma_start(out=io["dbg_nm"], in_=nm_h[:].rearrange("p a b -> p (a b)"),
                               reads=(b_nmh,))
                P.g.dma_start(out=io["dbg_nnb"], in_=nnb[:], reads=(b_nnb,))
            for c in range(NCH):
                i = c % 2
                mm_acc(P, pk[i][:, 0:128],
                       [(mqT[:, jj, c * 128:(c + 1) * 128], nm_h[:, jj, :]) for jj in range(6)],
                       reads=(b_mqT, b_nmh), writes=(b_pk[i],))
                P.v.tensor_scalar(t_mh[i][:], pk[i][:, 0:128], rdena[:, c:c + 1], None,
                                  ALU.mult, reads=(b_pk[i], b_rdena), writes=(b_tmh[i],))
                mm_acc(P, pgg[i][:, 0:128],
                       [(xinT[:, dd, c * 128:(c + 1) * 128], wg_h[:, dd, :])
                        for dd in range(NDCH)],
                       reads=(b_xinT, b_wg), writes=(b_pgg[i],))
                P.v.tensor_add(t_g[i][:], pgg[i][:, 0:128], biasg[:],
                               reads=(b_pgg[i], b_bg), writes=(b_tg[i],))
                P.a.activation(t_g[i][:], t_g[i][:], AF.Sigmoid, writes=(b_tg[i],))
                j = c % 2
                tr_group(P, [(pt[j][:, 0:128], xinT[:, h, c * 128:(c + 1) * 128])],
                         ident[:], reads=(b_xinT, b_ident), writes=(b_pt[j],))
                P.v.tensor_mul(t_g[i][:], t_mh[i][:], t_g[i][:],
                               reads=(b_tmh[i],), writes=(b_tg[i],))
                P.v.tensor_add(attn[:, c, h * 128:(h + 1) * 128], t_g[i][:], pt[j][:, 0:128],
                               reads=(b_tg[i], b_pt[j]), writes=(b_attn,))

      if "dbg_attn" in io and 5 in DBG_SET:
          for c2 in range(NCH):
              P.g.dma_start(out=io["dbg_attn"][c2 * 128:(c2 + 1) * 128, :],
                             in_=attn[:, c2, :], reads=(b_attn,))
      P.barrier()
      with contextlib.ExitStack() as es1:
        stores = emit_postln_residual(P, es1, lambda c: attn[:, c, :], b_attn,
                                      io["x0"], io["x1"], "l0a_post")
        P.fence(stores)



def prep_l0a_inputs(full, core):
    cm = _chunk_map()[core]
    hs = np.asarray(full["L0_attn_hidden_states"], np.float32).reshape(B * 32, 128, D)
    rows = np.concatenate([hs[b * 32 + j] for (b, j) in cm], axis=0)
    return {
        "x0": _f32(rows),
        "l0a_wqT": _f32(np.asarray(full["L0_attn_Wq"]).T),
        "l0a_wkT": _f32(np.asarray(full["L0_attn_Wk"]).T),
        "l0a_wvT": _f32(np.asarray(full["L0_attn_Wv"]).T),
        "l0a_wgT": _f32(np.asarray(full["L0_attn_Wg"]).T),
        "l0a_wmbT": _f32(np.asarray(full["L0_attn_Wmb"]).T),
        "l0a_bqB": _tile(full["L0_attn_bq"]),
        "l0a_bkB": _tile(full["L0_attn_bk"]),
        "l0a_bvB": _tile(full["L0_attn_bv"]),
        "l0a_bgB": _tile(full["L0_attn_bg"]),
        "l0a_bmbB": _tile(full["L0_attn_bmb"]),
        "l0a_mem": _f32(np.asarray(full["L0_attn_memory"]).reshape(16 * 768, 128)),
        "l0a_norm": _f32(full["L0_attn_norm"]),
        "l0a_normb": _f32(np.repeat(np.asarray(full["L0_attn_norm"], np.float32), 128, axis=0)),
        "ident": np.eye(128, dtype=ml_dtypes.bfloat16),
        "ones1": np.ones((1, 128), dtype=ml_dtypes.bfloat16),
    }


DBG = [False]
DBG_AROUT = [True]
DBG_SET = set(range(10)) | {11, 12, 13}


def build_l0a():
    nc = bass.Bass("TRN2", target_bir_lowering=False)
    io = {}
    io["x0"] = nc.dram_tensor("x0", [TLOC, D], F32, kind="ExternalInput").ap()
    io["x1"] = nc.dram_tensor("x1", [TLOC, D], F32, kind="ExternalOutput").ap()
    for nm, shp in [("l0a_wqT", [D, D]), ("l0a_wkT", [D, 512]), ("l0a_wvT", [D, 512]),
                    ("l0a_wgT", [D, D]), ("l0a_wmbT", [D, D]),
                    ("l0a_bqB", [128, D]), ("l0a_bkB", [128, 512]), ("l0a_bvB", [128, 512]),
                    ("l0a_bgB", [128, D]), ("l0a_bmbB", [128, D]),
                    ("l0a_mem", [16 * 768, 128]), ("l0a_norm", [16, 768]),
                    ("l0a_normb", [2048, 768])]:
        io[nm] = nc.dram_tensor(nm, shp, F32, kind="ExternalInput").ap()
    io["ident"] = nc.dram_tensor("ident", [128, 128], BF16, kind="ExternalInput").ap()
    io["ones1"] = nc.dram_tensor("ones1", [1, 128], BF16, kind="ExternalInput").ap()
    if DBG[0]:
        _sets = {"dbg_xinT": 1, "dbg_k": 1, "dbg_v": 1, "dbg_mk": 2, "dbg_mksq": 2,
                 "dbg_den": 3, "dbg_coef": 3, "dbg_wmv": 3, "dbg_mb": 3,
                 "dbg_mq": 4, "dbg_dena": 4, "dbg_nm": 4, "dbg_nnb": 4,
                 "dbg_attn": 5, "dbg_arout": 6}
        for nm, shp, dt in [("dbg_xinT", [2048, 2048], F32), ("dbg_k", [128, NCH * 512], F32),
                            ("dbg_v", [128, NCH * 512], F32), ("dbg_mk", [128, NCH * FDIM], F32),
                            ("dbg_mksq", [128, NCH], F32), ("dbg_den", [128, NCH], F32),
                            ("dbg_coef", [128, NCH], F32), ("dbg_wmv", [128, NCH * 128], F32),
                            ("dbg_mb", [128, NCH * 512], F32),
                            ("dbg_mq", [128, NCH * FDIM], F32), ("dbg_dena", [128, NCH], F32),
                            ("dbg_nm", [128, 768], F32), ("dbg_nnb", [128, FDIM], F32),
                            ("dbg_attn", [2048, 2048], F32),
                            ("dbg_arout", [12384, 128], F32)]:
            if _sets[nm] in DBG_SET:
                io[nm] = nc.dram_tensor(nm, shp, dt, kind="ExternalOutput").ap()
        if 13 in DBG_SET:
            for nm, shp in [("dbg_var", [128, NCH]), ("dbg_lnc", [128, 2048]), ("dbg_mn", [128, NCH]),
                            ("dbg_rstd", [128, NCH]), ("dbg_xc", [128, 2048])]:
                io[nm] = nc.dram_tensor(nm, shp, F32, kind="ExternalOutput").ap()
    io["arin"] = nc.dram_tensor("arin", [12384, 128], F32).ap()
    io["arout"] = nc.dram_tensor("arout", [12384, 128], F32).ap()
    io["fence_in"] = nc.dram_tensor("fence_in", [8, 128], F32).ap()
    io["fence_out"] = nc.dram_tensor("fence_out", [8, 128], F32).ap()

    P = Prog(nc)
    with contextlib.ExitStack() as es:
        emit_l0a(P, es, io)
        P.finalize_and_emit()
    P.close()
    return nc


def kernel(**inputs):
    cm = _chunk_map()
    cores = list(range(NCORES))
    meta = prep_l0f_meta(inputs)
    if FUSED[0]:
        maps = [prep_fused_inputs(inputs, c, meta) for c in cores]
        res = run_bass_kernel_spmd(build_fused(meta), maps, core_ids=cores)
        out = np.zeros((B * 32, 128, D), np.float32)
        for core in cores:
            x4 = np.asarray(res.results[core]["x4"]).reshape(NCH, 128, D)
            for s, (b, j) in enumerate(cm[core]):
                out[b * 32 + j] = x4[s]
        return out.reshape(B, S, D)

    res = run_bass_kernel_spmd(build_l0a(), [prep_l0a_inputs(inputs, c) for c in cores],
                               core_ids=cores)
    x1 = [np.asarray(res.results[c]["x1"]) for c in cores]

    maps = [prep_l0f_inputs(inputs, c, meta) for c in cores]
    for c in cores:
        maps[c]["x1"] = x1[c]
    res = run_bass_kernel_spmd(build_l0f(meta), maps, core_ids=cores)
    x2 = [np.asarray(res.results[c]["x2"]) for c in cores]

    maps = [prep_l1a_inputs(inputs, c) for c in cores]
    for c in cores:
        maps[c]["x2"] = x2[c]
    res = run_bass_kernel_spmd(build_l1a(), maps, core_ids=cores)
    x3 = [np.asarray(res.results[c]["x3"]) for c in cores]

    maps = [prep_l1f_inputs(inputs, c) for c in cores]
    for c in cores:
        maps[c]["x3"] = x3[c]
    res = run_bass_kernel_spmd(build_l1f(), maps, core_ids=cores)

    out = np.zeros((B * 32, 128, D), np.float32)
    for core in cores:
        x4 = np.asarray(res.results[core]["x4"]).reshape(NCH, 128, D)
        for s, (b, j) in enumerate(cm[core]):
            out[b * 32 + j] = x4[s]
    return out.reshape(B, S, D)



NS = 16384


def emit_l0f(P, es, io, meta):
    nc = P.nc
    ident = es.enter_context(nc.sbuf_tensor("ident_sb", [128, 128], BF16))
    b_ident = Buf("ident")
    P.sy.dma_start(out=ident[:], in_=io["ident"][:, :], writes=(b_ident,))
    iota = es.enter_context(nc.sbuf_tensor("iota_sb", [128, 128], F32))
    b_iota = Buf("iota")
    P.sy.dma_start(out=iota[:], in_=io["iota128"][:, :], writes=(b_iota,))

    xinT = es.enter_context(nc.sbuf_tensor("xinT", [128, NDCH, 2048], BF16))
    b_xinT = Buf("xinT")
    with contextlib.ExitStack() as es1:
        emit_preln_xinT(P, es1, io["x1"], xinT, b_xinT, ident, b_ident, "l0f_pre")
    P.barrier()

    xg = io["xg_dram"]
    b_xg = Buf("xg")
    for dd in range(NDCH):
        P.sy.dma_start(out=xg[dd * 128:(dd + 1) * 128, :], in_=xinT[:, dd, :],
                       reads=(b_xinT,), writes=(b_xg,) if dd == NDCH - 1 else (),
                       after=(b_xg.writer,))

    ffn_out = es.enter_context(nc.sbuf_tensor("ffn_out", [128, NCH, 2048], BF16))
    b_fo = Buf("ffn_out")

    with contextlib.ExitStack() as es1:
        gw = es1.enter_context(nc.sbuf_tensor("gw_sb", [128, NDCH, 8], BF16))
        b_gw = Buf("gw")
        P.g.dma_start(out=gw[:], in_=io["l0f_gwT"].rearrange("(dd p) n -> p dd n", p=128),
                      writes=(b_gw,))
        PT = es1.enter_context(nc.sbuf_tensor("PT_sb", [8, 2048], BF16))
        b_PT = Buf("PT")
        lg = es1.enter_context(nc.sbuf_tensor("lg", [128, 8], F32))
        t1 = es1.enter_context(nc.sbuf_tensor("rt1", [128, 8], F32))
        t2 = es1.enter_context(nc.sbuf_tensor("rt2", [128, 8], F32))
        m1 = es1.enter_context(nc.sbuf_tensor("rm1", [128, 1], F32))
        m2 = es1.enter_context(nc.sbuf_tensor("rm2", [128, 1], F32))
        w1 = es1.enter_context(nc.sbuf_tensor("rw1", [128, 1], F32))
        w2 = es1.enter_context(nc.sbuf_tensor("rw2", [128, 1], F32))
        pmix = es1.enter_context(nc.sbuf_tensor("pmix", [128, 8], BF16))
        b_lg, b_t1, b_t2, b_m1, b_m2, b_w1, b_w2, b_px = (
            Buf("lg"), Buf("t1"), Buf("t2"), Buf("m1"), Buf("m2"), Buf("w1"),
            Buf("w2"), Buf("px"))
        pk = [es1.enter_context(nc.psum_tensor(f"fk{i}", [128, 512], F32)) for i in range(2)]
        b_pk = [Buf(f"fk{i}") for i in range(2)]
        ptr = [es1.enter_context(nc.psum_tensor(f"ft{i}", [128, 512], BF16)) for i in range(2)]
        b_ptr = [Buf(f"ft{i}") for i in range(2)]

        for c in range(NCH):
            i = c % 2
            mm_acc(P, pk[i][:, 0:8],
                   [(xinT[:, dd, c * 128:(c + 1) * 128], gw[:, dd, :]) for dd in range(NDCH)],
                   reads=(b_xinT, b_gw), writes=(b_pk[i],))
            P.v.tensor_scalar(lg[:], pk[i][:, 0:8], 50.0, -50.0, ALU.min, ALU.max,
                              reads=(b_pk[i],), writes=(b_lg,))
            P.v.reduce_max(m1[:], lg[:], axis=AX.X, reads=(b_lg,), writes=(b_m1,))
            P.v.tensor_scalar(t1[:], lg[:], m1[:], None, ALU.is_equal,
                              reads=(b_lg, b_m1), writes=(b_t1,))
            P.v.scalar_tensor_tensor(t2[:], t1[:], -1e30, lg[:], ALU.mult, ALU.add,
                                     reads=(b_t1, b_lg), writes=(b_t2,))
            P.v.reduce_max(m2[:], t2[:], axis=AX.X, reads=(b_t2,), writes=(b_m2,))
            P.v.tensor_sub(w1[:], m1[:], m2[:], reads=(b_m1, b_m2), writes=(b_w1,))
            P.a.activation(w1[:], w1[:], AF.Sigmoid, writes=(b_w1,))
            P.v.tensor_scalar(w2[:], w1[:], -1.0, 1.0, ALU.mult, ALU.add,
                              reads=(b_w1,), writes=(b_w2,))
            P.v.tensor_scalar(t2[:], t2[:], m2[:], None, ALU.is_equal,
                              reads=(b_m2,), writes=(b_t2,))
            P.v.tensor_scalar(t1[:], t1[:], w1[:], None, ALU.mult,
                              reads=(b_w1,), writes=(b_t1,))
            P.v.tensor_scalar(t2[:], t2[:], w2[:], None, ALU.mult,
                              reads=(b_w2,), writes=(b_t2,))
            P.v.tensor_add(pmix[:], t1[:], t2[:], reads=(b_t1, b_t2), writes=(b_px,))
            j = c % 2
            tr_group(P, [(ptr[j][0:8, 0:128], pmix[:])], ident[:],
                     reads=(b_px, b_ident), writes=(b_ptr[j],))
            P.a.copy(PT[:, c * 128:(c + 1) * 128], ptr[j][0:8, 0:128],
                     reads=(b_ptr[j],), writes=(b_PT,))

        weighted = es1.enter_context(nc.sbuf_tensor("weighted", [8, NS], BF16))
        b_wt = Buf("weighted")
        eaw = es1.enter_context(nc.sbuf_tensor("eaw", [8, 64], F32))
        b_eaw = Buf("eaw")
        smT = es1.enter_context(nc.sbuf_tensor("smT", [64, 8], BF16))
        b_smT = Buf("smT")
        emx = es1.enter_context(nc.sbuf_tensor("emx", [8, 1], F32))
        esm = es1.enter_context(nc.sbuf_tensor("esm", [8, 1], F32))
        b_emx, b_esm = Buf("emx"), Buf("esm")
        atoms_sl = [es1.enter_context(nc.sbuf_tensor(f"atoms{i}", [64, 512], BF16))
                    for i in range(2)]
        b_at = [Buf(f"at{i}") for i in range(2)]
        imp_sl = [es1.enter_context(nc.sbuf_tensor(f"imp{i}", [8, 512], F32))
                  for i in range(2)]
        b_imp = [Buf(f"imp{i}") for i in range(2)]

        P.sy.dma_start(out=eaw[:], in_=io["l0f_eaw"][:, :], writes=(b_eaw,))
        P.v.reduce_max(emx[:], eaw[:], axis=AX.X, negate=True, reads=(b_eaw,), writes=(b_emx,))
        P.a.activation(eaw[:], eaw[:], AF.Exp, bias=emx[:],
                       accum_out=esm[:], reads=(b_emx,), writes=(b_eaw, b_esm))
        P.v.reciprocal(esm[:], esm[:], writes=(b_esm,))
        P.v.tensor_scalar(eaw[:], eaw[:], esm[:], None, ALU.mult,
                          reads=(b_esm,), writes=(b_eaw,))
        eawb = es1.enter_context(nc.sbuf_tensor("eawb", [8, 64], BF16))
        b_eawb = Buf("eawb")
        P.v.tensor_copy(eawb[:], eaw[:], reads=(b_eaw,), writes=(b_eawb,))
        tr_group(P, [(ptr[0][0:64, 0:8], eawb[:])], ident[0:8, 0:8],
                 reads=(b_eawb, b_ident), writes=(b_ptr[0],))
        P.a.copy(smT[:], ptr[0][0:64, 0:8], reads=(b_ptr[0],), writes=(b_smT,))
        for sl in range(NS // 512):
            i = sl % 2
            P.g.dma_start(out=atoms_sl[i][:], in_=io["l0f_atoms"][:, sl * 512:(sl + 1) * 512],
                          writes=(b_at[i],))
            P.sy.dma_start(out=imp_sl[i][:], in_=io["l0f_imp"][:, sl * 512:(sl + 1) * 512],
                           writes=(b_imp[i],))
            mm_acc(P, pk[i][0:8, :],
                   [(smT[:], atoms_sl[i][:])],
                   reads=(b_smT, b_at[i]), writes=(b_pk[i],))
            P.a.activation(imp_sl[i][:], imp_sl[i][:], AF.Sigmoid, writes=(b_imp[i],))
            P.v.tensor_mul(weighted[:, sl * 512:(sl + 1) * 512], pk[i][0:8, :], imp_sl[i][:],
                           reads=(b_pk[i], b_imp[i]), writes=(b_wt,))

        sidxT = es1.enter_context(nc.sbuf_tensor("sidxT", [128, 128], I32))
        rows_f = es1.enter_context(nc.sbuf_tensor("rows_f", [128, 128], F32))
        cols_i = es1.enter_context(nc.sbuf_tensor("cols_i", [128, 128], I32))
        scri = es1.enter_context(nc.sbuf_tensor("scri", [128, 128], I32))
        b_sidx, b_rows, b_cols = Buf("sidx"), Buf("rows"), Buf("cols")
        P.sy.dma_start(out=sidxT[:], in_=io["l0f_sidxT"][:, :], writes=(b_sidx,))
        P.v.tensor_scalar(scri[:], sidxT[:], 11, None, ALU.arith_shift_right,
                          reads=(b_sidx,), writes=(b_rows,))
        P.v.tensor_copy(rows_f[:], scri[:], writes=(b_rows,))
        P.v.tensor_scalar(cols_i[:], sidxT[:], 2047, None, ALU.bitwise_and,
                          reads=(b_sidx,), writes=(b_cols,))

        w0 = [es1.enter_context(nc.sbuf_tensor(f"w0_{i}", [128, NDCH, 128], BF16))
              for i in range(2)]
        b_w0 = [Buf(f"w0_{i}") for i in range(2)]
        g_sb = [es1.enter_context(nc.sbuf_tensor(f"g_sb{i}", [128, 2048], BF16))
                for i in range(2)]
        b_g = [Buf(f"g{i}") for i in range(2)]
        z_sb = [es1.enter_context(nc.sbuf_tensor(f"z_sb{i}", [128, 2048], BF16))
                for i in range(2)]
        b_z = [Buf(f"z{i}") for i in range(2)]
        oT = [es1.enter_context(nc.sbuf_tensor(f"oT{i}", [128, 2048], BF16))
              for i in range(2)]
        b_oT = [Buf(f"oT{i}") for i in range(2)]
        rmb = es1.enter_context(nc.sbuf_tensor("rmb", [128, 1], F32))
        Rt = es1.enter_context(nc.sbuf_tensor("Rt", [128, 128], BF16))
        b_rmb, b_R = Buf("rmb"), Buf("R")
        pdd = es1.enter_context(nc.psum_tensor("pdd", [128, 2048], F32))
        b_pdd = Buf("pdd")

        for hc in range(NCH):
            iw = hc % 2
            P.g.dma_start(out=w0[iw][:],
                          in_=io["l0f_w0T"][:, hc * 128:(hc + 1) * 128]
                          .rearrange("(dd p) n -> p dd n", p=128), writes=(b_w0[iw],))
            sch = meta["hc_chunks"][hc]
            dense_pairs = []
            for tg in range(4):
                for dd in range(NDCH):
                    dense_pairs.append(
                        (w0[iw][:, dd, :], xinT[:, dd, tg * 512:(tg + 1) * 512],
                         pdd[:, tg * 512:(tg + 1) * 512], dd == 0, False))
            mm_group(P, dense_pairs, reads=(b_w0[iw], b_xinT), writes=(b_pdd,))
            for si, sc in enumerate(sch):
                ig = si % 2
                P.g.indirect_dma_start(
                    g_sb[ig][:], None, xg[:, :],
                    IndirectOffsetOnAxis(cols_i[:, sc:sc + 1], 0),
                    reads=(b_cols, b_xg), writes=(b_g[ig],))
                for tg in range(4):
                    ip = tg % 2
                    mm_acc(P, pk[ip][:],
                           [(weighted[:, sc * 128:(sc + 1) * 128],
                             PT[:, tg * 512:(tg + 1) * 512])],
                           reads=(b_wt, b_PT), writes=(b_pk[ip],))
                    P.v.tensor_mul(z_sb[ig][:, tg * 512:(tg + 1) * 512], pk[ip][:],
                                   g_sb[ig][:, tg * 512:(tg + 1) * 512],
                                   reads=(b_pk[ip], b_g[ig]), writes=(b_z[ig],))
                P.v.tensor_scalar(rmb[:], rows_f[:, sc:sc + 1], float(hc * 128), None,
                                  ALU.subtract, reads=(b_rows,), writes=(b_rmb,))
                P.v.tensor_scalar(Rt[:], iota[:], rmb[:], None, ALU.is_equal,
                                  reads=(b_iota, b_rmb), writes=(b_R,))
                last = si == len(sch) - 1
                mm_group(P, [(Rt[:], z_sb[ig][:, tg * 512:(tg + 1) * 512],
                              pdd[:, tg * 512:(tg + 1) * 512], False, last)
                             for tg in range(4)],
                         reads=(b_R, b_z[ig]), writes=(b_pdd,))
            if not sch:
                mm_group(P, [(w0[iw][:, 0, :], xinT[:, 0, 0:1], pdd[:, tg * 512:tg * 512 + 1],
                              False, True) for tg in range(4)],
                         reads=(b_w0[iw], b_xinT), writes=(b_pdd,))
            io_ = hc % 2
            P.a.copy(oT[io_][:], pdd[:], reads=(b_pdd,), writes=(b_oT[io_],))
            for cg in range(4):
                j = cg % 2
                tr_group(P, [(ptr[j][:, k * 128:(k + 1) * 128],
                              oT[io_][:, (cg * 4 + k) * 128:(cg * 4 + k + 1) * 128])
                             for k in range(4)], ident[:],
                         reads=(b_oT[io_], b_ident), writes=(b_ptr[j],))
                P.a.copy(ffn_out[:, cg * 4:(cg + 1) * 4, hc * 128:(hc + 1) * 128],
                         ptr[j][:].rearrange("p (k f) -> p k f", k=4),
                         reads=(b_ptr[j],), writes=(b_fo,))

    P.barrier()
    with contextlib.ExitStack() as es1:
        stores = emit_postln_residual(P, es1, lambda c: ffn_out[:, c, :], b_fo,
                                      io["x1"], io["x2"], "l0f_post")
        P.fence(stores)


def prep_l0f_meta(full):
    mask_idx = np.asarray(full["L0_ffn_mask_idx"])
    perm = np.argsort(mask_idx, kind="stable")
    sidx = mask_idx[perm]
    rows = sidx >> 11
    hc_chunks = []
    for hc in range(NCH):
        lo = np.searchsorted(rows, hc * 128, side="left")
        hi = np.searchsorted(rows, (hc + 1) * 128, side="left")
        if hi > lo:
            hc_chunks.append(list(range(lo // 128, (hi + 127) // 128)))
        else:
            hc_chunks.append([])
    return {"perm": perm, "sidx": sidx, "hc_chunks": hc_chunks}


def prep_l0f_inputs(full, core, meta, x1=None):
    perm = meta["perm"]
    d = {
        "l0f_w0T": _f32(np.asarray(full["L0_ffn_W"]).T),
        "l0f_gwT": _f32(np.asarray(full["L0_ffn_gate_w"]).T),
        "l0f_atoms": _f32(np.asarray(full["L0_ffn_atoms"])[:, perm]),
        "l0f_eaw": _f32(full["L0_ffn_expert_atom_weights"]),
        "l0f_imp": _f32(np.asarray(full["L0_ffn_importance"])[:, perm]),
        "l0f_sidxT": np.ascontiguousarray(
            meta["sidx"].reshape(128, 128).T.astype(np.int32)),
        "iota128": np.ascontiguousarray(
            np.broadcast_to(np.arange(128, dtype=np.float32), (128, 128))),
        "ident": np.eye(128, dtype=ml_dtypes.bfloat16),
    }
    if x1 is not None:
        cm = _chunk_map()[core]
        xr = x1.reshape(B * 32, 128, D)
        d["x1"] = _f32(np.concatenate([xr[b * 32 + j] for (b, j) in cm], axis=0))
    return d


def build_l0f(meta):
    nc = bass.Bass("TRN2", target_bir_lowering=False)
    io = {}
    io["x1"] = nc.dram_tensor("x1", [TLOC, D], F32, kind="ExternalInput").ap()
    io["x2"] = nc.dram_tensor("x2", [TLOC, D], F32, kind="ExternalOutput").ap()
    for nm, shp, dt in [("l0f_w0T", [D, D], F32), ("l0f_gwT", [D, 8], F32),
                        ("l0f_atoms", [64, NS], F32), ("l0f_eaw", [8, 64], F32),
                        ("l0f_imp", [8, NS], F32), ("l0f_sidxT", [128, 128], I32),
                        ("iota128", [128, 128], F32)]:
        io[nm] = nc.dram_tensor(nm, shp, dt, kind="ExternalInput").ap()
    io["ident"] = nc.dram_tensor("ident", [128, 128], BF16, kind="ExternalInput").ap()
    io["xg_dram"] = nc.dram_tensor("xg_dram", [D, TLOC], BF16).ap()
    P = Prog(nc)
    with contextlib.ExitStack() as es:
        emit_l0f(P, es, io, meta)
        P.finalize_and_emit()
    P.close()
    return nc



def emit_postln_residual_dram(P, es, src_dram, x_dram, xout_dram, tag):
    nc = P.nc
    ln = LNStats(P, es, tag)
    sc = [es.enter_context(nc.sbuf_tensor(f"{tag}_sc{i}", [128, 2048], BF16)) for i in range(2)]
    b_sc = [Buf(f"{tag}_sc{i}") for i in range(2)]
    xc = [es.enter_context(nc.sbuf_tensor(f"{tag}_xc{i}", [128, 2048], F32)) for i in range(2)]
    b_xc = [Buf(f"{tag}_xc{i}") for i in range(2)]
    lo = [es.enter_context(nc.sbuf_tensor(f"{tag}_lo{i}", [128, 2048], F32)) for i in range(2)]
    b_lo = [Buf(f"{tag}_lo{i}") for i in range(2)]
    for c in range(NCH):
        i = c % 2
        P.sy.dma_start(out=sc[i][:], in_=src_dram[c * 128:(c + 1) * 128, :], writes=(b_sc[i],))
        ln.stat_chunk(P, c, sc[i][:], reads=(b_sc[i],))
    ln.finish(P, D)
    stores = []
    for c in range(NCH):
        i = c % 2
        P.sy.dma_start(out=sc[i][:], in_=src_dram[c * 128:(c + 1) * 128, :], writes=(b_sc[i],))
        P.sy.dma_start(out=xc[i][:], in_=x_dram[c * 128:(c + 1) * 128, :], writes=(b_xc[i],))
        ln.apply(P, c, lo[i][:], sc[i][:], reads=(b_sc[i],), writes=(b_lo[i],))
        P.v.tensor_add(lo[i][:], lo[i][:], xc[i][:], reads=(b_xc[i],), writes=(b_lo[i],))
        stores.append(P.sy.dma_start(out=xout_dram[c * 128:(c + 1) * 128, :], in_=lo[i][:],
                                     reads=(b_lo[i],)))
    return stores


def emit_l1a(P, es, io):
    nc = P.nc
    ident = es.enter_context(nc.sbuf_tensor("ident_sb", [128, 128], BF16))
    b_ident = Buf("ident")
    P.sy.dma_start(out=ident[:], in_=io["ident"][:, :], writes=(b_ident,))
    agin, agout, qTd = io["agin"], io["agout"], io["qT_dram"]
    b_agin, b_agout, b_qTd = Buf("agin"), Buf("agout"), Buf("qTd")

    with contextlib.ExitStack() as es1:
        xinT = es1.enter_context(nc.sbuf_tensor("xinT", [128, NDCH, 2048], BF16))
        b_xinT = Buf("xinT")
        with contextlib.ExitStack() as es2:
            emit_preln_xinT(P, es2, io["x2"], xinT, b_xinT, ident, b_ident, "l1a_pre")
        P.barrier()

        permT = es1.enter_context(nc.sbuf_tensor("permT", [128, 128], BF16))
        cosT = es1.enter_context(nc.sbuf_tensor("cosT", [128, 2048], BF16))
        sinT = es1.enter_context(nc.sbuf_tensor("sinT", [128, 2048], BF16))
        b_perm, b_cos, b_sin = Buf("perm"), Buf("cos"), Buf("sin")
        P.sy.dma_start(out=permT[:], in_=io["ropePermT"][:, :], writes=(b_perm,))
        P.sy.dma_start(out=cosT[:], in_=io["ropeCos"][:, :], writes=(b_cos,))
        P.sy.dma_start(out=sinT[:], in_=io["ropeSin"][:, :], writes=(b_sin,))

        wsl = es1.enter_context(nc.sbuf_tensor("wsl", [128, NDCH, 128], BF16))
        b_wsl = Buf("wsl")
        qraw = es1.enter_context(nc.sbuf_tensor("qraw", [128, 512], BF16))
        b_qraw = Buf("qraw")
        rt1 = es1.enter_context(nc.sbuf_tensor("rt1", [128, 512], F32))
        rt2 = es1.enter_context(nc.sbuf_tensor("rt2", [128, 512], F32))
        rt3 = es1.enter_context(nc.sbuf_tensor("rt3", [128, 512], BF16))
        b_rt1, b_rt2, b_rt3 = Buf("rt1"), Buf("rt2"), Buf("rt3")
        pk = [es1.enter_context(nc.psum_tensor(f"ak{i}", [128, 512], F32)) for i in range(2)]
        b_pk = [Buf(f"ak{i}") for i in range(2)]
        pr = es1.enter_context(nc.psum_tensor("ar", [128, 512], F32))
        b_pr = Buf("ar")

        for (wname, dst, b_dst, row0) in [("l1_wkT", agin, b_agin, 0)]:
            for h in range(16):
                P.g.dma_start(out=wsl[:], in_=io[wname][:, h * 128:(h + 1) * 128]
                              .rearrange("(dd p) n -> p dd n", p=128), writes=(b_wsl,))
                for tg in range(4):
                    i = tg % 2
                    mm_acc(P, pk[i][:],
                           [(wsl[:, dd, :], xinT[:, dd, tg * 512:(tg + 1) * 512])
                            for dd in range(NDCH)],
                           reads=(b_wsl, b_xinT), writes=(b_pk[i],))
                    P.a.copy(qraw[:], pk[i][:], reads=(b_pk[i],), writes=(b_qraw,))
                    mm_acc(P, pr[:], [(permT[:], qraw[:])],
                           reads=(b_perm, b_qraw), writes=(b_pr,))
                    P.v.tensor_mul(rt1[:], qraw[:], cosT[:, tg * 512:(tg + 1) * 512],
                                   reads=(b_qraw, b_cos), writes=(b_rt1,))
                    P.v.tensor_mul(rt2[:], pr[:], sinT[:, tg * 512:(tg + 1) * 512],
                                   reads=(b_pr, b_sin), writes=(b_rt2,))
                    P.v.tensor_add(rt3[:], rt1[:], rt2[:],
                                   reads=(b_rt1, b_rt2), writes=(b_rt3,))
                    P.sy.dma_start(out=dst[row0 + h * 128:row0 + (h + 1) * 128,
                                           tg * 512:(tg + 1) * 512],
                                   in_=rt3[:], reads=(b_rt3,), writes=(b_dst,))
        wv = es1.enter_context(nc.sbuf_tensor("wv", [128, NDCH, 512], BF16))
        b_kick = Buf("kick")
        b_wv = Buf("wv")
        vev = es1.enter_context(nc.sbuf_tensor("vev", [128, 512], BF16))
        b_vev = Buf("vev")
        for og in range(4):
            P.g.dma_start(out=wv[:], in_=io["l1_wvT"][:, og * 512:(og + 1) * 512]
                          .rearrange("(dd p) n -> p dd n", p=128), writes=(b_wv,))
            for c in range(NCH):
                i = c % 2
                mm_acc(P, pk[i][:],
                       [(xinT[:, dd, c * 128:(c + 1) * 128], wv[:, dd, :])
                        for dd in range(NDCH)],
                       reads=(b_xinT, b_wv), writes=(b_pk[i],))
                P.a.copy(vev[:], pk[i][:], reads=(b_pk[i],), writes=(b_vev,))
                for hh in range(4):
                    h = og * 4 + hh
                    P.sy.dma_start(
                        out=agin[2048 + h * 128:2048 + (h + 1) * 128,
                                 c * 128:(c + 1) * 128],
                        in_=vev[:, hh * 128:(hh + 1) * 128],
                        reads=(b_vev,), writes=(b_agin,))
        groups = [[0, 1], [2, 3], [4, 5], [6, 7]]
        P.g.collective_compute("AllGather", ALU.bypass, replica_groups=groups,
                               ins=[agin.tensor.ap().opt()],
                               outs=[agout.tensor.ap().opt()],
                               reads=(b_agin,), writes=(b_agout,))
        P.g.collective_compute("AllReduce", ALU.add, replica_groups=groups,
                               ins=[io["f2in"].tensor.ap().opt()],
                               outs=[io["f2out"].tensor.ap().opt()],
                               writes=(b_agout,))
        for (wname, dst, b_dst, row0) in [("l1_wqT", qTd, b_qTd, 0)]:
            for h in range(16):
                P.g.dma_start(out=wsl[:], in_=io[wname][:, h * 128:(h + 1) * 128]
                              .rearrange("(dd p) n -> p dd n", p=128), writes=(b_wsl,))
                for tg in range(4):
                    i = tg % 2
                    mm_acc(P, pk[i][:],
                           [(wsl[:, dd, :], xinT[:, dd, tg * 512:(tg + 1) * 512])
                            for dd in range(NDCH)],
                           reads=(b_wsl, b_xinT), writes=(b_pk[i],))
                    P.a.copy(qraw[:], pk[i][:], reads=(b_pk[i],), writes=(b_qraw,))
                    mm_acc(P, pr[:], [(permT[:], qraw[:])],
                           reads=(b_perm, b_qraw), writes=(b_pr,))
                    P.v.tensor_mul(rt1[:], qraw[:], cosT[:, tg * 512:(tg + 1) * 512],
                                   reads=(b_qraw, b_cos), writes=(b_rt1,))
                    P.v.tensor_mul(rt2[:], pr[:], sinT[:, tg * 512:(tg + 1) * 512],
                                   reads=(b_pr, b_sin), writes=(b_rt2,))
                    P.v.tensor_add(rt3[:], rt1[:], rt2[:],
                                   reads=(b_rt1, b_rt2), writes=(b_rt3,))
                    P.sy.dma_start(out=dst[row0 + h * 128:row0 + (h + 1) * 128,
                                           tg * 512:(tg + 1) * 512],
                                   in_=rt3[:], reads=(b_rt3,), writes=(b_dst,))
    P.barrier()

    attO = es.enter_context(nc.sbuf_tensor("attO", [128, NCH, 2048], BF16))
    b_attO = Buf("attO")

    with contextlib.ExitStack() as es1:
        maskT = es1.enter_context(nc.sbuf_tensor("maskT_sb", [128, 2, 4, 512], BF16))
        b_mask = Buf("mask")
        P.sy.dma_start(out=maskT[:, 0], in_=io["maskA"].rearrange("(m p) n -> p m n", p=128),
                       writes=(b_mask,))
        P.sy.dma_start(out=maskT[:, 1], in_=io["maskB"].rearrange("(m p) n -> p m n", p=128),
                       writes=(b_mask,))
        ones128 = es1.enter_context(nc.sbuf_tensor("ones128_sb", [128, 128], BF16))
        b_ones = Buf("ones128")
        P.sy.dma_start(out=ones128[:], in_=io["ones128"][:, :], writes=(b_ones,))
        kT_h = es1.enter_context(nc.sbuf_tensor("kT_h", [128, 2, 2048], BF16))
        v_h = es1.enter_context(nc.sbuf_tensor("v_h", [128, 2, 2048], BF16))
        qT_h = es1.enter_context(nc.sbuf_tensor("qT_h", [128, 2048], BF16))
        b_kTh, b_vh, b_qTh = Buf("kTh"), Buf("vh"), Buf("qTh")
        probs = [es1.enter_context(nc.sbuf_tensor(f"probs{i}", [128, 512], BF16))
                 for i in range(4)]
        b_probs = [Buf(f"probs{i}") for i in range(4)]
        rden = es1.enter_context(nc.sbuf_tensor("rden", [128, 512], F32))
        b_rden = Buf("rden")
        ps = [es1.enter_context(nc.psum_tensor(f"ps{i}", [128, 512], F32)) for i in range(4)]
        b_ps = [Buf(f"ps{i}") for i in range(4)]
        pv = es1.enter_context(nc.psum_tensor("pv", [128, 512], F32))
        b_pv = Buf("pv")
        pden = es1.enter_context(nc.psum_tensor("pden", [128, 512], F32))
        b_pden = Buf("pden")
        SC = 1.0 / math.sqrt(128.0)

        for h in range(16):
            P.sy.dma_start(out=qT_h[:], in_=qTd[h * 128:(h + 1) * 128, :],
                           reads=(b_qTd,), writes=(b_qTh,))
            for p_ in range(2):
                P.sy.dma_start(out=kT_h[:, p_, :],
                               in_=agout[p_ * 4096 + h * 128:p_ * 4096 + (h + 1) * 128, :],
                               reads=(b_agout,), writes=(b_kTh,))
                P.sy.dma_start(out=v_h[:, p_, :],
                               in_=agout[p_ * 4096 + 2048 + h * 128:
                                         p_ * 4096 + 2048 + (h + 1) * 128, :],
                               reads=(b_agout,), writes=(b_vh,))
            for tg in range(4):
                nu = 4 * tg + 4
                steps = [(p_, u) for p_ in range(2) for u in range(nu)]
                n = len(steps)

                def emit_scores(idx):
                    p_, u = steps[idx]
                    i = idx % 4
                    mm_acc(P, ps[i][:],
                           [(kT_h[:, p_, u * 128:(u + 1) * 128],
                             qT_h[:, tg * 512:(tg + 1) * 512])],
                           reads=(b_kTh, b_qTh), writes=(b_ps[i],))

                emit_scores(0)
                emit_scores(1)
                for idx in range(n):
                    p_, u = steps[idx]
                    i = idx % 4
                    if idx + 2 < n:
                        emit_scores(idx + 2)
                    P.a.activation(probs[i][:], ps[i][:], AF.Exp, scale=SC,
                                   reads=(b_ps[i],), writes=(b_probs[i],))
                    du = u - 4 * tg
                    if du >= 0:
                        P.v.tensor_mul(probs[i][:], probs[i][:], maskT[:, p_, du, :],
                                       reads=(b_mask,), writes=(b_probs[i],))
                    mm_group(P, [(ones128[:], probs[i][:], pden[:], idx == 0, idx == n - 1),
                                 (v_h[:, p_, u * 128:(u + 1) * 128], probs[i][:],
                                  pv[:], idx == 0, idx == n - 1)],
                             reads=(b_ones, b_probs[i], b_vh),
                             writes=(b_pden, b_pv))
                P.v.reciprocal(rden[:], pden[:], reads=(b_pden,), writes=(b_rden,))
                P.v.tensor_mul(attO[:, h, tg * 512:(tg + 1) * 512], pv[:], rden[:],
                               reads=(b_pv, b_rden), writes=(b_attO,))
    P.barrier()

    with contextlib.ExitStack() as es1:
        attn_tm = es1.enter_context(nc.sbuf_tensor("attn_tm", [128, NCH, 2048], BF16))
        b_atm = Buf("attn_tm")
        wo = es1.enter_context(nc.sbuf_tensor("wo", [128, NDCH, 512], BF16))
        b_wo = Buf("wo")
        pk = [es1.enter_context(nc.psum_tensor(f"ok{i}", [128, 512], F32)) for i in range(2)]
        b_pk = [Buf(f"ok{i}") for i in range(2)]
        for og in range(4):
            P.g.dma_start(out=wo[:], in_=io["l1_woT"][:, og * 512:(og + 1) * 512]
                          .rearrange("(dd p) n -> p dd n", p=128), writes=(b_wo,))
            for c in range(NCH):
                i = c % 2
                mm_acc(P, pk[i][:],
                       [(attO[:, hh, c * 128:(c + 1) * 128], wo[:, hh, :])
                        for hh in range(16)],
                       reads=(b_attO, b_wo), writes=(b_pk[i],))
                P.a.copy(attn_tm[:, c, og * 512:(og + 1) * 512], pk[i][:],
                         reads=(b_pk[i],), writes=(b_atm,))
        P.barrier()
        with contextlib.ExitStack() as es2:
            stores = emit_postln_residual(P, es2, lambda c: attn_tm[:, c, :], b_atm,
                                          io["x2"], io["x3"], "l1a_post")
            P.fence(stores)


def prep_l1a_inputs(full, core, x2=None):
    p = core % 2
    cm = _chunk_map()[core]
    d = {
        "l1_wqT": _f32(np.asarray(full["L1_attn_Wq"]).T),
        "l1_wkT": _f32(np.asarray(full["L1_attn_Wk"]).T),
        "l1_wvT": _f32(np.asarray(full["L1_attn_Wv"]).T),
        "l1_woT": _f32(np.asarray(full["L1_attn_Wo"]).T),
        "ident": np.eye(128, dtype=ml_dtypes.bfloat16),
        "ones128": np.ones((128, 128), dtype=ml_dtypes.bfloat16),
    }
    pos = np.concatenate([np.arange(128) + j * 128 for (_, j) in cm]).astype(np.float64)
    half = 64
    inv = 1.0 / (10000.0 ** (np.arange(half, dtype=np.float64) / half))
    fr = pos[None, :] * inv[:, None]
    cosT = np.repeat(np.cos(fr), 2, axis=0)
    sinT = np.repeat(np.sin(fr), 2, axis=0)
    d["ropeCos"] = cosT.astype(ml_dtypes.bfloat16)
    d["ropeSin"] = sinT.astype(ml_dtypes.bfloat16)
    M = np.zeros((128, 128), np.float32)
    for i2 in range(64):
        M[2 * i2, 2 * i2 + 1] = -1.0
        M[2 * i2 + 1, 2 * i2] = 1.0
    d["ropePermT"] = np.ascontiguousarray(M.T).astype(ml_dtypes.bfloat16)
    tri = (np.arange(128)[:, None] <= np.arange(128)[None, :]).astype(np.float32)
    ones = np.ones((128, 128), np.float32)
    zeros = np.zeros((128, 128), np.float32)
    mA = np.zeros((4, 128, 512), np.float32)
    mB = np.zeros((4, 128, 512), np.float32)
    for du in range(4):
        for ds in range(4):
            blk = slice(ds * 128, (ds + 1) * 128)
            if du < ds:
                mA[du][:, blk] = 1.0
                mB[du][:, blk] = 1.0
            elif du == ds:
                mA[du][:, blk] = tri if p == 0 else ones
                mB[du][:, blk] = zeros if p == 0 else tri
    d["maskA"] = mA.reshape(512, 512).astype(ml_dtypes.bfloat16)
    d["maskB"] = mB.reshape(512, 512).astype(ml_dtypes.bfloat16)
    if x2 is not None:
        xr = x2.reshape(B * 32, 128, D)
        d["x2"] = _f32(np.concatenate([xr[b * 32 + j] for (b, j) in cm], axis=0))
    return d


def build_l1a():
    nc = bass.Bass("TRN2", target_bir_lowering=False)
    io = {}
    io["x2"] = nc.dram_tensor("x2", [TLOC, D], F32, kind="ExternalInput").ap()
    io["x3"] = nc.dram_tensor("x3", [TLOC, D], F32, kind="ExternalOutput").ap()
    for nm in ["l1_wqT", "l1_wkT", "l1_wvT", "l1_woT"]:
        io[nm] = nc.dram_tensor(nm, [D, D], F32, kind="ExternalInput").ap()
    for nm, shp in [("ropeCos", [128, 2048]), ("ropeSin", [128, 2048]),
                    ("ropePermT", [128, 128]), ("ones128", [128, 128]),
                    ("maskA", [512, 512]), ("maskB", [512, 512]),
                    ("ident", [128, 128])]:
        io[nm] = nc.dram_tensor(nm, shp, BF16, kind="ExternalInput").ap()
    io["agin"] = nc.dram_tensor("agin", [4096, 2048], BF16).ap()
    io["agout"] = nc.dram_tensor("agout", [8192, 2048], BF16).ap()
    io["qT_dram"] = nc.dram_tensor("qT_dram", [2048, 2048], BF16).ap()
    io["f2in"] = nc.dram_tensor("f2in", [8, 128], F32).ap()
    io["f2out"] = nc.dram_tensor("f2out", [8, 128], F32).ap()
    P = Prog(nc)
    with contextlib.ExitStack() as es:
        emit_l1a(P, es, io)
        P.finalize_and_emit()
    P.close()
    return nc



G_CENTER = 1.0 / math.e
G_LOWER = 0.5 - math.log(4.0 / 3.0)
G_UPPER = 0.5


def emit_l1f(P, es, io):
    nc = P.nc
    ident = es.enter_context(nc.sbuf_tensor("ident_sb", [128, 128], BF16))
    b_ident = Buf("ident")
    P.sy.dma_start(out=ident[:], in_=io["ident"][:, :], writes=(b_ident,))
    e8bc = es.enter_context(nc.sbuf_tensor("e8bc_sb", [8, 1024], BF16))
    b_e8 = Buf("e8bc")
    P.sy.dma_start(out=e8bc[:], in_=io["e8bc"][:, :], writes=(b_e8,))

    xinT = es.enter_context(nc.sbuf_tensor("xinT", [128, NDCH, 2048], BF16))
    b_xinT = Buf("xinT")
    with contextlib.ExitStack() as es1:
        emit_preln_xinT(P, es1, io["x3"], xinT, b_xinT, ident, b_ident, "l1f_pre")
    P.barrier()

    wT = es.enter_context(nc.sbuf_tensor("wT_sb", [8, 2048], BF16))
    b_wT = Buf("wT")
    with contextlib.ExitStack() as es1:
        wr = es1.enter_context(nc.sbuf_tensor("wr_sb", [128, NDCH, 8], BF16))
        b_wr = Buf("wr")
        P.g.dma_start(out=wr[:], in_=io["l1f_wr"].rearrange("(dd p) n -> p dd n", p=128),
                      writes=(b_wr,))
        tmpB = es1.enter_context(nc.sbuf_tensor("tempB", [128, 1], F32))
        b_tmp = Buf("tempB")
        P.sy.dma_start(out=tmpB[:], in_=io["l1f_tempB"][:, :], writes=(b_tmp,))
        rtemp = es1.enter_context(nc.sbuf_tensor("rtemp", [128, 1], F32))
        b_rtemp = Buf("rtemp")
        P.v.reciprocal(rtemp[:], tmpB[:], reads=(b_tmp,), writes=(b_rtemp,))
        inh = es1.enter_context(nc.sbuf_tensor("inh", [128, 8], F32))
        zon = es1.enter_context(nc.sbuf_tensor("zon", [128, 8], F32))
        dist = es1.enter_context(nc.sbuf_tensor("dist", [128, 8], F32))
        wv_ = es1.enter_context(nc.sbuf_tensor("wv_", [128, 8], F32))
        fb = es1.enter_context(nc.sbuf_tensor("fb", [128, 8], F32))
        fb2 = es1.enter_context(nc.sbuf_tensor("fb2", [128, 8], F32))
        s1 = es1.enter_context(nc.sbuf_tensor("s1_", [128, 1], F32))
        s2 = es1.enter_context(nc.sbuf_tensor("s2_", [128, 1], F32))
        s3 = es1.enter_context(nc.sbuf_tensor("s3_", [128, 1], F32))
        mk1 = es1.enter_context(nc.sbuf_tensor("mk1", [128, 8], F32))
        wfin = es1.enter_context(nc.sbuf_tensor("wfin", [128, 8], BF16))
        b_inh, b_zon, b_dist, b_wv, b_fb, b_fb2 = (Buf("inh"), Buf("zon"), Buf("dist"),
                                                   Buf("wv"), Buf("fb"), Buf("fb2"))
        b_s1, b_s2, b_s3, b_mk1, b_wfin = Buf("s1"), Buf("s2"), Buf("s3"), Buf("mk1"), Buf("wfin")
        pk = [es1.enter_context(nc.psum_tensor(f"gk{i}", [128, 512], F32)) for i in range(2)]
        b_pk = [Buf(f"gk{i}") for i in range(2)]
        ptr = [es1.enter_context(nc.psum_tensor(f"gt{i}", [128, 128], BF16)) for i in range(2)]
        b_ptr = [Buf(f"gt{i}") for i in range(2)]

        for c in range(NCH):
            i = c % 2
            mm_acc(P, pk[i][:, 0:8],
                   [(xinT[:, dd, c * 128:(c + 1) * 128], wr[:, dd, :]) for dd in range(NDCH)],
                   reads=(b_xinT, b_wr), writes=(b_pk[i],))
            P.v.tensor_scalar(inh[:], pk[i][:, 0:8], rtemp[:], None, ALU.mult,
                              reads=(b_pk[i], b_rtemp), writes=(b_inh,))
            P.a.activation(inh[:], inh[:], AF.Sigmoid, writes=(b_inh,))
            P.v.tensor_scalar(zon[:], inh[:], G_LOWER, None, ALU.is_ge,
                              reads=(b_inh,), writes=(b_zon,))
            P.v.tensor_scalar(mk1[:], inh[:], G_UPPER, None, ALU.is_le,
                              reads=(b_inh,), writes=(b_mk1,))
            P.v.tensor_mul(zon[:], zon[:], mk1[:], reads=(b_mk1,), writes=(b_zon,))
            P.v.tensor_scalar(dist[:], inh[:], G_CENTER, None, ALU.subtract,
                              reads=(b_inh,), writes=(b_dist,))
            P.a.activation(dist[:], dist[:], AF.Abs, writes=(b_dist,))
            P.a.activation(wv_[:], dist[:], AF.Exp, scale=-10.0,
                           reads=(b_dist,), writes=(b_wv,))
            P.v.tensor_mul(wv_[:], wv_[:], zon[:], reads=(b_zon,), writes=(b_wv,))
            P.v.reduce_sum(s1[:], wv_[:], axis=AX.X, reads=(b_wv,), writes=(b_s1,))
            P.v.tensor_scalar(s1[:], s1[:], 1e-8, None, ALU.is_lt, writes=(b_s1,))
            P.a.activation(fb[:], dist[:], AF.Exp, scale=-1.0 / 0.3,
                           reads=(b_dist,), writes=(b_fb,))
            P.v.reduce_max(s2[:], fb[:], axis=AX.X, reads=(b_fb,), writes=(b_s2,))
            P.v.tensor_scalar(mk1[:], fb[:], s2[:], None, ALU.is_equal,
                              reads=(b_fb, b_s2), writes=(b_mk1,))
            P.v.scalar_tensor_tensor(fb2[:], mk1[:], -1e30, fb[:], ALU.mult, ALU.add,
                                     reads=(b_mk1, b_fb), writes=(b_fb2,))
            P.v.reduce_max(s2[:], fb2[:], axis=AX.X, reads=(b_fb2,), writes=(b_s2,))
            P.v.tensor_scalar(fb2[:], fb2[:], s2[:], None, ALU.is_equal,
                              reads=(b_s2,), writes=(b_fb2,))
            P.v.tensor_add(mk1[:], mk1[:], fb2[:], reads=(b_fb2,), writes=(b_mk1,))
            P.v.tensor_mul(fb[:], fb[:], mk1[:], reads=(b_mk1,), writes=(b_fb,))
            P.v.reduce_sum(s2[:], fb[:], axis=AX.X, reads=(b_fb,), writes=(b_s2,))
            P.v.tensor_scalar(s2[:], s2[:], 1e-8, None, ALU.max, writes=(b_s2,))
            P.v.reciprocal(s2[:], s2[:], writes=(b_s2,))
            P.v.tensor_scalar(fb[:], fb[:], s2[:], None, ALU.mult,
                              reads=(b_s2,), writes=(b_fb,))
            P.v.tensor_scalar(fb[:], fb[:], s1[:], None, ALU.mult,
                              reads=(b_s1,), writes=(b_fb,))
            P.v.tensor_scalar(s3[:], s1[:], -1.0, 1.0, ALU.mult, ALU.add,
                              reads=(b_s1,), writes=(b_s3,))
            P.v.tensor_scalar(wv_[:], wv_[:], s3[:], None, ALU.mult,
                              reads=(b_s3,), writes=(b_wv,))
            P.v.tensor_add(wv_[:], wv_[:], fb[:], reads=(b_fb,), writes=(b_wv,))
            P.v.reduce_sum(s2[:], wv_[:], axis=AX.X, reads=(b_wv,), writes=(b_s2,))
            P.v.tensor_scalar(s2[:], s2[:], 1e-8, None, ALU.max, writes=(b_s2,))
            P.v.reciprocal(s2[:], s2[:], writes=(b_s2,))
            P.v.tensor_scalar(wfin[:], wv_[:], s2[:], None, ALU.mult,
                              reads=(b_wv, b_s2), writes=(b_wfin,))
            j = c % 2
            tr_group(P, [(ptr[j][0:8, 0:128], wfin[:])], ident[:],
                     reads=(b_wfin, b_ident), writes=(b_ptr[j],))
            P.a.copy(wT[:, c * 128:(c + 1) * 128], ptr[j][0:8, 0:128],
                     reads=(b_ptr[j],), writes=(b_wT,))
    P.barrier()

    for th in range(2):
        t0 = th * 1024
        with contextlib.ExitStack() as es1:
            out_acc = es1.enter_context(nc.sbuf_tensor(f"oacc{th}", [128, 8, 2048], BF16))
            b_oacc = Buf("oacc")
            with contextlib.ExitStack() as es2:
                h_sb = es2.enter_context(nc.sbuf_tensor(f"h_sb{th}", [128, NDCH, 1024], BF16))
                b_h = Buf("h")
                wgu = [es2.enter_context(nc.sbuf_tensor(f"wgu{th}{i}", [128, NDCH, 256], BF16))
                       for i in range(2)]
                wd = [es2.enter_context(nc.sbuf_tensor(f"wd{th}{i}", [128, NDCH, 512], BF16))
                      for i in range(2)]
                b_wgu = [Buf(f"wgu{i}") for i in range(2)]
                b_wd = [Buf(f"wd{i}") for i in range(2)]
                wbc = es2.enter_context(nc.sbuf_tensor(f"wbc{th}", [128, 1024], BF16))
                b_wbc = Buf("wbc")
                sg = es2.enter_context(nc.sbuf_tensor(f"sg{th}", [128, 512], F32))
                b_sg = Buf("sg")
                pg = [es2.enter_context(nc.psum_tensor(f"pg{th}{i}", [128, 512], F32))
                      for i in range(2)]
                b_pg = [Buf(f"pg{i}") for i in range(2)]
                pu = [es2.enter_context(nc.psum_tensor(f"pu{th}{i}", [128, 512], F32))
                      for i in range(2)]
                b_pu = [Buf(f"pu{i}") for i in range(2)]
                pw = es2.enter_context(nc.psum_tensor(f"pw{th}", [128, 1024], F32))
                b_pw = Buf("pw")
                pd = [es2.enter_context(nc.psum_tensor(f"pd{th}{i}", [128, 512], F32))
                      for i in range(2)]
                b_pd = [Buf(f"pd{i}") for i in range(2)]

                for e in range(8):
                    mm_group(P, [(e8bc[:, e * 128:(e + 1) * 128],
                                  wT[0:8, t0 + g * 512:t0 + (g + 1) * 512],
                                  pw[:, g * 512:(g + 1) * 512], True, True) for g in range(2)],
                             reads=(b_e8, b_wT), writes=(b_pw,))
                    P.a.copy(wbc[:], pw[:], reads=(b_pw,), writes=(b_wbc,))
                    for fc in range(NDCH):
                        i = fc % 2
                        P.g.dma_start(
                            out=wgu[i][:],
                            in_=bass.AP(io["l1f_wgu"].tensor,
                                        (e * NDCH + fc) * D * 256,
                                        [[256, 128], [128 * 256, NDCH], [1, 256]]),
                            writes=(b_wgu[i],))
                        for g in range(2):
                            ii = g % 2
                            mm_acc(P, pg[ii][:],
                                   [(wgu[i][:, dd, 0:128],
                                     xinT[:, dd, t0 + g * 512:t0 + (g + 1) * 512])
                                    for dd in range(NDCH)],
                                   reads=(b_wgu[i], b_xinT), writes=(b_pg[ii],))
                            mm_acc(P, pu[ii][:],
                                   [(wgu[i][:, dd, 128:256],
                                     xinT[:, dd, t0 + g * 512:t0 + (g + 1) * 512])
                                    for dd in range(NDCH)],
                                   reads=(b_wgu[i], b_xinT), writes=(b_pu[ii],))
                            P.a.activation(sg[:], pg[ii][:], AF.Silu,
                                           reads=(b_pg[ii],), writes=(b_sg,))
                            P.v.tensor_mul(sg[:], sg[:], wbc[:, g * 512:(g + 1) * 512],
                                           reads=(b_wbc,), writes=(b_sg,))
                            P.v.tensor_mul(h_sb[:, fc, g * 512:(g + 1) * 512], sg[:], pu[ii][:],
                                           reads=(b_sg, b_pu[ii]), writes=(b_h,))
                    for og in range(4):
                        iw = og % 2
                        P.g.dma_start(out=wd[iw][:],
                                      in_=io["l1f_wd"][e * D:(e + 1) * D,
                                                       og * 512:(og + 1) * 512]
                                      .rearrange("(fc p) n -> p fc n", p=128),
                                      writes=(b_wd[iw],))
                        for cl in range(8):
                            ii = cl % 2
                            mm_acc(P, pd[ii][:],
                                   [(h_sb[:, fc, cl * 128:(cl + 1) * 128], wd[iw][:, fc, :])
                                    for fc in range(NDCH)],
                                   reads=(b_h, b_wd[iw]), writes=(b_pd[ii],))
                            if e == 0:
                                P.a.copy(out_acc[:, cl, og * 512:(og + 1) * 512],
                                         pd[ii][:], reads=(b_pd[ii],), writes=(b_oacc,))
                            else:
                                P.v.tensor_add(out_acc[:, cl, og * 512:(og + 1) * 512],
                                               out_acc[:, cl, og * 512:(og + 1) * 512],
                                               pd[ii][:], reads=(b_pd[ii],),
                                               writes=(b_oacc,))
            P.barrier()
            with contextlib.ExitStack() as es2:
                ln = LNStats(P, es2, f"l1f_post{th}")
                xc = [es2.enter_context(nc.sbuf_tensor(f"pf_xc{th}{i}", [128, 2048], F32))
                      for i in range(2)]
                b_xc = [Buf(f"xc{i}") for i in range(2)]
                lo = [es2.enter_context(nc.sbuf_tensor(f"pf_lo{th}{i}", [128, 2048], F32))
                      for i in range(2)]
                b_lo = [Buf(f"lo{i}") for i in range(2)]
                for cl in range(8):
                    ln.stat_chunk(P, cl, out_acc[:, cl, :], reads=(b_oacc,))
                ln.finish(P, D)
                stores = []
                for cl in range(8):
                    i = cl % 2
                    c = th * 8 + cl
                    P.sy.dma_start(out=xc[i][:], in_=io["x3"][c * 128:(c + 1) * 128, :],
                                   writes=(b_xc[i],))
                    ln.apply(P, cl, lo[i][:], out_acc[:, cl, :], reads=(b_oacc,),
                             writes=(b_lo[i],))
                    P.v.tensor_add(lo[i][:], lo[i][:], xc[i][:], reads=(b_xc[i],),
                                   writes=(b_lo[i],))
                    stores.append(P.sy.dma_start(out=io["x4"][c * 128:(c + 1) * 128, :],
                                                 in_=lo[i][:], reads=(b_lo[i],)))
        P.barrier()
    P.fence(stores)


def prep_l1f_inputs(full, core, x3=None):
    d = {
        "l1f_wgu": _f32(np.stack(
            [np.asarray(full["L1_ffn_Wg"]).reshape(8, D, NDCH, 128).transpose(0, 2, 1, 3),
             np.asarray(full["L1_ffn_Wu"]).reshape(8, D, NDCH, 128).transpose(0, 2, 1, 3)],
            axis=3).reshape(8 * NDCH * D, 256)),
        "l1f_wd": _f32(np.asarray(full["L1_ffn_Wd"]).reshape(8 * D, D)),
        "l1f_wr": _f32(full["L1_ffn_Wr"]),
        "l1f_tempB": np.full((128, 1), np.asarray(full["L1_ffn_temperature"]).ravel()[0],
                             np.float32),
        "ident": np.eye(128, dtype=ml_dtypes.bfloat16),
        "e8bc": np.ascontiguousarray(
            np.repeat(np.eye(8, dtype=np.float32), 128, axis=1)).astype(ml_dtypes.bfloat16),
    }
    if x3 is not None:
        cm = _chunk_map()[core]
        xr = x3.reshape(B * 32, 128, D)
        d["x3"] = _f32(np.concatenate([xr[b * 32 + j] for (b, j) in cm], axis=0))
    return d


def build_l1f():
    nc = bass.Bass("TRN2", target_bir_lowering=False)
    io = {}
    io["x3"] = nc.dram_tensor("x3", [TLOC, D], F32, kind="ExternalInput").ap()
    io["x4"] = nc.dram_tensor("x4", [TLOC, D], F32, kind="ExternalOutput").ap()
    for nm, shp in [("l1f_wgu", [8 * NDCH * D, 256]), ("l1f_wd", [8 * D, D]),
                    ("l1f_wr", [D, 8]), ("l1f_tempB", [128, 1])]:
        io[nm] = nc.dram_tensor(nm, shp, F32, kind="ExternalInput").ap()
    io["ident"] = nc.dram_tensor("ident", [128, 128], BF16, kind="ExternalInput").ap()
    io["e8bc"] = nc.dram_tensor("e8bc", [8, 1024], BF16, kind="ExternalInput").ap()
    P = Prog(nc)
    with contextlib.ExitStack() as es:
        emit_l1f(P, es, io)
        P.finalize_and_emit()
    P.close()
    return nc



class _NCProxy:

    def __init__(self, nc, tag):
        object.__setattr__(self, "_nc", nc)
        object.__setattr__(self, "_tag", tag)

    def sbuf_tensor(self, name, *a, **kw):
        return self._nc.sbuf_tensor(f"{self._tag}_{name}", *a, **kw)

    def psum_tensor(self, name, *a, **kw):
        return self._nc.psum_tensor(f"{self._tag}_{name}", *a, **kw)

    def __getattr__(self, k):
        return getattr(self._nc, k)


def build_fused(meta):
    nc = bass.Bass("TRN2", target_bir_lowering=False)
    io = {}
    io["x0"] = nc.dram_tensor("x0", [TLOC, D], F32, kind="ExternalInput").ap()
    io["x4"] = nc.dram_tensor("x4", [TLOC, D], F32, kind="ExternalOutput").ap()
    for nm in ["x1", "x2", "x3"]:
        io[nm] = nc.dram_tensor(nm, [TLOC, D], F32).ap()
    for nm, shp in [("l0a_wqT", [D, D]), ("l0a_wkT", [D, 512]), ("l0a_wvT", [D, 512]),
                    ("l0a_wgT", [D, D]), ("l0a_wmbT", [D, D]),
                    ("l0a_bqB", [128, D]), ("l0a_bkB", [128, 512]), ("l0a_bvB", [128, 512]),
                    ("l0a_bgB", [128, D]), ("l0a_bmbB", [128, D]),
                    ("l0a_mem", [16 * 768, 128]), ("l0a_norm", [16, 768]),
                    ("l0a_normb", [2048, 768])]:
        io[nm] = nc.dram_tensor(nm, shp, F32, kind="ExternalInput").ap()
    io["arin"] = nc.dram_tensor("arin", [12384, 128], F32).ap()
    io["arout"] = nc.dram_tensor("arout", [12384, 128], F32).ap()
    io["fence_in"] = nc.dram_tensor("fence_in", [8, 128], F32).ap()
    io["fence_out"] = nc.dram_tensor("fence_out", [8, 128], F32).ap()
    for nm, shp, dt in [("l0f_w0T", [D, D], F32), ("l0f_gwT", [D, 8], F32),
                        ("l0f_atoms", [64, NS], F32), ("l0f_eaw", [8, 64], F32),
                        ("l0f_imp", [8, NS], F32), ("l0f_sidxT", [128, 128], I32),
                        ("iota128", [128, 128], F32)]:
        io[nm] = nc.dram_tensor(nm, shp, dt, kind="ExternalInput").ap()
    io["xg_dram"] = nc.dram_tensor("xg_dram", [D, TLOC], BF16).ap()
    for nm in ["l1_wqT", "l1_wkT", "l1_wvT", "l1_woT"]:
        io[nm] = nc.dram_tensor(nm, [D, D], F32, kind="ExternalInput").ap()
    for nm, shp in [("ropeCos", [128, 2048]), ("ropeSin", [128, 2048]),
                    ("ropePermT", [128, 128]), ("ones128", [128, 128]),
                    ("maskA", [512, 512]), ("maskB", [512, 512]),
                    ("ident", [128, 128]), ("ones1", [1, 128]), ("e8bc", [8, 1024])]:
        io[nm] = nc.dram_tensor(nm, shp, BF16, kind="ExternalInput").ap()
    io["agin"] = nc.dram_tensor("agin", [4096, 2048], BF16).ap()
    io["agout"] = nc.dram_tensor("agout", [8192, 2048], BF16).ap()
    io["qT_dram"] = nc.dram_tensor("qT_dram", [2048, 2048], BF16).ap()
    io["f2in"] = nc.dram_tensor("f2in", [8, 128], F32).ap()
    io["f2out"] = nc.dram_tensor("f2out", [8, 128], F32).ap()
    for nm, shp in [("l1f_wgu", [8 * NDCH * D, 256]), ("l1f_wd", [8 * D, D]),
                    ("l1f_wr", [D, 8]), ("l1f_tempB", [128, 1])]:
        io[nm] = nc.dram_tensor(nm, shp, F32, kind="ExternalInput").ap()

    P = Prog(nc)
    real_nc = P.nc

    def phase(tag, emitter, io_in, io_out, *extra):
        d = dict(io)
        d[io_in[1]] = io[io_in[0]]
        d[io_out[1]] = io[io_out[0]]
        object.__setattr__(P, "nc", _NCProxy(real_nc, tag))
        with contextlib.ExitStack() as es:
            emitter(P, es, d, *extra)
        P.barrier()

    phase("A", emit_l0a, ("x0", "x0"), ("x1", "x1"))
    phase("B", emit_l0f, ("x1", "x1"), ("x2", "x2"), meta)
    phase("C", emit_l1a, ("x2", "x2"), ("x3", "x3"))
    phase("D", emit_l1f, ("x3", "x3"), ("x4", "x4"))
    object.__setattr__(P, "nc", real_nc)
    P.finalize_and_emit()
    P.close()
    return nc


def prep_fused_inputs(full, core, meta):
    d = {}
    d.update(prep_l0a_inputs(full, core))
    d.update(prep_l0f_inputs(full, core, meta))
    d.update(prep_l1a_inputs(full, core))
    d.update(prep_l1f_inputs(full, core))
    return d


FUSED = [True]


# version 08 (final)
# speedup vs baseline: 1.2794x; 1.0742x over previous
import contextlib
import math

import numpy as np
import ml_dtypes

import concourse.bass as bass
import concourse.mybir as mybir
from concourse.bass_utils import run_bass_kernel_spmd
from concourse.bass_core_rust import IndirectOffsetOnAxis

F32 = mybir.dt.float32
BF16 = mybir.dt.bfloat16
I32 = mybir.dt.int32
AF = mybir.ActivationFunctionType
ALU = mybir.AluOpType
AX = mybir.AxisListType

NCORES = 8
B, S, D = 4, 4096, 2048
TLOC = 2048
NCH = 16
NDCH = 16
EPS0 = 1e-8
FDIM = 768
H0 = 16

ENGINES = ("sync", "tensor", "scalar", "vector", "gpsimd")



class Op:
    __slots__ = ("engine", "fn", "after", "amt", "need_inc", "tok", "idx", "is_async")

    def __init__(self, engine, fn, after, amt, idx, is_async):
        self.engine, self.fn, self.after, self.amt, self.idx = engine, fn, after, amt, idx
        self.need_inc = False
        self.tok = None
        self.is_async = is_async


class Buf:
    __slots__ = ("writer", "readers", "name")

    def __init__(self, name=""):
        self.writer, self.readers, self.name = None, [], name


class _Rec:
    def __init__(self, P, engine):
        self._P, self._e = P, engine

    def __getattr__(self, name):
        P, engine = self._P, self._e
        is_dma = name.startswith("dma") or name == "indirect_dma_start"
        is_coll = name == "collective_compute"

        def record(*args, reads=(), writes=(), after=(), **kw):
            def fn(eng):
                return getattr(eng, name)(*args, **kw)
            return P.op(engine, fn, reads=reads, writes=writes, after=after,
                        dma=is_dma, is_async=is_dma or is_coll)
        return record


class Prog:
    def __init__(self, nc):
        self.nc = nc
        self.ops = []
        self._es = contextlib.ExitStack()
        self.sems = {e: self._es.enter_context(nc.semaphore(f"s_{e}")) for e in ENGINES}
        self.sy = _Rec(self, "sync")
        self.t = _Rec(self, "tensor")
        self.a = _Rec(self, "scalar")
        self.v = _Rec(self, "vector")
        self.g = _Rec(self, "gpsimd")
        self._consts = {}
        self._const_sb = self._es.enter_context(nc.sbuf_tensor("constants", [128, 16], F32))
        self._const_buf = Buf("constants")
        self._const_n = 0
        self._barrier = []

    def barrier(self):
        last = {}
        for o in self.ops:
            last[o.engine] = o
        self._barrier = list(last.values())

    def const_tile(self, v):
        v = float(v)
        if v not in self._consts:
            i = self._const_n
            assert i < 16
            self._const_n += 1
            self.v.memset(self._const_sb[:, i:i + 1], v, writes=(self._const_buf,))
            self._consts[v] = self._const_sb[:, i:i + 1]
        return self._consts[v]

    def op(self, engine, fn, reads=(), writes=(), after=(), dma=False, is_async=False):
        deps = list(after) + self._barrier
        for b in reads:
            if b.writer is not None:
                deps.append(b.writer)
        for b in writes:
            if b.writer is not None:
                deps.append(b.writer)
            deps.extend(b.readers)
        o = Op(engine, fn, tuple(dict.fromkeys(d for d in deps if d is not None)),
               16 if dma else 1, len(self.ops), is_async)
        if dma:
            o.need_inc = True
        for a in o.after:
            a.need_inc = True
        for b in reads:
            b.readers.append(o)
        for b in writes:
            b.writer, b.readers = o, []
        self.ops.append(o)
        return o

    def raw(self, engine, fn, reads=(), writes=(), after=(), dma=False):
        return self.op(engine, fn, reads=reads, writes=writes, after=after, dma=dma,
                       is_async=dma)

    def fence(self, after):
        last_dma = {}
        for o in self.ops:
            if o.amt == 16:
                last_dma[o.engine] = o
        return self.op("sync", lambda e: e.nop(),
                       after=list(after) + list(last_dma.values()))

    def finalize_and_emit(self):
        counts = {e: 0 for e in ENGINES}
        for o in self.ops:
            if o.need_inc:
                counts[o.engine] += o.amt
                o.tok = counts[o.engine]
        per = {e: [o for o in self.ops if o.engine == e] for e in ENGINES}
        with self.nc.Block() as block:
            for ename in ENGINES:
                if not per[ename]:
                    continue

                def run(eng, ename=ename):
                    waited = {}
                    for o in per[ename]:
                        for a in o.after:
                            if a.tok is None:
                                continue
                            if waited.get(a.engine, -1) < a.tok:
                                eng.wait_ge(self.sems[a.engine], a.tok)
                                waited[a.engine] = a.tok
                        res = o.fn(eng)
                        if o.need_inc:
                            last = res[-1] if isinstance(res, (list, tuple)) else res
                            assert last is not None, f"inc-needed op emitted no instr ({ename})"
                            last.then_inc(self.sems[ename], o.amt)
                getattr(block, ename)(run)

    def close(self):
        self._es.close()


def mm_group(P, pairs, reads=(), writes=(), after=()):
    for (l, r, o_ap, st, sp) in pairs:
        assert l.shape[0] == r.shape[0], f"K mismatch {l.shape} vs {r.shape}"
        assert o_ap.shape[-1] == r.shape[-1], f"N mismatch {o_ap.shape} vs {r.shape}"
        assert o_ap.shape[0] == l.shape[-1], f"M mismatch {o_ap.shape} vs {l.shape}"
    def fn(eng):
        return [eng.matmul(o_ap, l, r, start=st, stop=sp) for (l, r, o_ap, st, sp) in pairs]
    return P.raw("tensor", fn, reads=reads, writes=writes, after=after)


def mm_acc(P, psum_ap, lhs_rhs, reads=(), writes=(), after=()):
    n = len(lhs_rhs)
    pairs = [(l, r, psum_ap, i == 0, i == n - 1) for i, (l, r) in enumerate(lhs_rhs)]
    return mm_group(P, pairs, reads=reads, writes=writes, after=after)


def tr_group(P, items, ident, reads=(), writes=(), after=()):
    def fn(eng):
        return [eng.transpose(o, i, ident) for o, i in items]
    return P.raw("tensor", fn, reads=reads, writes=writes, after=after)



def _chunk_map():
    cores = []
    for core in range(NCORES):
        b, p = core // 2, core % 2
        cores.append([(b, 2 * s + p) for s in range(NCH)])
    return cores


def _bf(x):
    return np.ascontiguousarray(np.asarray(x).astype(ml_dtypes.bfloat16))


def _f32(x):
    return np.ascontiguousarray(np.asarray(x), dtype=np.float32)


def _tile(v, n=128):
    v = np.asarray(v, np.float32)
    return np.ascontiguousarray(np.broadcast_to(v[None, :], (n, v.shape[0])))


def rowap(dram_ap, row0, nrows, ncols=128):
    n = nrows * ncols
    return bass.AP(dram_ap.tensor, row0 * ncols, [[n, 1], [1, n]])



class LNStats:
    def __init__(self, P, es, tag):
        nc = P.nc
        self.mean_neg = es.enter_context(nc.sbuf_tensor(f"{tag}_mn", [128, NCH], F32))
        self.rstd = es.enter_context(nc.sbuf_tensor(f"{tag}_rs", [128, NCH], F32))
        self.sq = es.enter_context(nc.sbuf_tensor(f"{tag}_sq", [128, NCH], F32))
        self.mean2 = es.enter_context(nc.sbuf_tensor(f"{tag}_m2", [128, NCH], F32))
        self.junk = es.enter_context(nc.sbuf_tensor(f"{tag}_jk", [128, 2048], F32))
        self.b = Buf(tag + "_stats")

    def stat_chunk(self, P, c, ap, reads=(), after=()):
        P.v.reduce_sum(self.mean_neg[:, c:c + 1], ap, axis=AX.X,
                       reads=reads, writes=(self.b,), after=after)
        P.a.activation(self.junk[:, 0:ap.shape[-1]], ap, AF.Square,
                       accum_out=self.sq[:, c:c + 1],
                       reads=reads, writes=(self.b,), after=after)

    def finish(self, P, n):
        P.v.tensor_scalar(self.mean_neg[:], self.mean_neg[:], -1.0 / n, None,
                          ALU.mult, writes=(self.b,))
        P.v.tensor_mul(self.mean2[:], self.mean_neg[:], self.mean_neg[:], writes=(self.b,))
        P.v.tensor_scalar(self.sq[:], self.sq[:], 1.0 / n, None, ALU.mult, writes=(self.b,))
        P.v.tensor_sub(self.sq[:], self.sq[:], self.mean2[:], writes=(self.b,))
        P.a.activation(self.rstd[:], self.sq[:], AF.Sqrt, bias=P.const_tile(1e-5),
                       reads=(P._const_buf,), writes=(self.b,))
        P.v.reciprocal(self.rstd[:], self.rstd[:], writes=(self.b,))

    def apply(self, P, c, out_ap, in_ap, reads=(), writes=(), after=()):
        return P.v.tensor_scalar(out_ap, in_ap, self.mean_neg[:, c:c + 1],
                                 self.rstd[:, c:c + 1], ALU.add, ALU.mult,
                                 reads=tuple(reads) + (self.b,), writes=writes, after=after)


def emit_preln_xinT(P, es, x_dram, xinT, b_xinT, ident, b_ident, tag, dbg=None):
    nc = P.nc
    ln = LNStats(P, es, tag)
    xc = [es.enter_context(nc.sbuf_tensor(f"{tag}_xc{i}", [128, 2048], F32)) for i in range(2)]
    b_xc = [Buf(f"{tag}_xc{i}") for i in range(2)]
    lnc = [es.enter_context(nc.sbuf_tensor(f"{tag}_ln{i}", [128, 2048], BF16)) for i in range(2)]
    b_lnc = [Buf(f"{tag}_ln{i}") for i in range(2)]
    pt = [es.enter_context(nc.psum_tensor(f"{tag}_pt{i}", [128, 512], BF16)) for i in range(2)]
    b_pt = [Buf(f"{tag}_pt{i}") for i in range(2)]
    for c in range(NCH):
        i = c % 2
        P.sy.dma_start(out=xc[i][:], in_=x_dram[c * 128:(c + 1) * 128, :], writes=(b_xc[i],))
        ln.stat_chunk(P, c, xc[i][:], reads=(b_xc[i],))
    ln.finish(P, D)
    for c in range(NCH):
        i = c % 2
        P.sy.dma_start(out=xc[i][:], in_=x_dram[c * 128:(c + 1) * 128, :], writes=(b_xc[i],))
        ln.apply(P, c, lnc[i][:], xc[i][:], reads=(b_xc[i],), writes=(b_lnc[i],))
        if c == 0 and dbg is not None:
            P.g.dma_start(out=dbg["dbg_var"], in_=ln.sq[:], reads=(ln.b,))
            P.g.dma_start(out=dbg["dbg_lnc"], in_=lnc[i][:], reads=(b_lnc[i],))
            P.g.dma_start(out=dbg["dbg_mn"], in_=ln.mean_neg[:], reads=(ln.b,))
            P.g.dma_start(out=dbg["dbg_rstd"], in_=ln.rstd[:], reads=(ln.b,))
            P.g.dma_start(out=dbg["dbg_xc"], in_=xc[i][:], reads=(b_xc[i],))
        for g in range(4):
            j = (c * 4 + g) % 2
            tr_group(P, [(pt[j][:, k * 128:(k + 1) * 128],
                          lnc[i][:, (g * 4 + k) * 128:(g * 4 + k + 1) * 128])
                         for k in range(4)], ident[:],
                     reads=(b_lnc[i], b_ident), writes=(b_pt[j],))
            P.a.copy(xinT[:, g * 4:(g + 1) * 4, c * 128:(c + 1) * 128],
                     pt[j][:].rearrange("p (k f) -> p k f", k=4),
                     reads=(b_pt[j],), writes=(b_xinT,))


def emit_postln_residual(P, es, src_ap_fn, b_src, x_dram, xout_dram, tag, extra_reads=()):
    nc = P.nc
    ln = LNStats(P, es, tag)
    xc = [es.enter_context(nc.sbuf_tensor(f"{tag}_xc{i}", [128, 2048], F32)) for i in range(2)]
    b_xc = [Buf(f"{tag}_xc{i}") for i in range(2)]
    lo = [es.enter_context(nc.sbuf_tensor(f"{tag}_lo{i}", [128, 2048], F32)) for i in range(2)]
    b_lo = [Buf(f"{tag}_lo{i}") for i in range(2)]
    for c in range(NCH):
        ln.stat_chunk(P, c, src_ap_fn(c), reads=(b_src,) + tuple(extra_reads))
    ln.finish(P, D)
    stores = []
    for c in range(NCH):
        i = c % 2
        P.sy.dma_start(out=xc[i][:], in_=x_dram[c * 128:(c + 1) * 128, :], writes=(b_xc[i],))
        ln.apply(P, c, lo[i][:], src_ap_fn(c), reads=(b_src,), writes=(b_lo[i],))
        P.v.tensor_add(lo[i][:], lo[i][:], xc[i][:], reads=(b_xc[i],), writes=(b_lo[i],))
        st = P.sy.dma_start(out=xout_dram[c * 128:(c + 1) * 128, :], in_=lo[i][:],
                            reads=(b_lo[i],))
        stores.append(st)
    return stores



def emit_l0a(P, es, io):
    nc = P.nc
    ident = es.enter_context(nc.sbuf_tensor("ident_sb", [128, 128], BF16))
    b_ident = Buf("ident")
    P.sy.dma_start(out=ident[:], in_=io["ident"][:, :], writes=(b_ident,))

    xinT = es.enter_context(nc.sbuf_tensor("xinT", [128, NDCH, 2048], BF16))
    b_xinT = Buf("xinT")
    u4sh = es.enter_context(nc.sbuf_tensor("dpfp_u_sh", [128, NCH, 256], BF16))
    b_u4sh = Buf("u4sh")

    with contextlib.ExitStack() as es1:
        emit_preln_xinT(P, es1, io["x0"], xinT, b_xinT, ident, b_ident, "l0a_pre",
                        dbg=io if (13 in DBG_SET and "dbg_lnc" in io) else None)
    P.barrier()

    arin, arout = io["arin"], io["arout"]
    b_arin = Buf("arin")

    with contextlib.ExitStack() as es1:
        k_bf = es1.enter_context(nc.sbuf_tensor("k_bf", [128, NCH, 512], BF16))
        v_bf = es1.enter_context(nc.sbuf_tensor("v_bf", [128, NCH, 512], BF16))
        b_k, b_v = Buf("k"), Buf("v")
        wsl = es1.enter_context(nc.sbuf_tensor("wsl", [128, NDCH, 512], BF16))
        b_wsl = Buf("wsl")
        bias = es1.enter_context(nc.sbuf_tensor("biasb", [128, 512], F32))
        b_bias = Buf("biasb")
        pk = [es1.enter_context(nc.psum_tensor(f"pk{i}", [128, 512], F32)) for i in range(2)]
        b_pk = [Buf(f"pk{i}") for i in range(2)]

        for (wname, bname, dst, b_dst) in [("l0a_wkT", "l0a_bkB", k_bf, b_k),
                                           ("l0a_wvT", "l0a_bvB", v_bf, b_v)]:
            P.g.dma_start(out=wsl[:], in_=io[wname].rearrange("(dd p) n -> p dd n", p=128),
                          writes=(b_wsl,))
            P.sy.dma_start(out=bias[:], in_=io[bname][:, :], writes=(b_bias,))
            for c in range(NCH):
                i = c % 2
                mm_acc(P, pk[i][:],
                       [(xinT[:, dd, c * 128:(c + 1) * 128], wsl[:, dd, :]) for dd in range(NDCH)],
                       reads=(b_xinT, b_wsl), writes=(b_pk[i],))
                P.v.tensor_add(dst[:, c, :], pk[i][:], bias[:],
                               reads=(b_pk[i], b_bias), writes=(b_dst,))

        mk_bf = es1.enter_context(nc.sbuf_tensor("mk_bf", [128, NCH, FDIM], BF16))
        mkT = es1.enter_context(nc.sbuf_tensor("mkT", [128, 6, 2048], BF16))
        b_mk, b_mkT = Buf("mk"), Buf("mkT")
        mb_g = es1.enter_context(nc.sbuf_tensor("mb_g", [128, NCH, 512], BF16))
        b_mb = Buf("mb")
        wmv = es1.enter_context(nc.sbuf_tensor("wmv", [128, NCH, 128], BF16))
        b_wmv = Buf("wmv")
        u4, b_u4 = u4sh, b_u4sh
        nsq = es1.enter_context(nc.sbuf_tensor("nsq", [128, NCH], F32))
        rno = es1.enter_context(nc.sbuf_tensor("rno", [128, NCH], F32))
        mksq = es1.enter_context(nc.sbuf_tensor("mksq", [128, NCH], F32))
        b_nsq, b_rno, b_mksq = Buf("nsq"), Buf("rno"), Buf("mksq")
        den = es1.enter_context(nc.sbuf_tensor("den", [128, NCH], F32))
        rden = es1.enter_context(nc.sbuf_tensor("rden", [128, NCH], F32))
        coef = es1.enter_context(nc.sbuf_tensor("coef", [128, NCH], BF16))
        scr = es1.enter_context(nc.sbuf_tensor("scr", [128, NCH], F32))
        b_den, b_rden, b_coef, b_scr = Buf("den"), Buf("rden"), Buf("coef"), Buf("scr")
        normb = es1.enter_context(nc.sbuf_tensor("normb", [128, FDIM], BF16))
        b_normb = Buf("normb")
        mem_h = es1.enter_context(nc.sbuf_tensor("mem_h", [128, 6, 128], BF16))
        b_memh = Buf("memh")
        dsc = es1.enter_context(nc.sbuf_tensor("delta_s", [128, 768], F32))
        b_dsc = Buf("delta_s")
        dnr = es1.enter_context(nc.sbuf_tensor("dn_s", [1, 768], F32))
        b_dnr = Buf("dn_s")
        junk = es1.enter_context(nc.sbuf_tensor("jk768", [128, FDIM], F32))
        pv_sb = es1.enter_context(nc.sbuf_tensor("pv_sb", [128, NCH, 128], BF16))
        b_pvsb = Buf("pv_sb")

        pt = [es1.enter_context(nc.psum_tensor(f"qt{i}", [128, 512], BF16)) for i in range(2)]
        b_pt = [Buf(f"qt{i}") for i in range(2)]
        p768 = es1.enter_context(nc.psum_tensor("p768", [128, 768], F32))
        b_p768 = Buf("p768")
        p1x = es1.enter_context(nc.psum_tensor("p1x", [1, 768], F32))
        b_p1x = Buf("p1x")

        def dpfp_batch(dst4, b_dst, src4_ap, src_bufs):
            P.a.activation(u4[:, :, 0:128], src4_ap, AF.Relu, reads=src_bufs, writes=(b_u4,))
            P.a.activation(u4[:, :, 128:256], src4_ap, AF.Relu, scale=-1.0,
                           reads=src_bufs, writes=(b_u4,))
            for j in (1, 2, 3):
                s = (j - 1) * 256
                P.v.tensor_copy(dst4[:, :, s:s + j], u4[:, :, 256 - j:256],
                                reads=(b_u4,), writes=(b_dst,))
                P.v.tensor_copy(dst4[:, :, s + j:s + 256], u4[:, :, 0:256 - j],
                                reads=(b_u4,), writes=(b_dst,))
            for j in (1, 2, 3):
                s = (j - 1) * 256
                P.v.tensor_mul(dst4[:, :, s:s + 256], dst4[:, :, s:s + 256], u4[:, :, 0:256],
                               reads=(b_u4,), writes=(b_dst,))
            for c in range(NCH):
                P.a.activation(junk[:], dst4[:, c, :], AF.Square, accum_out=nsq[:, c:c + 1],
                               reads=(b_dst,), writes=(b_nsq,))

        def norm_scale(dst4, b_dst, with_mksq):
            P.a.activation(rno[:], nsq[:], AF.Sqrt, reads=(b_nsq,), writes=(b_rno,))
            P.v.tensor_scalar(rno[:], rno[:], 1e-12, None, ALU.max, writes=(b_rno,))
            P.v.reciprocal(rno[:], rno[:], writes=(b_rno,))
            for c in range(NCH):
                P.v.tensor_scalar(dst4[:, c, :], dst4[:, c, :], rno[:, c:c + 1], None,
                                  ALU.mult, reads=(b_rno,), writes=(b_dst,))
                if with_mksq:
                    P.v.scalar_tensor_tensor(junk[:], dst4[:, c, :], 1.0, dst4[:, c, :],
                                             ALU.bypass, ALU.mult, accum_out=mksq[:, c:c + 1],
                                             reads=(b_dst,), writes=(b_mksq,))

        def transpose_to(dstT, b_dstT, src4, b_src):
            for jj in range(6):
                for cg in range(4):
                    j = (jj * 4 + cg) % 2
                    tr_group(P, [(pt[j][:, k * 128:(k + 1) * 128],
                                  src4[:, cg * 4 + k, jj * 128:(jj + 1) * 128])
                                 for k in range(4)], ident[:],
                             reads=(b_src, b_ident), writes=(b_pt[j],))
                    P.a.copy(dstT[:, jj, cg * 512:(cg + 1) * 512], pt[j][:],
                             reads=(b_pt[j],), writes=(b_dstT,))

        if "dbg_xinT" in io and 1 in DBG_SET:
            if 11 in DBG_SET:
                for dd in range(NDCH):
                    P.g.dma_start(out=io["dbg_xinT"][dd * 128:(dd + 1) * 128, :],
                                  in_=xinT[:, dd, :], reads=(b_xinT,))
            if 12 in DBG_SET:
                for c2 in range(NCH):
                    P.g.dma_start(out=io["dbg_k"][:, c2 * 512:(c2 + 1) * 512],
                                  in_=k_bf[:, c2, :], reads=(b_k,))
                    P.g.dma_start(out=io["dbg_v"][:, c2 * 512:(c2 + 1) * 512],
                                  in_=v_bf[:, c2, :], reads=(b_v,))
        for g in range(4):
            dpfp_batch(mk_bf, b_mk, k_bf[:, :, g * 128:(g + 1) * 128], (b_k,))
            norm_scale(mk_bf, b_mk, with_mksq=True)
            if g == 0 and "dbg_mk" in io and 2 in DBG_SET:
                for c2 in range(NCH):
                    P.g.dma_start(out=io["dbg_mk"][:, c2 * FDIM:(c2 + 1) * FDIM],
                                   in_=mk_bf[:, c2, :], reads=(b_mk,))
                P.g.dma_start(out=io["dbg_mksq"], in_=mksq[:], reads=(b_mksq,))
            transpose_to(mkT, b_mkT, mk_bf, b_mk)
            P.g.dma_start(out=wsl[:], in_=io["l0a_wmbT"][:, g * 512:(g + 1) * 512]
                          .rearrange("(dd p) n -> p dd n", p=128), writes=(b_wsl,))
            P.sy.dma_start(out=bias[:], in_=io["l0a_bmbB"][:, g * 512:(g + 1) * 512],
                           writes=(b_bias,))
            for c in range(NCH):
                i = c % 2
                mm_acc(P, pk[i][:],
                       [(xinT[:, dd, c * 128:(c + 1) * 128], wsl[:, dd, :]) for dd in range(NDCH)],
                       reads=(b_xinT, b_wsl), writes=(b_pk[i],))
                P.v.tensor_add(mb_g[:, c, :], pk[i][:], bias[:],
                               reads=(b_pk[i], b_bias), writes=(b_mb,))
                P.a.activation(mb_g[:, c, :], mb_g[:, c, :], AF.Sigmoid, writes=(b_mb,))
            for hh in range(4):
                h = 4 * g + hh
                P.g.dma_start(out=normb[:], in_=io["l0a_normb"][h * 128:(h + 1) * 128, :],
                              writes=(b_normb,))
                P.g.dma_start(out=mem_h[:], in_=io["l0a_mem"][h * 768:(h + 1) * 768, :]
                              .rearrange("(jj p) d -> p jj d", p=128), writes=(b_memh,))
                for c in range(NCH):
                    P.v.scalar_tensor_tensor(junk[:], mk_bf[:, c, :], 1.0, normb[:],
                                             ALU.bypass, ALU.mult, accum_out=den[:, c:c + 1],
                                             reads=(b_mk, b_normb), writes=(b_den,))
                P.v.tensor_scalar(den[:], den[:], EPS0, None, ALU.add, writes=(b_den,))
                P.v.reciprocal(rden[:], den[:], reads=(b_den,), writes=(b_rden,))
                P.v.reciprocal(scr[:], mksq[:], reads=(b_mksq,), writes=(b_scr,))
                P.v.tensor_mul(scr[:], den[:], scr[:], reads=(b_den,), writes=(b_scr,))
                P.v.tensor_scalar(scr[:], scr[:], -1.0, 1.0, ALU.mult, ALU.add, writes=(b_scr,))
                P.v.tensor_scalar(coef[:], scr[:], 0.0, 1.0, ALU.max, ALU.min,
                                  reads=(b_scr,), writes=(b_coef,))
                for c in range(NCH):
                    i = c % 2
                    mm_acc(P, pk[i][:, 0:128],
                           [(mkT[:, jj, c * 128:(c + 1) * 128], mem_h[:, jj, :])
                            for jj in range(6)],
                           reads=(b_mkT, b_memh), writes=(b_pk[i],))
                    P.v.tensor_scalar(pv_sb[:, c, :], pk[i][:, 0:128], rden[:, c:c + 1],
                                      None, ALU.mult, reads=(b_pk[i], b_rden),
                                      writes=(b_pvsb,))
                P.v.tensor_sub(pv_sb[:], v_bf[:, :, g * 128:(g + 1) * 128], pv_sb[:],
                               reads=(b_v,), writes=(b_pvsb,))
                P.v.tensor_mul(wmv[:], pv_sb[:], mb_g[:, :, hh * 128:(hh + 1) * 128],
                               reads=(b_pvsb, b_mb), writes=(b_wmv,))
                if h == 0 and "dbg_den" in io and 3 in DBG_SET:
                    P.g.dma_start(out=io["dbg_den"], in_=den[:], reads=(b_den,))
                    P.g.dma_start(out=io["dbg_coef"], in_=coef[:], reads=(b_coef,))
                    P.g.dma_start(out=io["dbg_wmv"],
                                   in_=wmv[:].rearrange("p c n -> p (c n)"), reads=(b_wmv,))
                    for c2 in range(NCH):
                        P.g.dma_start(out=io["dbg_mb"][:, c2 * 512:(c2 + 1) * 512],
                                       in_=mb_g[:, c2, :], reads=(b_mb,))
                for jj in range(6):
                    mm_group(P, [(mk_bf[:, c, jj * 128:(jj + 1) * 128], wmv[:, c, :],
                                  p768[:, jj * 128:(jj + 1) * 128], c == 0, c == NCH - 1)
                                 for c in range(NCH)],
                             reads=(b_mk, b_wmv), writes=(b_p768,))
                P.a.copy(dsc[:], p768[:], reads=(b_p768,), writes=(b_dsc,))
                P.sy.dma_start(out=arin[h * 768:(h + 1) * 768, :]
                               .rearrange("(jj p) d -> p jj d", p=128),
                               in_=dsc[:].rearrange("p (jj d) -> p jj d", jj=6),
                               reads=(b_dsc,), writes=(b_arin,))
                mm_group(P, [(coef[:, c:c + 1], mk_bf[:, c, 0:512],
                              p1x[:, 0:512], c == 0, c == NCH - 1) for c in range(NCH)] +
                            [(coef[:, c:c + 1], mk_bf[:, c, 512:768],
                              p1x[:, 512:768], c == 0, c == NCH - 1) for c in range(NCH)],
                         reads=(b_coef, b_mk), writes=(b_p1x,))
                P.a.copy(dnr[:], p1x[:], reads=(b_p1x,), writes=(b_dnr,))
                P.sy.dma_start(out=rowap(arin, 12288 + 6 * h, 6),
                               in_=dnr[:], reads=(b_dnr,), writes=(b_arin,))

    P.barrier()

    b_arout = Buf("arout")
    P.g.collective_compute(
        "AllReduce", ALU.add, replica_groups=[list(range(NCORES))],
        ins=[arin.tensor.ap().opt()], outs=[arout.tensor.ap().opt()],
        reads=(b_arin,), writes=(b_arout,))
    P.g.collective_compute(
        "AllReduce", ALU.add, replica_groups=[list(range(NCORES))],
        ins=[io["fence_in"].tensor.ap().opt()], outs=[io["fence_out"].tensor.ap().opt()],
        writes=(b_arout,))
    if "dbg_arout" in io and 6 in DBG_SET:
        P.g.dma_start(out=io["dbg_arout"], in_=arout[:, :], reads=(b_arout,))

    with contextlib.ExitStack() as es0:
      attn = es0.enter_context(nc.sbuf_tensor("attn", [128, NCH, 2048], BF16))
      b_attn = Buf("attn")
      with contextlib.ExitStack() as es1:
        mq_bf = es1.enter_context(nc.sbuf_tensor("mq_bf", [128, NCH, FDIM], BF16))
        mqT = es1.enter_context(nc.sbuf_tensor("mqT", [128, 6, 2048], BF16))
        b_mq, b_mqT = Buf("mq"), Buf("mqT")
        q_h = es1.enter_context(nc.sbuf_tensor("q_h", [128, NCH, 128], BF16))
        b_qh = Buf("qh")
        u4, b_u4 = u4sh, b_u4sh
        nsq = es1.enter_context(nc.sbuf_tensor("nsq2", [128, NCH], F32))
        rno = es1.enter_context(nc.sbuf_tensor("rno2", [128, NCH], F32))
        b_nsq, b_rno = Buf("nsq2"), Buf("rno2")
        dena = es1.enter_context(nc.sbuf_tensor("dena", [128, NCH], F32))
        rdena = es1.enter_context(nc.sbuf_tensor("rdena", [128, NCH], F32))
        b_dena, b_rdena = Buf("dena"), Buf("rdena")
        nm_h = es1.enter_context(nc.sbuf_tensor("nm_h", [128, 6, 128], BF16))
        nm_f = es1.enter_context(nc.sbuf_tensor("nm_f", [128, 6, 128], BF16))
        b_nmh, b_nmf = Buf("nmh"), Buf("nmf")
        nn_row = es1.enter_context(nc.sbuf_tensor("nn_row", [1, 768], F32))
        nn_bf = es1.enter_context(nc.sbuf_tensor("nn_bf", [1, 768], BF16))
        nnb = es1.enter_context(nc.sbuf_tensor("nnb", [128, FDIM], BF16))
        b_nnr, b_nnbf, b_nnb = Buf("nnr"), Buf("nnbf"), Buf("nnb")
        ones1 = es1.enter_context(nc.sbuf_tensor("ones1_sb", [1, 128], BF16))
        b_ones = Buf("ones1")
        P.sy.dma_start(out=ones1[:], in_=io["ones1"][:, :], writes=(b_ones,))
        wq_h = es1.enter_context(nc.sbuf_tensor("wq_h", [128, NDCH, 128], BF16))
        wg_h = es1.enter_context(nc.sbuf_tensor("wg_h", [128, NDCH, 128], BF16))
        b_wq, b_wg = Buf("wq"), Buf("wg")
        biasq = es1.enter_context(nc.sbuf_tensor("biasq", [128, 128], BF16))
        biasg = es1.enter_context(nc.sbuf_tensor("biasg", [128, 128], BF16))
        b_bq, b_bg = Buf("bq"), Buf("bg")
        t_mh = [es1.enter_context(nc.sbuf_tensor(f"t_mh{i}", [128, 128], BF16))
                for i in range(2)]
        t_g = [es1.enter_context(nc.sbuf_tensor(f"t_g{i}", [128, 128], BF16))
               for i in range(2)]
        b_tmh = [Buf(f"tmh{i}") for i in range(2)]
        b_tg = [Buf(f"tg{i}") for i in range(2)]

        pk = [es1.enter_context(nc.psum_tensor(f"qq{i}", [128, 512], F32)) for i in range(2)]
        b_pk = [Buf(f"qq{i}") for i in range(2)]
        pgg = [es1.enter_context(nc.psum_tensor(f"qg{i}", [128, 512], F32)) for i in range(2)]
        b_pgg = [Buf(f"qg{i}") for i in range(2)]
        pt = [es1.enter_context(nc.psum_tensor(f"qw{i}", [128, 512], BF16)) for i in range(2)]
        b_pt = [Buf(f"qw{i}") for i in range(2)]
        p768 = es1.enter_context(nc.psum_tensor("p768b", [128, 768], F32))
        b_p768 = Buf("p768b")

        def dpfp_batch2(dst4, b_dst, src4_ap, src_bufs, dump_ap):
            P.a.activation(u4[:, :, 0:128], src4_ap, AF.Relu, reads=src_bufs, writes=(b_u4,))
            P.a.activation(u4[:, :, 128:256], src4_ap, AF.Relu, scale=-1.0,
                           reads=src_bufs, writes=(b_u4,))
            for j in (1, 2, 3):
                s = (j - 1) * 256
                P.v.tensor_copy(dst4[:, :, s:s + j], u4[:, :, 256 - j:256],
                                reads=(b_u4,), writes=(b_dst,))
                P.v.tensor_copy(dst4[:, :, s + j:s + 256], u4[:, :, 0:256 - j],
                                reads=(b_u4,), writes=(b_dst,))
            for j in (1, 2, 3):
                s = (j - 1) * 256
                P.v.tensor_mul(dst4[:, :, s:s + 256], dst4[:, :, s:s + 256], u4[:, :, 0:256],
                               reads=(b_u4,), writes=(b_dst,))
            for c in range(NCH):
                P.a.activation(dump_ap, dst4[:, c, :], AF.Square,
                               accum_out=nsq[:, c:c + 1], reads=(b_dst,), writes=(b_nsq,))

        for h in range(H0):
            P.g.dma_start(out=wq_h[:], in_=io["l0a_wqT"][:, h * 128:(h + 1) * 128]
                          .rearrange("(dd p) n -> p dd n", p=128), writes=(b_wq,))
            P.g.dma_start(out=wg_h[:], in_=io["l0a_wgT"][:, h * 128:(h + 1) * 128]
                          .rearrange("(dd p) n -> p dd n", p=128), writes=(b_wg,))
            P.g.dma_start(out=biasq[:], in_=io["l0a_bqB"][:, h * 128:(h + 1) * 128],
                          writes=(b_bq,))
            P.g.dma_start(out=biasg[:], in_=io["l0a_bgB"][:, h * 128:(h + 1) * 128],
                          writes=(b_bg,))
            for c in range(NCH):
                i = c % 2
                mm_acc(P, pk[i][:, 0:128],
                       [(xinT[:, dd, c * 128:(c + 1) * 128], wq_h[:, dd, :])
                        for dd in range(NDCH)],
                       reads=(b_xinT, b_wq), writes=(b_pk[i],))
                P.v.tensor_add(q_h[:, c, :], pk[i][:, 0:128], biasq[:],
                               reads=(b_pk[i], b_bq), writes=(b_qh,))
            dpfp_batch2(mq_bf, b_mq, q_h[:], (b_qh,),
                        attn[:, 0:6, h * 128:(h + 1) * 128])
            P.a.activation(rno[:], nsq[:], AF.Sqrt, reads=(b_nsq,), writes=(b_rno,))
            P.v.tensor_scalar(rno[:], rno[:], 1e-12, None, ALU.max, writes=(b_rno,))
            P.v.reciprocal(rno[:], rno[:], writes=(b_rno,))
            for c in range(NCH):
                P.v.tensor_scalar(mq_bf[:, c, :], mq_bf[:, c, :], rno[:, c:c + 1], None,
                                  ALU.mult, reads=(b_rno,), writes=(b_mq,))
            for jj in range(6):
                for cg in range(4):
                    j = (jj * 4 + cg) % 2
                    tr_group(P, [(pt[j][:, k * 128:(k + 1) * 128],
                                  mq_bf[:, cg * 4 + k, jj * 128:(jj + 1) * 128])
                                 for k in range(4)], ident[:],
                             reads=(b_mq, b_ident), writes=(b_pt[j],))
                    P.a.copy(mqT[:, jj, cg * 512:(cg + 1) * 512], pt[j][:],
                             reads=(b_pt[j],), writes=(b_mqT,))
            P.g.dma_start(out=nm_f[:], in_=io["l0a_mem"][h * 768:(h + 1) * 768, :]
                          .rearrange("(jj p) d -> p jj d", p=128), writes=(b_nmf,))
            P.g.dma_start(out=nm_h[:], in_=arout[h * 768:(h + 1) * 768, :]
                          .rearrange("(jj p) d -> p jj d", p=128),
                          reads=(b_arout,), writes=(b_nmh,))
            P.v.tensor_add(nm_h[:], nm_f[:], nm_h[:], reads=(b_nmf,), writes=(b_nmh,))
            P.sy.dma_start(out=nn_row[:], in_=io["l0a_norm"][h:h + 1, :], writes=(b_nnr,))
            P.g.dma_start(out=nn_row[:], in_=rowap(arout, 12288 + 6 * h, 6),
                          accum_op=ALU.add, reads=(b_arout,), writes=(b_nnr,))
            P.v.tensor_copy(nn_bf[:], nn_row[:], reads=(b_nnr,), writes=(b_nnbf,))
            mm_group(P, [(ones1[:], nn_bf[:, 0:512], p768[:, 0:512], True, True),
                         (ones1[:], nn_bf[:, 512:768], p768[:, 512:768], True, True)],
                     reads=(b_ones, b_nnbf), writes=(b_p768,))
            P.a.copy(nnb[:], p768[:], reads=(b_p768,), writes=(b_nnb,))
            for c in range(NCH):
                P.v.scalar_tensor_tensor(q_h[:, 0:6, :], mq_bf[:, c, :], 1.0, nnb[:],
                                         ALU.bypass, ALU.mult, accum_out=dena[:, c:c + 1],
                                         reads=(b_mq, b_nnb), writes=(b_dena,))
            P.v.tensor_scalar(dena[:], dena[:], EPS0, None, ALU.add, writes=(b_dena,))
            P.v.reciprocal(rdena[:], dena[:], reads=(b_dena,), writes=(b_rdena,))
            if h == 0 and "dbg_mq" in io and 4 in DBG_SET:
                for c2 in range(NCH):
                    P.g.dma_start(out=io["dbg_mq"][:, c2 * FDIM:(c2 + 1) * FDIM],
                                   in_=mq_bf[:, c2, :], reads=(b_mq,))
                P.g.dma_start(out=io["dbg_dena"], in_=dena[:], reads=(b_dena,))
                P.g.dma_start(out=io["dbg_nm"], in_=nm_h[:].rearrange("p a b -> p (a b)"),
                               reads=(b_nmh,))
                P.g.dma_start(out=io["dbg_nnb"], in_=nnb[:], reads=(b_nnb,))
            for c in range(NCH):
                i = c % 2
                mm_acc(P, pk[i][:, 0:128],
                       [(mqT[:, jj, c * 128:(c + 1) * 128], nm_h[:, jj, :]) for jj in range(6)],
                       reads=(b_mqT, b_nmh), writes=(b_pk[i],))
                P.v.tensor_scalar(t_mh[i][:], pk[i][:, 0:128], rdena[:, c:c + 1], None,
                                  ALU.mult, reads=(b_pk[i], b_rdena), writes=(b_tmh[i],))
                mm_acc(P, pgg[i][:, 0:128],
                       [(xinT[:, dd, c * 128:(c + 1) * 128], wg_h[:, dd, :])
                        for dd in range(NDCH)],
                       reads=(b_xinT, b_wg), writes=(b_pgg[i],))
                P.v.tensor_add(t_g[i][:], pgg[i][:, 0:128], biasg[:],
                               reads=(b_pgg[i], b_bg), writes=(b_tg[i],))
                P.a.activation(t_g[i][:], t_g[i][:], AF.Sigmoid, writes=(b_tg[i],))
                j = c % 2
                tr_group(P, [(pt[j][:, 0:128], xinT[:, h, c * 128:(c + 1) * 128])],
                         ident[:], reads=(b_xinT, b_ident), writes=(b_pt[j],))
                P.v.tensor_mul(t_g[i][:], t_mh[i][:], t_g[i][:],
                               reads=(b_tmh[i],), writes=(b_tg[i],))
                P.v.tensor_add(attn[:, c, h * 128:(h + 1) * 128], t_g[i][:], pt[j][:, 0:128],
                               reads=(b_tg[i], b_pt[j]), writes=(b_attn,))

      if "dbg_attn" in io and 5 in DBG_SET:
          for c2 in range(NCH):
              P.g.dma_start(out=io["dbg_attn"][c2 * 128:(c2 + 1) * 128, :],
                             in_=attn[:, c2, :], reads=(b_attn,))
      P.barrier()
      with contextlib.ExitStack() as es1:
        stores = emit_postln_residual(P, es1, lambda c: attn[:, c, :], b_attn,
                                      io["x0"], io["x1"], "l0a_post")
        P.fence(stores)



def prep_l0a_inputs(full, core):
    cm = _chunk_map()[core]
    hs = np.asarray(full["L0_attn_hidden_states"], np.float32).reshape(B * 32, 128, D)
    rows = np.concatenate([hs[b * 32 + j] for (b, j) in cm], axis=0)
    return {
        "x0": _f32(rows),
        "l0a_wqT": _f32(np.asarray(full["L0_attn_Wq"]).T),
        "l0a_wkT": _f32(np.asarray(full["L0_attn_Wk"]).T),
        "l0a_wvT": _f32(np.asarray(full["L0_attn_Wv"]).T),
        "l0a_wgT": _f32(np.asarray(full["L0_attn_Wg"]).T),
        "l0a_wmbT": _f32(np.asarray(full["L0_attn_Wmb"]).T),
        "l0a_bqB": _tile(full["L0_attn_bq"]),
        "l0a_bkB": _tile(full["L0_attn_bk"]),
        "l0a_bvB": _tile(full["L0_attn_bv"]),
        "l0a_bgB": _tile(full["L0_attn_bg"]),
        "l0a_bmbB": _tile(full["L0_attn_bmb"]),
        "l0a_mem": _f32(np.asarray(full["L0_attn_memory"]).reshape(16 * 768, 128)),
        "l0a_norm": _f32(full["L0_attn_norm"]),
        "l0a_normb": _f32(np.repeat(np.asarray(full["L0_attn_norm"], np.float32), 128, axis=0)),
        "ident": np.eye(128, dtype=ml_dtypes.bfloat16),
        "ones1": np.ones((1, 128), dtype=ml_dtypes.bfloat16),
    }


DBG = [False]
DBG_AROUT = [True]
DBG_SET = set(range(10)) | {11, 12, 13}


def build_l0a():
    nc = bass.Bass("TRN2", target_bir_lowering=False)
    io = {}
    io["x0"] = nc.dram_tensor("x0", [TLOC, D], F32, kind="ExternalInput").ap()
    io["x1"] = nc.dram_tensor("x1", [TLOC, D], F32, kind="ExternalOutput").ap()
    for nm, shp in [("l0a_wqT", [D, D]), ("l0a_wkT", [D, 512]), ("l0a_wvT", [D, 512]),
                    ("l0a_wgT", [D, D]), ("l0a_wmbT", [D, D]),
                    ("l0a_bqB", [128, D]), ("l0a_bkB", [128, 512]), ("l0a_bvB", [128, 512]),
                    ("l0a_bgB", [128, D]), ("l0a_bmbB", [128, D]),
                    ("l0a_mem", [16 * 768, 128]), ("l0a_norm", [16, 768]),
                    ("l0a_normb", [2048, 768])]:
        io[nm] = nc.dram_tensor(nm, shp, F32, kind="ExternalInput").ap()
    io["ident"] = nc.dram_tensor("ident", [128, 128], BF16, kind="ExternalInput").ap()
    io["ones1"] = nc.dram_tensor("ones1", [1, 128], BF16, kind="ExternalInput").ap()
    if DBG[0]:
        _sets = {"dbg_xinT": 1, "dbg_k": 1, "dbg_v": 1, "dbg_mk": 2, "dbg_mksq": 2,
                 "dbg_den": 3, "dbg_coef": 3, "dbg_wmv": 3, "dbg_mb": 3,
                 "dbg_mq": 4, "dbg_dena": 4, "dbg_nm": 4, "dbg_nnb": 4,
                 "dbg_attn": 5, "dbg_arout": 6}
        for nm, shp, dt in [("dbg_xinT", [2048, 2048], F32), ("dbg_k", [128, NCH * 512], F32),
                            ("dbg_v", [128, NCH * 512], F32), ("dbg_mk", [128, NCH * FDIM], F32),
                            ("dbg_mksq", [128, NCH], F32), ("dbg_den", [128, NCH], F32),
                            ("dbg_coef", [128, NCH], F32), ("dbg_wmv", [128, NCH * 128], F32),
                            ("dbg_mb", [128, NCH * 512], F32),
                            ("dbg_mq", [128, NCH * FDIM], F32), ("dbg_dena", [128, NCH], F32),
                            ("dbg_nm", [128, 768], F32), ("dbg_nnb", [128, FDIM], F32),
                            ("dbg_attn", [2048, 2048], F32),
                            ("dbg_arout", [12384, 128], F32)]:
            if _sets[nm] in DBG_SET:
                io[nm] = nc.dram_tensor(nm, shp, dt, kind="ExternalOutput").ap()
        if 13 in DBG_SET:
            for nm, shp in [("dbg_var", [128, NCH]), ("dbg_lnc", [128, 2048]), ("dbg_mn", [128, NCH]),
                            ("dbg_rstd", [128, NCH]), ("dbg_xc", [128, 2048])]:
                io[nm] = nc.dram_tensor(nm, shp, F32, kind="ExternalOutput").ap()
    io["arin"] = nc.dram_tensor("arin", [12384, 128], F32).ap()
    io["arout"] = nc.dram_tensor("arout", [12384, 128], F32).ap()
    io["fence_in"] = nc.dram_tensor("fence_in", [8, 128], F32).ap()
    io["fence_out"] = nc.dram_tensor("fence_out", [8, 128], F32).ap()

    P = Prog(nc)
    with contextlib.ExitStack() as es:
        emit_l0a(P, es, io)
        P.finalize_and_emit()
    P.close()
    return nc


def kernel(**inputs):
    cm = _chunk_map()
    cores = list(range(NCORES))
    meta = prep_l0f_meta(inputs)
    if FUSED[0]:
        maps = [prep_fused_inputs(inputs, c, meta) for c in cores]
        res = run_bass_kernel_spmd(build_fused(meta), maps, core_ids=cores)
        out = np.zeros((B * 32, 128, D), np.float32)
        for core in cores:
            x4 = np.asarray(res.results[core]["x4"]).reshape(NCH, 128, D)
            for s, (b, j) in enumerate(cm[core]):
                out[b * 32 + j] = x4[s]
        return out.reshape(B, S, D)

    res = run_bass_kernel_spmd(build_l0a(), [prep_l0a_inputs(inputs, c) for c in cores],
                               core_ids=cores)
    x1 = [np.asarray(res.results[c]["x1"]) for c in cores]

    maps = [prep_l0f_inputs(inputs, c, meta) for c in cores]
    for c in cores:
        maps[c]["x1"] = x1[c]
    res = run_bass_kernel_spmd(build_l0f(meta), maps, core_ids=cores)
    x2 = [np.asarray(res.results[c]["x2"]) for c in cores]

    maps = [prep_l1a_inputs(inputs, c) for c in cores]
    for c in cores:
        maps[c]["x2"] = x2[c]
    res = run_bass_kernel_spmd(build_l1a(), maps, core_ids=cores)
    x3 = [np.asarray(res.results[c]["x3"]) for c in cores]

    maps = [prep_l1f_inputs(inputs, c) for c in cores]
    for c in cores:
        maps[c]["x3"] = x3[c]
    res = run_bass_kernel_spmd(build_l1f(), maps, core_ids=cores)

    out = np.zeros((B * 32, 128, D), np.float32)
    for core in cores:
        x4 = np.asarray(res.results[core]["x4"]).reshape(NCH, 128, D)
        for s, (b, j) in enumerate(cm[core]):
            out[b * 32 + j] = x4[s]
    return out.reshape(B, S, D)



NS = 16384


def emit_l0f(P, es, io, meta):
    nc = P.nc
    ident = es.enter_context(nc.sbuf_tensor("ident_sb", [128, 128], BF16))
    b_ident = Buf("ident")
    P.sy.dma_start(out=ident[:], in_=io["ident"][:, :], writes=(b_ident,))
    iota = es.enter_context(nc.sbuf_tensor("iota_sb", [128, 128], F32))
    b_iota = Buf("iota")
    P.sy.dma_start(out=iota[:], in_=io["iota128"][:, :], writes=(b_iota,))

    xinT = es.enter_context(nc.sbuf_tensor("xinT", [128, NDCH, 2048], BF16))
    b_xinT = Buf("xinT")
    with contextlib.ExitStack() as es1:
        emit_preln_xinT(P, es1, io["x1"], xinT, b_xinT, ident, b_ident, "l0f_pre")
    P.barrier()

    xg = io["xg_dram"]
    b_xg = Buf("xg")
    for dd in range(NDCH):
        P.sy.dma_start(out=xg[dd * 128:(dd + 1) * 128, :], in_=xinT[:, dd, :],
                       reads=(b_xinT,), writes=(b_xg,) if dd == NDCH - 1 else (),
                       after=(b_xg.writer,))

    ffn_out = es.enter_context(nc.sbuf_tensor("ffn_out", [128, NCH, 2048], BF16))
    b_fo = Buf("ffn_out")

    with contextlib.ExitStack() as es1:
        gw = es1.enter_context(nc.sbuf_tensor("gw_sb", [128, NDCH, 8], BF16))
        b_gw = Buf("gw")
        P.g.dma_start(out=gw[:], in_=io["l0f_gwT"].rearrange("(dd p) n -> p dd n", p=128),
                      writes=(b_gw,))
        PT = es1.enter_context(nc.sbuf_tensor("PT_sb", [8, 2048], BF16))
        b_PT = Buf("PT")
        lg = es1.enter_context(nc.sbuf_tensor("lg", [128, 8], F32))
        t1 = es1.enter_context(nc.sbuf_tensor("rt1", [128, 8], F32))
        t2 = es1.enter_context(nc.sbuf_tensor("rt2", [128, 8], F32))
        m1 = es1.enter_context(nc.sbuf_tensor("rm1", [128, 1], F32))
        m2 = es1.enter_context(nc.sbuf_tensor("rm2", [128, 1], F32))
        w1 = es1.enter_context(nc.sbuf_tensor("rw1", [128, 1], F32))
        w2 = es1.enter_context(nc.sbuf_tensor("rw2", [128, 1], F32))
        pmix = es1.enter_context(nc.sbuf_tensor("pmix", [128, 8], BF16))
        b_lg, b_t1, b_t2, b_m1, b_m2, b_w1, b_w2, b_px = (
            Buf("lg"), Buf("t1"), Buf("t2"), Buf("m1"), Buf("m2"), Buf("w1"),
            Buf("w2"), Buf("px"))
        pk = [es1.enter_context(nc.psum_tensor(f"fk{i}", [128, 512], F32)) for i in range(2)]
        b_pk = [Buf(f"fk{i}") for i in range(2)]
        ptr = [es1.enter_context(nc.psum_tensor(f"ft{i}", [128, 512], BF16)) for i in range(2)]
        b_ptr = [Buf(f"ft{i}") for i in range(2)]

        for c in range(NCH):
            i = c % 2
            mm_acc(P, pk[i][:, 0:8],
                   [(xinT[:, dd, c * 128:(c + 1) * 128], gw[:, dd, :]) for dd in range(NDCH)],
                   reads=(b_xinT, b_gw), writes=(b_pk[i],))
            P.v.tensor_scalar(lg[:], pk[i][:, 0:8], 50.0, -50.0, ALU.min, ALU.max,
                              reads=(b_pk[i],), writes=(b_lg,))
            P.v.reduce_max(m1[:], lg[:], axis=AX.X, reads=(b_lg,), writes=(b_m1,))
            P.v.tensor_scalar(t1[:], lg[:], m1[:], None, ALU.is_equal,
                              reads=(b_lg, b_m1), writes=(b_t1,))
            P.v.scalar_tensor_tensor(t2[:], t1[:], -1e30, lg[:], ALU.mult, ALU.add,
                                     reads=(b_t1, b_lg), writes=(b_t2,))
            P.v.reduce_max(m2[:], t2[:], axis=AX.X, reads=(b_t2,), writes=(b_m2,))
            P.v.tensor_sub(w1[:], m1[:], m2[:], reads=(b_m1, b_m2), writes=(b_w1,))
            P.a.activation(w1[:], w1[:], AF.Sigmoid, writes=(b_w1,))
            P.v.tensor_scalar(w2[:], w1[:], -1.0, 1.0, ALU.mult, ALU.add,
                              reads=(b_w1,), writes=(b_w2,))
            P.v.tensor_scalar(t2[:], t2[:], m2[:], None, ALU.is_equal,
                              reads=(b_m2,), writes=(b_t2,))
            P.v.tensor_scalar(t1[:], t1[:], w1[:], None, ALU.mult,
                              reads=(b_w1,), writes=(b_t1,))
            P.v.tensor_scalar(t2[:], t2[:], w2[:], None, ALU.mult,
                              reads=(b_w2,), writes=(b_t2,))
            P.v.tensor_add(pmix[:], t1[:], t2[:], reads=(b_t1, b_t2), writes=(b_px,))
            j = c % 2
            tr_group(P, [(ptr[j][0:8, 0:128], pmix[:])], ident[:],
                     reads=(b_px, b_ident), writes=(b_ptr[j],))
            P.a.copy(PT[:, c * 128:(c + 1) * 128], ptr[j][0:8, 0:128],
                     reads=(b_ptr[j],), writes=(b_PT,))

        weighted = es1.enter_context(nc.sbuf_tensor("weighted", [8, NS], BF16))
        b_wt = Buf("weighted")
        eaw = es1.enter_context(nc.sbuf_tensor("eaw", [8, 64], F32))
        b_eaw = Buf("eaw")
        smT = es1.enter_context(nc.sbuf_tensor("smT", [64, 8], BF16))
        b_smT = Buf("smT")
        emx = es1.enter_context(nc.sbuf_tensor("emx", [8, 1], F32))
        esm = es1.enter_context(nc.sbuf_tensor("esm", [8, 1], F32))
        b_emx, b_esm = Buf("emx"), Buf("esm")
        atoms_sl = [es1.enter_context(nc.sbuf_tensor(f"atoms{i}", [64, 512], BF16))
                    for i in range(2)]
        b_at = [Buf(f"at{i}") for i in range(2)]
        imp_sl = [es1.enter_context(nc.sbuf_tensor(f"imp{i}", [8, 512], F32))
                  for i in range(2)]
        b_imp = [Buf(f"imp{i}") for i in range(2)]

        P.sy.dma_start(out=eaw[:], in_=io["l0f_eaw"][:, :], writes=(b_eaw,))
        P.v.reduce_max(emx[:], eaw[:], axis=AX.X, negate=True, reads=(b_eaw,), writes=(b_emx,))
        P.a.activation(eaw[:], eaw[:], AF.Exp, bias=emx[:],
                       accum_out=esm[:], reads=(b_emx,), writes=(b_eaw, b_esm))
        P.v.reciprocal(esm[:], esm[:], writes=(b_esm,))
        P.v.tensor_scalar(eaw[:], eaw[:], esm[:], None, ALU.mult,
                          reads=(b_esm,), writes=(b_eaw,))
        eawb = es1.enter_context(nc.sbuf_tensor("eawb", [8, 64], BF16))
        b_eawb = Buf("eawb")
        P.v.tensor_copy(eawb[:], eaw[:], reads=(b_eaw,), writes=(b_eawb,))
        tr_group(P, [(ptr[0][0:64, 0:8], eawb[:])], ident[0:8, 0:8],
                 reads=(b_eawb, b_ident), writes=(b_ptr[0],))
        P.a.copy(smT[:], ptr[0][0:64, 0:8], reads=(b_ptr[0],), writes=(b_smT,))
        for sl in range(NS // 512):
            i = sl % 2
            P.g.dma_start(out=atoms_sl[i][:], in_=io["l0f_atoms"][:, sl * 512:(sl + 1) * 512],
                          writes=(b_at[i],))
            P.sy.dma_start(out=imp_sl[i][:], in_=io["l0f_imp"][:, sl * 512:(sl + 1) * 512],
                           writes=(b_imp[i],))
            mm_acc(P, pk[i][0:8, :],
                   [(smT[:], atoms_sl[i][:])],
                   reads=(b_smT, b_at[i]), writes=(b_pk[i],))
            P.a.activation(imp_sl[i][:], imp_sl[i][:], AF.Sigmoid, writes=(b_imp[i],))
            P.v.tensor_mul(weighted[:, sl * 512:(sl + 1) * 512], pk[i][0:8, :], imp_sl[i][:],
                           reads=(b_pk[i], b_imp[i]), writes=(b_wt,))

        sidxT = es1.enter_context(nc.sbuf_tensor("sidxT", [128, 128], I32))
        rows_f = es1.enter_context(nc.sbuf_tensor("rows_f", [128, 128], F32))
        cols_i = es1.enter_context(nc.sbuf_tensor("cols_i", [128, 128], I32))
        scri = es1.enter_context(nc.sbuf_tensor("scri", [128, 128], I32))
        b_sidx, b_rows, b_cols = Buf("sidx"), Buf("rows"), Buf("cols")
        P.sy.dma_start(out=sidxT[:], in_=io["l0f_sidxT"][:, :], writes=(b_sidx,))
        P.v.tensor_scalar(scri[:], sidxT[:], 11, None, ALU.arith_shift_right,
                          reads=(b_sidx,), writes=(b_rows,))
        P.v.tensor_copy(rows_f[:], scri[:], writes=(b_rows,))
        P.v.tensor_scalar(cols_i[:], sidxT[:], 2047, None, ALU.bitwise_and,
                          reads=(b_sidx,), writes=(b_cols,))

        w0 = [es1.enter_context(nc.sbuf_tensor(f"w0_{i}", [128, NDCH, 128], BF16))
              for i in range(2)]
        b_w0 = [Buf(f"w0_{i}") for i in range(2)]
        g_sb = [es1.enter_context(nc.sbuf_tensor(f"g_sb{i}", [128, 2048], BF16))
                for i in range(2)]
        b_g = [Buf(f"g{i}") for i in range(2)]
        z_sb = [es1.enter_context(nc.sbuf_tensor(f"z_sb{i}", [128, 2048], BF16))
                for i in range(2)]
        b_z = [Buf(f"z{i}") for i in range(2)]
        oT = [es1.enter_context(nc.sbuf_tensor(f"oT{i}", [128, 2048], BF16))
              for i in range(2)]
        b_oT = [Buf(f"oT{i}") for i in range(2)]
        rmb = es1.enter_context(nc.sbuf_tensor("rmb", [128, 1], F32))
        Rt = es1.enter_context(nc.sbuf_tensor("Rt", [128, 128], BF16))
        b_rmb, b_R = Buf("rmb"), Buf("R")
        pdd = es1.enter_context(nc.psum_tensor("pdd", [128, 2048], F32))
        b_pdd = Buf("pdd")

        for hc in range(NCH):
            iw = hc % 2
            P.g.dma_start(out=w0[iw][:],
                          in_=io["l0f_w0T"][:, hc * 128:(hc + 1) * 128]
                          .rearrange("(dd p) n -> p dd n", p=128), writes=(b_w0[iw],))
            sch = meta["hc_chunks"][hc]
            dense_pairs = []
            for tg in range(4):
                for dd in range(NDCH):
                    dense_pairs.append(
                        (w0[iw][:, dd, :], xinT[:, dd, tg * 512:(tg + 1) * 512],
                         pdd[:, tg * 512:(tg + 1) * 512], dd == 0, False))
            mm_group(P, dense_pairs, reads=(b_w0[iw], b_xinT), writes=(b_pdd,))
            for si, sc in enumerate(sch):
                ig = si % 2
                P.g.indirect_dma_start(
                    g_sb[ig][:], None, xg[:, :],
                    IndirectOffsetOnAxis(cols_i[:, sc:sc + 1], 0),
                    reads=(b_cols, b_xg), writes=(b_g[ig],))
                for tg in range(4):
                    ip = tg % 2
                    mm_acc(P, pk[ip][:],
                           [(weighted[:, sc * 128:(sc + 1) * 128],
                             PT[:, tg * 512:(tg + 1) * 512])],
                           reads=(b_wt, b_PT), writes=(b_pk[ip],))
                    P.v.tensor_mul(z_sb[ig][:, tg * 512:(tg + 1) * 512], pk[ip][:],
                                   g_sb[ig][:, tg * 512:(tg + 1) * 512],
                                   reads=(b_pk[ip], b_g[ig]), writes=(b_z[ig],))
                P.v.tensor_scalar(rmb[:], rows_f[:, sc:sc + 1], float(hc * 128), None,
                                  ALU.subtract, reads=(b_rows,), writes=(b_rmb,))
                P.v.tensor_scalar(Rt[:], iota[:], rmb[:], None, ALU.is_equal,
                                  reads=(b_iota, b_rmb), writes=(b_R,))
                last = si == len(sch) - 1
                mm_group(P, [(Rt[:], z_sb[ig][:, tg * 512:(tg + 1) * 512],
                              pdd[:, tg * 512:(tg + 1) * 512], False, last)
                             for tg in range(4)],
                         reads=(b_R, b_z[ig]), writes=(b_pdd,))
            if not sch:
                mm_group(P, [(w0[iw][:, 0, :], xinT[:, 0, 0:1], pdd[:, tg * 512:tg * 512 + 1],
                              False, True) for tg in range(4)],
                         reads=(b_w0[iw], b_xinT), writes=(b_pdd,))
            io_ = hc % 2
            P.a.copy(oT[io_][:], pdd[:], reads=(b_pdd,), writes=(b_oT[io_],))
            for cg in range(4):
                j = cg % 2
                tr_group(P, [(ptr[j][:, k * 128:(k + 1) * 128],
                              oT[io_][:, (cg * 4 + k) * 128:(cg * 4 + k + 1) * 128])
                             for k in range(4)], ident[:],
                         reads=(b_oT[io_], b_ident), writes=(b_ptr[j],))
                P.a.copy(ffn_out[:, cg * 4:(cg + 1) * 4, hc * 128:(hc + 1) * 128],
                         ptr[j][:].rearrange("p (k f) -> p k f", k=4),
                         reads=(b_ptr[j],), writes=(b_fo,))

    P.barrier()
    with contextlib.ExitStack() as es1:
        stores = emit_postln_residual(P, es1, lambda c: ffn_out[:, c, :], b_fo,
                                      io["x1"], io["x2"], "l0f_post")
        P.fence(stores)


def prep_l0f_meta(full):
    mask_idx = np.asarray(full["L0_ffn_mask_idx"])
    perm = np.argsort(mask_idx, kind="stable")
    sidx = mask_idx[perm]
    rows = sidx >> 11
    hc_chunks = []
    for hc in range(NCH):
        lo = np.searchsorted(rows, hc * 128, side="left")
        hi = np.searchsorted(rows, (hc + 1) * 128, side="left")
        if hi > lo:
            hc_chunks.append(list(range(lo // 128, (hi + 127) // 128)))
        else:
            hc_chunks.append([])
    return {"perm": perm, "sidx": sidx, "hc_chunks": hc_chunks}


def prep_l0f_inputs(full, core, meta, x1=None):
    perm = meta["perm"]
    d = {
        "l0f_w0T": _f32(np.asarray(full["L0_ffn_W"]).T),
        "l0f_gwT": _f32(np.asarray(full["L0_ffn_gate_w"]).T),
        "l0f_atoms": _f32(np.asarray(full["L0_ffn_atoms"])[:, perm]),
        "l0f_eaw": _f32(full["L0_ffn_expert_atom_weights"]),
        "l0f_imp": _f32(np.asarray(full["L0_ffn_importance"])[:, perm]),
        "l0f_sidxT": np.ascontiguousarray(
            meta["sidx"].reshape(128, 128).T.astype(np.int32)),
        "iota128": np.ascontiguousarray(
            np.broadcast_to(np.arange(128, dtype=np.float32), (128, 128))),
        "ident": np.eye(128, dtype=ml_dtypes.bfloat16),
    }
    if x1 is not None:
        cm = _chunk_map()[core]
        xr = x1.reshape(B * 32, 128, D)
        d["x1"] = _f32(np.concatenate([xr[b * 32 + j] for (b, j) in cm], axis=0))
    return d


def build_l0f(meta):
    nc = bass.Bass("TRN2", target_bir_lowering=False)
    io = {}
    io["x1"] = nc.dram_tensor("x1", [TLOC, D], F32, kind="ExternalInput").ap()
    io["x2"] = nc.dram_tensor("x2", [TLOC, D], F32, kind="ExternalOutput").ap()
    for nm, shp, dt in [("l0f_w0T", [D, D], F32), ("l0f_gwT", [D, 8], F32),
                        ("l0f_atoms", [64, NS], F32), ("l0f_eaw", [8, 64], F32),
                        ("l0f_imp", [8, NS], F32), ("l0f_sidxT", [128, 128], I32),
                        ("iota128", [128, 128], F32)]:
        io[nm] = nc.dram_tensor(nm, shp, dt, kind="ExternalInput").ap()
    io["ident"] = nc.dram_tensor("ident", [128, 128], BF16, kind="ExternalInput").ap()
    io["xg_dram"] = nc.dram_tensor("xg_dram", [D, TLOC], BF16).ap()
    P = Prog(nc)
    with contextlib.ExitStack() as es:
        emit_l0f(P, es, io, meta)
        P.finalize_and_emit()
    P.close()
    return nc



def emit_postln_residual_dram(P, es, src_dram, x_dram, xout_dram, tag):
    nc = P.nc
    ln = LNStats(P, es, tag)
    sc = [es.enter_context(nc.sbuf_tensor(f"{tag}_sc{i}", [128, 2048], BF16)) for i in range(2)]
    b_sc = [Buf(f"{tag}_sc{i}") for i in range(2)]
    xc = [es.enter_context(nc.sbuf_tensor(f"{tag}_xc{i}", [128, 2048], F32)) for i in range(2)]
    b_xc = [Buf(f"{tag}_xc{i}") for i in range(2)]
    lo = [es.enter_context(nc.sbuf_tensor(f"{tag}_lo{i}", [128, 2048], F32)) for i in range(2)]
    b_lo = [Buf(f"{tag}_lo{i}") for i in range(2)]
    for c in range(NCH):
        i = c % 2
        P.sy.dma_start(out=sc[i][:], in_=src_dram[c * 128:(c + 1) * 128, :], writes=(b_sc[i],))
        ln.stat_chunk(P, c, sc[i][:], reads=(b_sc[i],))
    ln.finish(P, D)
    stores = []
    for c in range(NCH):
        i = c % 2
        P.sy.dma_start(out=sc[i][:], in_=src_dram[c * 128:(c + 1) * 128, :], writes=(b_sc[i],))
        P.sy.dma_start(out=xc[i][:], in_=x_dram[c * 128:(c + 1) * 128, :], writes=(b_xc[i],))
        ln.apply(P, c, lo[i][:], sc[i][:], reads=(b_sc[i],), writes=(b_lo[i],))
        P.v.tensor_add(lo[i][:], lo[i][:], xc[i][:], reads=(b_xc[i],), writes=(b_lo[i],))
        stores.append(P.sy.dma_start(out=xout_dram[c * 128:(c + 1) * 128, :], in_=lo[i][:],
                                     reads=(b_lo[i],)))
    return stores


def emit_l1a(P, es, io):
    nc = P.nc
    ident = es.enter_context(nc.sbuf_tensor("ident_sb", [128, 128], BF16))
    b_ident = Buf("ident")
    P.sy.dma_start(out=ident[:], in_=io["ident"][:, :], writes=(b_ident,))
    agin, agout, qTd = io["agin"], io["agout"], io["qT_dram"]
    b_agin, b_agout, b_qTd = Buf("agin"), Buf("agout"), Buf("qTd")

    with contextlib.ExitStack() as es1:
        xinT = es1.enter_context(nc.sbuf_tensor("xinT", [128, NDCH, 2048], BF16))
        b_xinT = Buf("xinT")
        with contextlib.ExitStack() as es2:
            emit_preln_xinT(P, es2, io["x2"], xinT, b_xinT, ident, b_ident, "l1a_pre")
        P.barrier()

        permT = es1.enter_context(nc.sbuf_tensor("permT", [128, 128], BF16))
        cosT = es1.enter_context(nc.sbuf_tensor("cosT", [128, 2048], BF16))
        sinT = es1.enter_context(nc.sbuf_tensor("sinT", [128, 2048], BF16))
        b_perm, b_cos, b_sin = Buf("perm"), Buf("cos"), Buf("sin")
        P.sy.dma_start(out=permT[:], in_=io["ropePermT"][:, :], writes=(b_perm,))
        P.sy.dma_start(out=cosT[:], in_=io["ropeCos"][:, :], writes=(b_cos,))
        P.sy.dma_start(out=sinT[:], in_=io["ropeSin"][:, :], writes=(b_sin,))

        wsl = es1.enter_context(nc.sbuf_tensor("wsl", [128, NDCH, 128], BF16))
        b_wsl = Buf("wsl")
        qraw = es1.enter_context(nc.sbuf_tensor("qraw", [128, 512], BF16))
        b_qraw = Buf("qraw")
        rt1 = es1.enter_context(nc.sbuf_tensor("rt1", [128, 512], F32))
        rt2 = es1.enter_context(nc.sbuf_tensor("rt2", [128, 512], F32))
        rt3 = es1.enter_context(nc.sbuf_tensor("rt3", [128, 512], BF16))
        b_rt1, b_rt2, b_rt3 = Buf("rt1"), Buf("rt2"), Buf("rt3")
        pk = [es1.enter_context(nc.psum_tensor(f"ak{i}", [128, 512], F32)) for i in range(2)]
        b_pk = [Buf(f"ak{i}") for i in range(2)]
        pr = es1.enter_context(nc.psum_tensor("ar", [128, 512], F32))
        b_pr = Buf("ar")

        for (wname, dst, b_dst, row0) in [("l1_wkT", agin, b_agin, 0)]:
            for h in range(16):
                P.g.dma_start(out=wsl[:], in_=io[wname][:, h * 128:(h + 1) * 128]
                              .rearrange("(dd p) n -> p dd n", p=128), writes=(b_wsl,))
                for tg in range(4):
                    i = tg % 2
                    mm_acc(P, pk[i][:],
                           [(wsl[:, dd, :], xinT[:, dd, tg * 512:(tg + 1) * 512])
                            for dd in range(NDCH)],
                           reads=(b_wsl, b_xinT), writes=(b_pk[i],))
                    P.a.copy(qraw[:], pk[i][:], reads=(b_pk[i],), writes=(b_qraw,))
                    mm_acc(P, pr[:], [(permT[:], qraw[:])],
                           reads=(b_perm, b_qraw), writes=(b_pr,))
                    P.v.tensor_mul(rt1[:], qraw[:], cosT[:, tg * 512:(tg + 1) * 512],
                                   reads=(b_qraw, b_cos), writes=(b_rt1,))
                    P.v.tensor_mul(rt2[:], pr[:], sinT[:, tg * 512:(tg + 1) * 512],
                                   reads=(b_pr, b_sin), writes=(b_rt2,))
                    P.v.tensor_add(rt3[:], rt1[:], rt2[:],
                                   reads=(b_rt1, b_rt2), writes=(b_rt3,))
                    P.sy.dma_start(out=dst[row0 + h * 128:row0 + (h + 1) * 128,
                                           tg * 512:(tg + 1) * 512],
                                   in_=rt3[:], reads=(b_rt3,), writes=(b_dst,))
        wv = es1.enter_context(nc.sbuf_tensor("wv", [128, NDCH, 512], BF16))
        b_kick = Buf("kick")
        b_wv = Buf("wv")
        vev = es1.enter_context(nc.sbuf_tensor("vev", [128, 512], BF16))
        b_vev = Buf("vev")
        for og in range(4):
            P.g.dma_start(out=wv[:], in_=io["l1_wvT"][:, og * 512:(og + 1) * 512]
                          .rearrange("(dd p) n -> p dd n", p=128), writes=(b_wv,))
            for c in range(NCH):
                i = c % 2
                mm_acc(P, pk[i][:],
                       [(xinT[:, dd, c * 128:(c + 1) * 128], wv[:, dd, :])
                        for dd in range(NDCH)],
                       reads=(b_xinT, b_wv), writes=(b_pk[i],))
                P.a.copy(vev[:], pk[i][:], reads=(b_pk[i],), writes=(b_vev,))
                for hh in range(4):
                    h = og * 4 + hh
                    P.sy.dma_start(
                        out=agin[2048 + h * 128:2048 + (h + 1) * 128,
                                 c * 128:(c + 1) * 128],
                        in_=vev[:, hh * 128:(hh + 1) * 128],
                        reads=(b_vev,), writes=(b_agin,))
        groups = [[0, 1], [2, 3], [4, 5], [6, 7]]
        P.g.collective_compute("AllGather", ALU.bypass, replica_groups=groups,
                               ins=[agin.tensor.ap().opt()],
                               outs=[agout.tensor.ap().opt()],
                               reads=(b_agin,), writes=(b_agout,))
        P.g.collective_compute("AllReduce", ALU.add, replica_groups=groups,
                               ins=[io["f2in"].tensor.ap().opt()],
                               outs=[io["f2out"].tensor.ap().opt()],
                               writes=(b_agout,))
        for (wname, dst, b_dst, row0) in [("l1_wqT", qTd, b_qTd, 0)]:
            for h in range(16):
                P.g.dma_start(out=wsl[:], in_=io[wname][:, h * 128:(h + 1) * 128]
                              .rearrange("(dd p) n -> p dd n", p=128), writes=(b_wsl,))
                for tg in range(4):
                    i = tg % 2
                    mm_acc(P, pk[i][:],
                           [(wsl[:, dd, :], xinT[:, dd, tg * 512:(tg + 1) * 512])
                            for dd in range(NDCH)],
                           reads=(b_wsl, b_xinT), writes=(b_pk[i],))
                    P.a.copy(qraw[:], pk[i][:], reads=(b_pk[i],), writes=(b_qraw,))
                    mm_acc(P, pr[:], [(permT[:], qraw[:])],
                           reads=(b_perm, b_qraw), writes=(b_pr,))
                    P.v.tensor_mul(rt1[:], qraw[:], cosT[:, tg * 512:(tg + 1) * 512],
                                   reads=(b_qraw, b_cos), writes=(b_rt1,))
                    P.v.tensor_mul(rt2[:], pr[:], sinT[:, tg * 512:(tg + 1) * 512],
                                   reads=(b_pr, b_sin), writes=(b_rt2,))
                    P.v.tensor_add(rt3[:], rt1[:], rt2[:],
                                   reads=(b_rt1, b_rt2), writes=(b_rt3,))
                    P.sy.dma_start(out=dst[row0 + h * 128:row0 + (h + 1) * 128,
                                           tg * 512:(tg + 1) * 512],
                                   in_=rt3[:], reads=(b_rt3,), writes=(b_dst,))
    P.barrier()

    attO = es.enter_context(nc.sbuf_tensor("attO", [128, NCH, 2048], BF16))
    b_attO = Buf("attO")

    with contextlib.ExitStack() as es1:
        maskT = es1.enter_context(nc.sbuf_tensor("maskT_sb", [128, 2, 4, 512], BF16))
        b_mask = Buf("mask")
        P.sy.dma_start(out=maskT[:, 0], in_=io["maskA"].rearrange("(m p) n -> p m n", p=128),
                       writes=(b_mask,))
        P.sy.dma_start(out=maskT[:, 1], in_=io["maskB"].rearrange("(m p) n -> p m n", p=128),
                       writes=(b_mask,))
        ones128 = es1.enter_context(nc.sbuf_tensor("ones128_sb", [128, 128], BF16))
        b_ones = Buf("ones128")
        P.sy.dma_start(out=ones128[:], in_=io["ones128"][:, :], writes=(b_ones,))
        kT_h = es1.enter_context(nc.sbuf_tensor("kT_h", [128, 2, 2048], BF16))
        v_h = es1.enter_context(nc.sbuf_tensor("v_h", [128, 2, 2048], BF16))
        qT_h = es1.enter_context(nc.sbuf_tensor("qT_h", [128, 2048], BF16))
        b_kTh, b_vh, b_qTh = Buf("kTh"), Buf("vh"), Buf("qTh")
        probs = [es1.enter_context(nc.sbuf_tensor(f"probs{i}", [128, 512], BF16))
                 for i in range(4)]
        b_probs = [Buf(f"probs{i}") for i in range(4)]
        rden = es1.enter_context(nc.sbuf_tensor("rden", [128, 512], F32))
        b_rden = Buf("rden")
        ps = [es1.enter_context(nc.psum_tensor(f"ps{i}", [128, 512], F32)) for i in range(4)]
        b_ps = [Buf(f"ps{i}") for i in range(4)]
        pv = es1.enter_context(nc.psum_tensor("pv", [128, 512], F32))
        b_pv = Buf("pv")
        pden = es1.enter_context(nc.psum_tensor("pden", [128, 512], F32))
        b_pden = Buf("pden")
        SC = 1.0 / math.sqrt(128.0)

        for h in range(16):
            P.sy.dma_start(out=qT_h[:], in_=qTd[h * 128:(h + 1) * 128, :],
                           reads=(b_qTd,), writes=(b_qTh,))
            for p_ in range(2):
                P.sy.dma_start(out=kT_h[:, p_, :],
                               in_=agout[p_ * 4096 + h * 128:p_ * 4096 + (h + 1) * 128, :],
                               reads=(b_agout,), writes=(b_kTh,))
                P.sy.dma_start(out=v_h[:, p_, :],
                               in_=agout[p_ * 4096 + 2048 + h * 128:
                                         p_ * 4096 + 2048 + (h + 1) * 128, :],
                               reads=(b_agout,), writes=(b_vh,))
            for tg in range(4):
                nu = 4 * tg + 4
                steps = [(p_, u) for p_ in range(2) for u in range(nu)]
                n = len(steps)

                def emit_scores(idx):
                    p_, u = steps[idx]
                    i = idx % 4
                    mm_acc(P, ps[i][:],
                           [(kT_h[:, p_, u * 128:(u + 1) * 128],
                             qT_h[:, tg * 512:(tg + 1) * 512])],
                           reads=(b_kTh, b_qTh), writes=(b_ps[i],))

                emit_scores(0)
                emit_scores(1)
                for idx in range(n):
                    p_, u = steps[idx]
                    i = idx % 4
                    if idx + 2 < n:
                        emit_scores(idx + 2)
                    P.a.activation(probs[i][:], ps[i][:], AF.Exp, scale=SC,
                                   reads=(b_ps[i],), writes=(b_probs[i],))
                    du = u - 4 * tg
                    if du >= 0:
                        P.v.tensor_mul(probs[i][:], probs[i][:], maskT[:, p_, du, :],
                                       reads=(b_mask,), writes=(b_probs[i],))
                    mm_group(P, [(ones128[:], probs[i][:], pden[:], idx == 0, idx == n - 1),
                                 (v_h[:, p_, u * 128:(u + 1) * 128], probs[i][:],
                                  pv[:], idx == 0, idx == n - 1)],
                             reads=(b_ones, b_probs[i], b_vh),
                             writes=(b_pden, b_pv))
                P.v.reciprocal(rden[:], pden[:], reads=(b_pden,), writes=(b_rden,))
                P.v.tensor_mul(attO[:, h, tg * 512:(tg + 1) * 512], pv[:], rden[:],
                               reads=(b_pv, b_rden), writes=(b_attO,))
    P.barrier()

    with contextlib.ExitStack() as es1:
        attn_tm = es1.enter_context(nc.sbuf_tensor("attn_tm", [128, NCH, 2048], BF16))
        b_atm = Buf("attn_tm")
        wo = es1.enter_context(nc.sbuf_tensor("wo", [128, NDCH, 512], BF16))
        b_wo = Buf("wo")
        pk = [es1.enter_context(nc.psum_tensor(f"ok{i}", [128, 512], F32)) for i in range(2)]
        b_pk = [Buf(f"ok{i}") for i in range(2)]
        for og in range(4):
            P.g.dma_start(out=wo[:], in_=io["l1_woT"][:, og * 512:(og + 1) * 512]
                          .rearrange("(dd p) n -> p dd n", p=128), writes=(b_wo,))
            for c in range(NCH):
                i = c % 2
                mm_acc(P, pk[i][:],
                       [(attO[:, hh, c * 128:(c + 1) * 128], wo[:, hh, :])
                        for hh in range(16)],
                       reads=(b_attO, b_wo), writes=(b_pk[i],))
                P.a.copy(attn_tm[:, c, og * 512:(og + 1) * 512], pk[i][:],
                         reads=(b_pk[i],), writes=(b_atm,))
        P.barrier()
        with contextlib.ExitStack() as es2:
            stores = emit_postln_residual(P, es2, lambda c: attn_tm[:, c, :], b_atm,
                                          io["x2"], io["x3"], "l1a_post")
            P.fence(stores)


def prep_l1a_inputs(full, core, x2=None):
    p = core % 2
    cm = _chunk_map()[core]
    d = {
        "l1_wqT": _f32(np.asarray(full["L1_attn_Wq"]).T),
        "l1_wkT": _f32(np.asarray(full["L1_attn_Wk"]).T),
        "l1_wvT": _f32(np.asarray(full["L1_attn_Wv"]).T),
        "l1_woT": _f32(np.asarray(full["L1_attn_Wo"]).T),
        "ident": np.eye(128, dtype=ml_dtypes.bfloat16),
        "ones128": np.ones((128, 128), dtype=ml_dtypes.bfloat16),
    }
    pos = np.concatenate([np.arange(128) + j * 128 for (_, j) in cm]).astype(np.float64)
    half = 64
    inv = 1.0 / (10000.0 ** (np.arange(half, dtype=np.float64) / half))
    fr = pos[None, :] * inv[:, None]
    cosT = np.repeat(np.cos(fr), 2, axis=0)
    sinT = np.repeat(np.sin(fr), 2, axis=0)
    d["ropeCos"] = cosT.astype(ml_dtypes.bfloat16)
    d["ropeSin"] = sinT.astype(ml_dtypes.bfloat16)
    M = np.zeros((128, 128), np.float32)
    for i2 in range(64):
        M[2 * i2, 2 * i2 + 1] = -1.0
        M[2 * i2 + 1, 2 * i2] = 1.0
    d["ropePermT"] = np.ascontiguousarray(M.T).astype(ml_dtypes.bfloat16)
    tri = (np.arange(128)[:, None] <= np.arange(128)[None, :]).astype(np.float32)
    ones = np.ones((128, 128), np.float32)
    zeros = np.zeros((128, 128), np.float32)
    mA = np.zeros((4, 128, 512), np.float32)
    mB = np.zeros((4, 128, 512), np.float32)
    for du in range(4):
        for ds in range(4):
            blk = slice(ds * 128, (ds + 1) * 128)
            if du < ds:
                mA[du][:, blk] = 1.0
                mB[du][:, blk] = 1.0
            elif du == ds:
                mA[du][:, blk] = tri if p == 0 else ones
                mB[du][:, blk] = zeros if p == 0 else tri
    d["maskA"] = mA.reshape(512, 512).astype(ml_dtypes.bfloat16)
    d["maskB"] = mB.reshape(512, 512).astype(ml_dtypes.bfloat16)
    if x2 is not None:
        xr = x2.reshape(B * 32, 128, D)
        d["x2"] = _f32(np.concatenate([xr[b * 32 + j] for (b, j) in cm], axis=0))
    return d


def build_l1a():
    nc = bass.Bass("TRN2", target_bir_lowering=False)
    io = {}
    io["x2"] = nc.dram_tensor("x2", [TLOC, D], F32, kind="ExternalInput").ap()
    io["x3"] = nc.dram_tensor("x3", [TLOC, D], F32, kind="ExternalOutput").ap()
    for nm in ["l1_wqT", "l1_wkT", "l1_wvT", "l1_woT"]:
        io[nm] = nc.dram_tensor(nm, [D, D], F32, kind="ExternalInput").ap()
    for nm, shp in [("ropeCos", [128, 2048]), ("ropeSin", [128, 2048]),
                    ("ropePermT", [128, 128]), ("ones128", [128, 128]),
                    ("maskA", [512, 512]), ("maskB", [512, 512]),
                    ("ident", [128, 128])]:
        io[nm] = nc.dram_tensor(nm, shp, BF16, kind="ExternalInput").ap()
    io["agin"] = nc.dram_tensor("agin", [4096, 2048], BF16).ap()
    io["agout"] = nc.dram_tensor("agout", [8192, 2048], BF16).ap()
    io["qT_dram"] = nc.dram_tensor("qT_dram", [2048, 2048], BF16).ap()
    io["f2in"] = nc.dram_tensor("f2in", [8, 128], F32).ap()
    io["f2out"] = nc.dram_tensor("f2out", [8, 128], F32).ap()
    P = Prog(nc)
    with contextlib.ExitStack() as es:
        emit_l1a(P, es, io)
        P.finalize_and_emit()
    P.close()
    return nc



G_CENTER = 1.0 / math.e
G_LOWER = 0.5 - math.log(4.0 / 3.0)
G_UPPER = 0.5


def emit_l1f(P, es, io):
    nc = P.nc
    ident = es.enter_context(nc.sbuf_tensor("ident_sb", [128, 128], BF16))
    b_ident = Buf("ident")
    P.sy.dma_start(out=ident[:], in_=io["ident"][:, :], writes=(b_ident,))
    e8bc = es.enter_context(nc.sbuf_tensor("e8bc_sb", [8, 1024], BF16))
    b_e8 = Buf("e8bc")
    P.sy.dma_start(out=e8bc[:], in_=io["e8bc"][:, :], writes=(b_e8,))

    xinT = es.enter_context(nc.sbuf_tensor("xinT", [128, NDCH, 2048], BF16))
    b_xinT = Buf("xinT")
    with contextlib.ExitStack() as es1:
        emit_preln_xinT(P, es1, io["x3"], xinT, b_xinT, ident, b_ident, "l1f_pre")
    P.barrier()

    wT = es.enter_context(nc.sbuf_tensor("wT_sb", [8, 2048], BF16))
    b_wT = Buf("wT")
    with contextlib.ExitStack() as es1:
        wr = es1.enter_context(nc.sbuf_tensor("wr_sb", [128, NDCH, 8], BF16))
        b_wr = Buf("wr")
        P.g.dma_start(out=wr[:], in_=io["l1f_wr"].rearrange("(dd p) n -> p dd n", p=128),
                      writes=(b_wr,))
        tmpB = es1.enter_context(nc.sbuf_tensor("tempB", [128, 1], F32))
        b_tmp = Buf("tempB")
        P.sy.dma_start(out=tmpB[:], in_=io["l1f_tempB"][:, :], writes=(b_tmp,))
        rtemp = es1.enter_context(nc.sbuf_tensor("rtemp", [128, 1], F32))
        b_rtemp = Buf("rtemp")
        P.v.reciprocal(rtemp[:], tmpB[:], reads=(b_tmp,), writes=(b_rtemp,))
        inh = es1.enter_context(nc.sbuf_tensor("inh", [128, 8], F32))
        zon = es1.enter_context(nc.sbuf_tensor("zon", [128, 8], F32))
        dist = es1.enter_context(nc.sbuf_tensor("dist", [128, 8], F32))
        wv_ = es1.enter_context(nc.sbuf_tensor("wv_", [128, 8], F32))
        fb = es1.enter_context(nc.sbuf_tensor("fb", [128, 8], F32))
        fb2 = es1.enter_context(nc.sbuf_tensor("fb2", [128, 8], F32))
        s1 = es1.enter_context(nc.sbuf_tensor("s1_", [128, 1], F32))
        s2 = es1.enter_context(nc.sbuf_tensor("s2_", [128, 1], F32))
        s3 = es1.enter_context(nc.sbuf_tensor("s3_", [128, 1], F32))
        mk1 = es1.enter_context(nc.sbuf_tensor("mk1", [128, 8], F32))
        wfin = es1.enter_context(nc.sbuf_tensor("wfin", [128, 8], BF16))
        b_inh, b_zon, b_dist, b_wv, b_fb, b_fb2 = (Buf("inh"), Buf("zon"), Buf("dist"),
                                                   Buf("wv"), Buf("fb"), Buf("fb2"))
        b_s1, b_s2, b_s3, b_mk1, b_wfin = Buf("s1"), Buf("s2"), Buf("s3"), Buf("mk1"), Buf("wfin")
        pk = [es1.enter_context(nc.psum_tensor(f"gk{i}", [128, 512], F32)) for i in range(2)]
        b_pk = [Buf(f"gk{i}") for i in range(2)]
        ptr = [es1.enter_context(nc.psum_tensor(f"gt{i}", [128, 128], BF16)) for i in range(2)]
        b_ptr = [Buf(f"gt{i}") for i in range(2)]

        for c in range(NCH):
            i = c % 2
            mm_acc(P, pk[i][:, 0:8],
                   [(xinT[:, dd, c * 128:(c + 1) * 128], wr[:, dd, :]) for dd in range(NDCH)],
                   reads=(b_xinT, b_wr), writes=(b_pk[i],))
            P.v.tensor_scalar(inh[:], pk[i][:, 0:8], rtemp[:], None, ALU.mult,
                              reads=(b_pk[i], b_rtemp), writes=(b_inh,))
            P.a.activation(inh[:], inh[:], AF.Sigmoid, writes=(b_inh,))
            P.v.tensor_scalar(zon[:], inh[:], G_LOWER, None, ALU.is_ge,
                              reads=(b_inh,), writes=(b_zon,))
            P.v.tensor_scalar(mk1[:], inh[:], G_UPPER, None, ALU.is_le,
                              reads=(b_inh,), writes=(b_mk1,))
            P.v.tensor_mul(zon[:], zon[:], mk1[:], reads=(b_mk1,), writes=(b_zon,))
            P.v.tensor_scalar(dist[:], inh[:], G_CENTER, None, ALU.subtract,
                              reads=(b_inh,), writes=(b_dist,))
            P.a.activation(dist[:], dist[:], AF.Abs, writes=(b_dist,))
            P.a.activation(wv_[:], dist[:], AF.Exp, scale=-10.0,
                           reads=(b_dist,), writes=(b_wv,))
            P.v.tensor_mul(wv_[:], wv_[:], zon[:], reads=(b_zon,), writes=(b_wv,))
            P.v.reduce_sum(s1[:], wv_[:], axis=AX.X, reads=(b_wv,), writes=(b_s1,))
            P.v.tensor_scalar(s1[:], s1[:], 1e-8, None, ALU.is_lt, writes=(b_s1,))
            P.a.activation(fb[:], dist[:], AF.Exp, scale=-1.0 / 0.3,
                           reads=(b_dist,), writes=(b_fb,))
            P.v.reduce_max(s2[:], fb[:], axis=AX.X, reads=(b_fb,), writes=(b_s2,))
            P.v.tensor_scalar(mk1[:], fb[:], s2[:], None, ALU.is_equal,
                              reads=(b_fb, b_s2), writes=(b_mk1,))
            P.v.scalar_tensor_tensor(fb2[:], mk1[:], -1e30, fb[:], ALU.mult, ALU.add,
                                     reads=(b_mk1, b_fb), writes=(b_fb2,))
            P.v.reduce_max(s2[:], fb2[:], axis=AX.X, reads=(b_fb2,), writes=(b_s2,))
            P.v.tensor_scalar(fb2[:], fb2[:], s2[:], None, ALU.is_equal,
                              reads=(b_s2,), writes=(b_fb2,))
            P.v.tensor_add(mk1[:], mk1[:], fb2[:], reads=(b_fb2,), writes=(b_mk1,))
            P.v.tensor_mul(fb[:], fb[:], mk1[:], reads=(b_mk1,), writes=(b_fb,))
            P.v.reduce_sum(s2[:], fb[:], axis=AX.X, reads=(b_fb,), writes=(b_s2,))
            P.v.tensor_scalar(s2[:], s2[:], 1e-8, None, ALU.max, writes=(b_s2,))
            P.v.reciprocal(s2[:], s2[:], writes=(b_s2,))
            P.v.tensor_scalar(fb[:], fb[:], s2[:], None, ALU.mult,
                              reads=(b_s2,), writes=(b_fb,))
            P.v.tensor_scalar(fb[:], fb[:], s1[:], None, ALU.mult,
                              reads=(b_s1,), writes=(b_fb,))
            P.v.tensor_scalar(s3[:], s1[:], -1.0, 1.0, ALU.mult, ALU.add,
                              reads=(b_s1,), writes=(b_s3,))
            P.v.tensor_scalar(wv_[:], wv_[:], s3[:], None, ALU.mult,
                              reads=(b_s3,), writes=(b_wv,))
            P.v.tensor_add(wv_[:], wv_[:], fb[:], reads=(b_fb,), writes=(b_wv,))
            P.v.reduce_sum(s2[:], wv_[:], axis=AX.X, reads=(b_wv,), writes=(b_s2,))
            P.v.tensor_scalar(s2[:], s2[:], 1e-8, None, ALU.max, writes=(b_s2,))
            P.v.reciprocal(s2[:], s2[:], writes=(b_s2,))
            P.v.tensor_scalar(wfin[:], wv_[:], s2[:], None, ALU.mult,
                              reads=(b_wv, b_s2), writes=(b_wfin,))
            j = c % 2
            tr_group(P, [(ptr[j][0:8, 0:128], wfin[:])], ident[:],
                     reads=(b_wfin, b_ident), writes=(b_ptr[j],))
            P.a.copy(wT[:, c * 128:(c + 1) * 128], ptr[j][0:8, 0:128],
                     reads=(b_ptr[j],), writes=(b_wT,))
    P.barrier()

    for th in range(2):
        t0 = th * 1024
        with contextlib.ExitStack() as es1:
            out_acc = es1.enter_context(nc.sbuf_tensor(f"oacc{th}", [128, 8, 2048], BF16))
            b_oacc = Buf("oacc")
            with contextlib.ExitStack() as es2:
                h_sb = es2.enter_context(nc.sbuf_tensor(f"h_sb{th}", [128, NDCH, 1024], BF16))
                b_h = Buf("h")
                wgu = [es2.enter_context(nc.sbuf_tensor(f"wgu{th}{i}", [128, NDCH, 256], BF16))
                       for i in range(2)]
                wd = [es2.enter_context(nc.sbuf_tensor(f"wd{th}{i}", [128, NDCH, 512], BF16))
                      for i in range(2)]
                b_wgu = [Buf(f"wgu{i}") for i in range(2)]
                b_wd = [Buf(f"wd{i}") for i in range(2)]
                wbc = es2.enter_context(nc.sbuf_tensor(f"wbc{th}", [128, 1024], BF16))
                b_wbc = Buf("wbc")
                sg = es2.enter_context(nc.sbuf_tensor(f"sg{th}", [128, 512], F32))
                b_sg = Buf("sg")
                pg = [es2.enter_context(nc.psum_tensor(f"pg{th}{i}", [128, 512], F32))
                      for i in range(2)]
                b_pg = [Buf(f"pg{i}") for i in range(2)]
                pu = [es2.enter_context(nc.psum_tensor(f"pu{th}{i}", [128, 512], F32))
                      for i in range(2)]
                b_pu = [Buf(f"pu{i}") for i in range(2)]
                pw = es2.enter_context(nc.psum_tensor(f"pw{th}", [128, 1024], F32))
                b_pw = Buf("pw")
                pd = [es2.enter_context(nc.psum_tensor(f"pd{th}{i}", [128, 512], F32))
                      for i in range(2)]
                b_pd = [Buf(f"pd{i}") for i in range(2)]

                for e in range(8):
                    mm_group(P, [(e8bc[:, e * 128:(e + 1) * 128],
                                  wT[0:8, t0 + g * 512:t0 + (g + 1) * 512],
                                  pw[:, g * 512:(g + 1) * 512], True, True) for g in range(2)],
                             reads=(b_e8, b_wT), writes=(b_pw,))
                    P.a.copy(wbc[:], pw[:], reads=(b_pw,), writes=(b_wbc,))
                    for fc in range(NDCH):
                        i = fc % 2
                        P.g.dma_start(
                            out=wgu[i][:],
                            in_=bass.AP(io["l1f_wgu"].tensor,
                                        (e * NDCH + fc) * D * 256,
                                        [[256, 128], [128 * 256, NDCH], [1, 256]]),
                            writes=(b_wgu[i],))
                        for g in range(2):
                            ii = g % 2
                            mm_acc(P, pg[ii][:],
                                   [(wgu[i][:, dd, 0:128],
                                     xinT[:, dd, t0 + g * 512:t0 + (g + 1) * 512])
                                    for dd in range(NDCH)],
                                   reads=(b_wgu[i], b_xinT), writes=(b_pg[ii],))
                            mm_acc(P, pu[ii][:],
                                   [(wgu[i][:, dd, 128:256],
                                     xinT[:, dd, t0 + g * 512:t0 + (g + 1) * 512])
                                    for dd in range(NDCH)],
                                   reads=(b_wgu[i], b_xinT), writes=(b_pu[ii],))
                            P.a.activation(sg[:], pg[ii][:], AF.Silu,
                                           reads=(b_pg[ii],), writes=(b_sg,))
                            P.v.tensor_mul(sg[:], sg[:], wbc[:, g * 512:(g + 1) * 512],
                                           reads=(b_wbc,), writes=(b_sg,))
                            P.v.tensor_mul(h_sb[:, fc, g * 512:(g + 1) * 512], sg[:], pu[ii][:],
                                           reads=(b_sg, b_pu[ii]), writes=(b_h,))
                    for og in range(4):
                        iw = og % 2
                        P.g.dma_start(out=wd[iw][:],
                                      in_=io["l1f_wd"][e * D:(e + 1) * D,
                                                       og * 512:(og + 1) * 512]
                                      .rearrange("(fc p) n -> p fc n", p=128),
                                      writes=(b_wd[iw],))
                        for cl in range(8):
                            ii = cl % 2
                            mm_acc(P, pd[ii][:],
                                   [(h_sb[:, fc, cl * 128:(cl + 1) * 128], wd[iw][:, fc, :])
                                    for fc in range(NDCH)],
                                   reads=(b_h, b_wd[iw]), writes=(b_pd[ii],))
                            if e == 0:
                                P.a.copy(out_acc[:, cl, og * 512:(og + 1) * 512],
                                         pd[ii][:], reads=(b_pd[ii],), writes=(b_oacc,))
                            else:
                                P.v.tensor_add(out_acc[:, cl, og * 512:(og + 1) * 512],
                                               out_acc[:, cl, og * 512:(og + 1) * 512],
                                               pd[ii][:], reads=(b_pd[ii],),
                                               writes=(b_oacc,))
            P.barrier()
            with contextlib.ExitStack() as es2:
                ln = LNStats(P, es2, f"l1f_post{th}")
                xc = [es2.enter_context(nc.sbuf_tensor(f"pf_xc{th}{i}", [128, 2048], F32))
                      for i in range(2)]
                b_xc = [Buf(f"xc{i}") for i in range(2)]
                lo = [es2.enter_context(nc.sbuf_tensor(f"pf_lo{th}{i}", [128, 2048], F32))
                      for i in range(2)]
                b_lo = [Buf(f"lo{i}") for i in range(2)]
                for cl in range(8):
                    ln.stat_chunk(P, cl, out_acc[:, cl, :], reads=(b_oacc,))
                ln.finish(P, D)
                stores = []
                for cl in range(8):
                    i = cl % 2
                    c = th * 8 + cl
                    P.sy.dma_start(out=xc[i][:], in_=io["x3"][c * 128:(c + 1) * 128, :],
                                   writes=(b_xc[i],))
                    ln.apply(P, cl, lo[i][:], out_acc[:, cl, :], reads=(b_oacc,),
                             writes=(b_lo[i],))
                    P.v.tensor_add(lo[i][:], lo[i][:], xc[i][:], reads=(b_xc[i],),
                                   writes=(b_lo[i],))
                    stores.append(P.sy.dma_start(out=io["x4"][c * 128:(c + 1) * 128, :],
                                                 in_=lo[i][:], reads=(b_lo[i],)))
        P.barrier()
    P.fence(stores)


def prep_l1f_inputs(full, core, x3=None):
    d = {
        "l1f_wgu": _f32(np.stack(
            [np.asarray(full["L1_ffn_Wg"]).reshape(8, D, NDCH, 128).transpose(0, 2, 1, 3),
             np.asarray(full["L1_ffn_Wu"]).reshape(8, D, NDCH, 128).transpose(0, 2, 1, 3)],
            axis=3).reshape(8 * NDCH * D, 256)),
        "l1f_wd": _f32(np.asarray(full["L1_ffn_Wd"]).reshape(8 * D, D)),
        "l1f_wr": _f32(full["L1_ffn_Wr"]),
        "l1f_tempB": np.full((128, 1), np.asarray(full["L1_ffn_temperature"]).ravel()[0],
                             np.float32),
        "ident": np.eye(128, dtype=ml_dtypes.bfloat16),
        "e8bc": np.ascontiguousarray(
            np.repeat(np.eye(8, dtype=np.float32), 128, axis=1)).astype(ml_dtypes.bfloat16),
    }
    if x3 is not None:
        cm = _chunk_map()[core]
        xr = x3.reshape(B * 32, 128, D)
        d["x3"] = _f32(np.concatenate([xr[b * 32 + j] for (b, j) in cm], axis=0))
    return d


def build_l1f():
    nc = bass.Bass("TRN2", target_bir_lowering=False)
    io = {}
    io["x3"] = nc.dram_tensor("x3", [TLOC, D], F32, kind="ExternalInput").ap()
    io["x4"] = nc.dram_tensor("x4", [TLOC, D], F32, kind="ExternalOutput").ap()
    for nm, shp in [("l1f_wgu", [8 * NDCH * D, 256]), ("l1f_wd", [8 * D, D]),
                    ("l1f_wr", [D, 8]), ("l1f_tempB", [128, 1])]:
        io[nm] = nc.dram_tensor(nm, shp, F32, kind="ExternalInput").ap()
    io["ident"] = nc.dram_tensor("ident", [128, 128], BF16, kind="ExternalInput").ap()
    io["e8bc"] = nc.dram_tensor("e8bc", [8, 1024], BF16, kind="ExternalInput").ap()
    P = Prog(nc)
    with contextlib.ExitStack() as es:
        emit_l1f(P, es, io)
        P.finalize_and_emit()
    P.close()
    return nc



class _NCProxy:

    def __init__(self, nc, tag):
        object.__setattr__(self, "_nc", nc)
        object.__setattr__(self, "_tag", tag)

    def sbuf_tensor(self, name, *a, **kw):
        return self._nc.sbuf_tensor(f"{self._tag}_{name}", *a, **kw)

    def psum_tensor(self, name, *a, **kw):
        return self._nc.psum_tensor(f"{self._tag}_{name}", *a, **kw)

    def __getattr__(self, k):
        return getattr(self._nc, k)


def build_fused(meta):
    nc = bass.Bass("TRN2", target_bir_lowering=False)
    io = {}
    io["x0"] = nc.dram_tensor("x0", [TLOC, D], F32, kind="ExternalInput").ap()
    io["x4"] = nc.dram_tensor("x4", [TLOC, D], F32, kind="ExternalOutput").ap()
    for nm in ["x1", "x2", "x3"]:
        io[nm] = nc.dram_tensor(nm, [TLOC, D], F32).ap()
    for nm, shp in [("l0a_wqT", [D, D]), ("l0a_wkT", [D, 512]), ("l0a_wvT", [D, 512]),
                    ("l0a_wgT", [D, D]), ("l0a_wmbT", [D, D]),
                    ("l0a_bqB", [128, D]), ("l0a_bkB", [128, 512]), ("l0a_bvB", [128, 512]),
                    ("l0a_bgB", [128, D]), ("l0a_bmbB", [128, D]),
                    ("l0a_mem", [16 * 768, 128]), ("l0a_norm", [16, 768]),
                    ("l0a_normb", [2048, 768])]:
        io[nm] = nc.dram_tensor(nm, shp, F32, kind="ExternalInput").ap()
    io["arin"] = nc.dram_tensor("arin", [12384, 128], F32).ap()
    io["arout"] = nc.dram_tensor("arout", [12384, 128], F32).ap()
    io["fence_in"] = nc.dram_tensor("fence_in", [8, 128], F32).ap()
    io["fence_out"] = nc.dram_tensor("fence_out", [8, 128], F32).ap()
    for nm, shp, dt in [("l0f_w0T", [D, D], F32), ("l0f_gwT", [D, 8], F32),
                        ("l0f_atoms", [64, NS], F32), ("l0f_eaw", [8, 64], F32),
                        ("l0f_imp", [8, NS], F32), ("l0f_sidxT", [128, 128], I32),
                        ("iota128", [128, 128], F32)]:
        io[nm] = nc.dram_tensor(nm, shp, dt, kind="ExternalInput").ap()
    io["xg_dram"] = nc.dram_tensor("xg_dram", [D, TLOC], BF16).ap()
    for nm in ["l1_wqT", "l1_wkT", "l1_wvT", "l1_woT"]:
        io[nm] = nc.dram_tensor(nm, [D, D], F32, kind="ExternalInput").ap()
    for nm, shp in [("ropeCos", [128, 2048]), ("ropeSin", [128, 2048]),
                    ("ropePermT", [128, 128]), ("ones128", [128, 128]),
                    ("maskA", [512, 512]), ("maskB", [512, 512]),
                    ("ident", [128, 128]), ("ones1", [1, 128]), ("e8bc", [8, 1024])]:
        io[nm] = nc.dram_tensor(nm, shp, BF16, kind="ExternalInput").ap()
    io["agin"] = nc.dram_tensor("agin", [4096, 2048], BF16).ap()
    io["agout"] = nc.dram_tensor("agout", [8192, 2048], BF16).ap()
    io["qT_dram"] = nc.dram_tensor("qT_dram", [2048, 2048], BF16).ap()
    io["f2in"] = nc.dram_tensor("f2in", [8, 128], F32).ap()
    io["f2out"] = nc.dram_tensor("f2out", [8, 128], F32).ap()
    for nm, shp in [("l1f_wgu", [8 * NDCH * D, 256]), ("l1f_wd", [8 * D, D]),
                    ("l1f_wr", [D, 8]), ("l1f_tempB", [128, 1])]:
        io[nm] = nc.dram_tensor(nm, shp, F32, kind="ExternalInput").ap()

    P = Prog(nc)
    real_nc = P.nc

    def phase(tag, emitter, io_in, io_out, *extra):
        d = dict(io)
        d[io_in[1]] = io[io_in[0]]
        d[io_out[1]] = io[io_out[0]]
        object.__setattr__(P, "nc", _NCProxy(real_nc, tag))
        with contextlib.ExitStack() as es:
            emitter(P, es, d, *extra)
        P.barrier()

    phase("A", emit_l0a, ("x0", "x0"), ("x1", "x1"))
    phase("B", emit_l0f, ("x1", "x1"), ("x2", "x2"), meta)
    phase("C", emit_l1a, ("x2", "x2"), ("x3", "x3"))
    phase("D", emit_l1f, ("x3", "x3"), ("x4", "x4"))
    object.__setattr__(P, "nc", real_nc)
    P.finalize_and_emit()
    P.close()
    return nc


def prep_fused_inputs(full, core, meta):
    d = {}
    d.update(prep_l0a_inputs(full, core))
    d.update(prep_l0f_inputs(full, core, meta))
    d.update(prep_l1a_inputs(full, core))
    d.update(prep_l1f_inputs(full, core))
    return d


FUSED = [True]


def _enable_ldw_opt():
    try:
        from concourse.compiler_utils import get_compiler_flags, set_compiler_flags
        flags = get_compiler_flags()
        out = [f.replace("--enable-ldw-opt=false", "--enable-ldw-opt=true")
               for f in flags]
        if out != flags:
            set_compiler_flags(out)
    except Exception:
        pass


_enable_ldw_opt()
